# version 04
# baseline (speedup 1.0000x reference)
import contextlib
import numpy as np
import ml_dtypes

import concourse.bass as bass
import concourse.mybir as mybir
from concourse.bass_utils import run_bass_kernel_spmd
from concourse.alu_op_type import AluOpType as OP

F32 = mybir.dt.float32
BF16 = mybir.dt.bfloat16
F16 = mybir.dt.float16
I16 = mybir.dt.int16
AF = mybir.ActivationFunctionType

N_CORES = 8
SEQ, D, NH, DN, DR = 8192, 2048, 16, 128, 64
QL, KVL = 512, 512
CH = 512
NCH = SEQ // CH
NSLOT = 8
OWN = NSLOT * 128
KT = D // 128
SCALE = float(np.float32(1.0) / np.float32(np.sqrt(np.float32(192.0))))
NEG_MASK = -1e30
NHP = 8
EPS_LN = 1e-5
I_FF = 5632

ENGS = ("tensor", "vector", "scalar", "sync", "gpsimd")
BF = ml_dtypes.bfloat16


class Sched:
    def __init__(self, nc):
        self.nc = nc
        self.ops = {e: [] for e in ENGS}
        self.cnt = {e: 0 for e in ENGS}
        self.sems = {}
        self.last_wait = {}
        self.dsems = {}
        self.dcnt = {}
        self._stack = None
        self.waited = {e: set() for e in ENGS}
        self.inc_at = {e: {} for e in ENGS}

    def setup(self, stack):
        self._stack = stack
        for e in ENGS:
            self.sems[e] = stack.enter_context(self.nc.semaphore("sem_" + e))

    def dsem(self, name):
        if name not in self.dsems:
            self.dsems[name] = self._stack.enter_context(
                self.nc.semaphore("d_" + name))
            self.dcnt[name] = 0
        return self.dsems[name]

    def op(self, eng, fn, inc=1):
        sem = self.sems[eng]
        self.cnt[eng] += inc
        ordinal = self.cnt[eng]

        def run(h, fn=fn, sem=sem, eng=eng, ordinal=ordinal):
            inst = fn(h)
            d = self.inc_at[eng].get(ordinal)
            if d:
                inst.then_inc(sem, d)

        self.ops[eng].append(run)
        return ordinal

    def dma(self, eng, out, in_, sem=None):
        assert sem is not None
        s = self.dsem(sem)

        def run(h, o=out, i=in_, s=s):
            h.dma_start(out=o, in_=i).then_inc(s, 16)

        self.ops[eng].append(run)
        self.dcnt[sem] += 16
        return self.dcnt[sem]

    def dwait(self, eng, sem, cnt=None):
        s = self.dsem(sem)
        if cnt is None:
            cnt = self.dcnt[sem]
        if cnt <= 0:
            return
        key = (eng, "d_" + sem)
        if self.last_wait.get(key, -1) >= cnt:
            return
        self.last_wait[key] = cnt
        self.ops[eng].append(lambda h, s=s, c=cnt: h.wait_ge(s, c))

    def wait(self, eng, on, cnt=None):
        if cnt is None:
            cnt = self.cnt[on]
        if cnt <= 0:
            return
        key = (eng, on)
        if self.last_wait.get(key, -1) >= cnt:
            return
        self.last_wait[key] = cnt
        self.waited[on].add(cnt)
        sem = self.sems[on]
        self.ops[eng].append(lambda h, s=sem, c=cnt: h.wait_ge(s, c))

    def finalize(self):
        for e in ENGS:
            prev = 0
            for c in sorted(self.waited[e]):
                if c > self.cnt[e]:
                    continue
                self.inc_at[e][c] = c - prev
                prev = c

    def barrier(self):
        snap = dict(self.cnt)
        dsnap = dict(self.dcnt)
        for e in ENGS:
            for o in ENGS:
                if o != e:
                    self.wait(e, o, snap[o])
            for name, c in dsnap.items():
                self.dwait(e, name, c)


def build_program():
    nc = bass.Bass("TRN2", target_bir_lowering=False, debug=False,
                   detect_race_conditions=False)

    def din(name, shape, dt):
        return nc.dram_tensor(name, shape, dt, kind="ExternalInput").ap()

    def dint(name, shape, dt):
        return nc.dram_tensor(name, shape, dt, kind="Internal").ap()

    xT = din("xT", [D, SEQ], F32)
    xqT = din("xqT", [D, OWN], F32)
    ropeq_c = din("ropeq_c", [DR, OWN], F32)
    ropeq_s = din("ropeq_s", [DR, OWN], F32)
    ropek_c = din("ropek_c", [DR, SEQ], F32)
    ropek_s = din("ropek_s", [DR, SEQ], F32)
    masks_d = din("masks", [NSLOT * 2, 128, CH], BF16)
    wqa_d = din("wqa", [D, QL], F32)
    wqbn_d = din("wqbn", [QL, NH * DN], F32)
    wqbp_d = din("wqbp", [QL, NH * DR], F32)
    wqbps_d = din("wqbps", [QL, NH * DR], F32)
    wkva_d = din("wkva", [D, 640], F32)
    wkvbk_d = din("wkvbk", [KVL, NH * DN], F32)
    wkvbv_d = din("wkvbv", [KVL, NH * DN], F32)
    wo_d = din("wo", [D, D], F32)
    gu_rep = din("gu_rep", [D, 2 * I_FF], I16)
    gu_sc_d = din("gu_sc", [16, 2 * I_FF], F16)
    dn_rep = din("dn_rep", [I_FF, D], I16)
    dn_sc_d = din("dn_sc", [44, D], F16)
    identb_d = din("identb", [128, 128], BF16)
    identf_d = din("identf", [128, 128], F32)
    colsb_d = din("colsb", [128, 2], BF16)
    colsf_d = din("colsf", [128, 3], F32)
    rowb_d = din("rowb", [1, 128], BF16)
    rowf_d = din("rowf", [1, 128], F32)
    shf_d = din("shf", [128, 2], I16)
    shfull_d = din("shfull", [128, 512], I16)

    out_tok = nc.dram_tensor("out_tok", [OWN, D], F32, kind="ExternalOutput").ap()

    sp_kn = dint("sp_kn", [NCH, 128, NH * CH], BF16)
    sp_v = dint("sp_v", [NCH, 128, 4 * NH * 128], BF16)
    sp_kpe = dint("sp_kpe", [NCH, DR, CH], BF16)
    sp_qnB = dint("sp_qnB", [128, NHP * OWN], BF16)
    sp_qpB = dint("sp_qpB", [DR, NHP * OWN], BF16)
    sp_x1 = dint("sp_x1", [128, KT * OWN], F32)
    sp_wd = dint("sp_wd", [44, 128, D], BF16)
    sp_oA = dint("sp_oA", [128, NSLOT * NHP * 128], BF16)
    sp_oB = dint("sp_oB", [128, NSLOT * NHP * 128], BF16)

    sch = Sched(nc)

    def mm(out, lhsT, rhs, start, stop):
        return sch.op("tensor", lambda h, o=out, l=lhsT, r=rhs, s=start, e=stop:
                      h.matmul(o, l, r, start=s, stop=e))

    with contextlib.ExitStack() as root:
        sch.setup(root)

        def sb(st, name, shape, dt, side=None):
            if side is None:
                return st.enter_context(nc.sbuf_tensor(name, shape, dt))
            return st.enter_context(nc.sbuf_tensor(name, shape, dt, side=side))

        c_idb = sb(root, "c_idb", [128, 128], BF16)
        c_idf = sb(root, "c_idf", [128, 128], F32)
        c_colsb = sb(root, "c_colsb", [128, 2], BF16)
        c_colsf = sb(root, "c_colsf", [128, 3], F32)
        c_rowb = sb(root, "c_rowb", [1, 128], BF16)
        c_rowf = sb(root, "c_rowf", [1, 128], F32)
        c_shf = sb(root, "c_shf", [128, 2], I16)
        c_shfull = sb(root, "c_shfull", [128, 512], I16)
        sums_acc = sb(root, "sums_acc", [128, NSLOT * NHP], F32)
        bfc = [sb(root, f"bfc{i}", [128, 512], BF16) for i in range(2)]
        rsum_sb = sb(root, "rsum_sb", [1, NHP * 128], F32)

        inv2048_b = c_colsb[:, 0:1]
        inv2048_f = c_colsf[:, 0:1]
        nc.const_aps.register(F32, EPS_LN, c_colsf[:, 2:3])
        ones_row_b = c_rowb[:, :]
        ones_row_f = c_rowf[:, :]
        shifts = c_shf[:, 0:1]
        shifts3 = c_shf[:, 1:2]

        PS = [root.enter_context(nc.psum_tensor(f"ps{i}", [128, 512], F32))
              for i in range(6)]
        PTb = [root.enter_context(nc.psum_tensor(f"pt{i}", [128, 1024], BF16))
               for i in range(2)]
        PTf = [p.ap().bitcast(F32) for p in PTb]

        sch.dma("sync", c_idb[:], identb_d, sem="consts")
        sch.dma("sync", c_idf[:], identf_d, sem="consts")
        sch.dma("sync", c_colsb[:], colsb_d, sem="consts")
        sch.dma("sync", c_colsf[:], colsf_d, sem="consts")
        sch.dma("sync", c_rowb[:], rowb_d, sem="consts")
        sch.dma("sync", c_rowf[:], rowf_d, sem="consts")
        sch.dma("sync", c_shf[:], shf_d, sem="consts")
        sch.dma("sync", c_shfull[:], shfull_d, sem="consts")
        for e in ("tensor", "vector", "scalar", "gpsimd", "sync"):
            sch.dwait(e, "consts")

        def stats_rows(get_tile, sqbufs, rows_m, rows_r, rows_b, width=512, src_f32=False):
            sch.wait("scalar", "vector", sch.cnt["vector"])
            sch.wait("tensor", "vector", sch.cnt["vector"])
            for half in range(2):
                mrow = (PS[0] if half == 0 else PS[1])[0:1, 0:width]
                s2row = (PTf[0] if half == 0 else PTf[1])[0:1, 0:width]
                sq_guard = [0, 0]
                bf_guard = [0, 0]
                for k in range(KT):
                    xt = get_tile(k, half)
                    par = k % 2
                    sch.wait("scalar", "tensor", sq_guard[par])
                    c_sq = sch.op("scalar", lambda h, o=sqbufs[par][:, 0:width], i=xt:
                                  h.activation(o, i, AF.Square))
                    if src_f32:
                        btile = bfc[par][:, 0:width]
                        sch.wait("vector", "tensor", bf_guard[par])
                        vb = sch.op("vector", lambda h, o=btile, i=xt: h.tensor_copy(o, i))
                        sch.wait("tensor", "vector", vb)
                        mm(mrow, inv2048_b, btile, k == 0, k == KT - 1)
                        bf_guard[par] = sch.cnt["tensor"]
                    else:
                        mm(mrow, inv2048_b, xt, k == 0, k == KT - 1)
                    sch.wait("tensor", "scalar", c_sq)
                    mm(s2row, inv2048_b, sqbufs[par][:, 0:width], k == 0, k == KT - 1)
                    sq_guard[par] = sch.cnt["tensor"]
                sch.wait("vector", "tensor", sch.cnt["tensor"])
                m_f = rows_m[0:1, half * width: (half + 1) * width]
                m_b = rows_b[0:1, half * width: (half + 1) * width]
                r_f = rows_r[0:1, half * width: (half + 1) * width]
                sch.op("vector", lambda h, o=m_f, i=mrow: h.tensor_copy(o, i))
                sch.op("vector", lambda h, o=m_b, i=mrow: h.tensor_copy(o, i))
                sch.op("vector", lambda h, o=r_f, a=mrow, b=m_f:
                       h.tensor_tensor(o, a, b, OP.mult))
                sch.op("vector", lambda h, o=r_f, a=r_f, b=s2row:
                       h.scalar_tensor_tensor(o, a, -1.0, b, OP.mult, OP.add))
                sch.wait("scalar", "vector", sch.cnt["vector"])
                c_sd = sch.op("scalar", lambda h, o=r_f, i=r_f:
                              h.activation(o, i, AF.Sqrt, bias=EPS_LN))
                sch.wait("vector", "scalar", c_sd)
                sch.op("vector", lambda h, o=r_f, i=r_f: h.reciprocal(o, i))
            return sch.cnt["vector"]

        def ln_apply(get_in, get_out, rows_r, rows_b, tmp2, extra_res=None):
            for half in range(2):
                sch.wait("tensor", "vector", sch.cnt["vector"])
                mm(PS[2][:, 0:512], ones_row_b,
                   rows_b[0:1, half * 512: half * 512 + 512], True, True)
                tb = mm(PS[3][:, 0:512], ones_row_f,
                        rows_r[0:1, half * 512: half * 512 + 512], True, True)
                sch.wait("vector", "tensor", tb)
                for k in range(KT):
                    it = get_in(k, half)
                    ot = get_out(k, half)
                    tf = tmp2[k % 2][:]
                    sch.op("vector", lambda h, o=tf, a=it, b=PS[2][:, 0:512]:
                           h.tensor_tensor(o, a, b, OP.subtract))
                    if extra_res is None:
                        sch.op("vector", lambda h, o=ot, a=tf, b=PS[3][:, 0:512]:
                               h.tensor_tensor(o, a, b, OP.mult))
                    else:
                        rt = extra_res(k, half)
                        sch.op("vector", lambda h, o=tf, a=tf, b=PS[3][:, 0:512]:
                               h.tensor_tensor(o, a, b, OP.mult))
                        sch.op("vector", lambda h, o=ot, a=rt, b=tf:
                               h.tensor_tensor(o, a, b, OP.add))
            return sch.cnt["vector"]

        stack_q = contextlib.ExitStack()
        qnA = sb(stack_q, "qnA", [128, NHP * OWN], BF16, side="right")
        qpA = sb(stack_q, "qpA", [DR, NHP * OWN], BF16, side="right")

        with contextlib.ExitStack() as ph0:
            xq = sb(ph0, "xq", [128, KT * OWN], BF16)
            sqb = [sb(ph0, f"ph0sq{i}", [128, 512], BF16) for i in range(2)]
            rows0m = sb(ph0, "ph0rowm", [1, OWN], F32)
            rows0r = sb(ph0, "ph0rowr", [1, OWN], F32)
            rows0_b = sb(ph0, "ph0rows_b", [1, OWN], BF16)
            tmpf = [sb(ph0, f"ph0tmp{i}", [128, 512], F32) for i in range(2)]
            w_qa = sb(ph0, "w_qa", [128, KT * QL], BF16)
            w_qbn = sb(ph0, "w_qbn", [128, 4 * NH * DN], BF16)
            w_qbp = sb(ph0, "w_qbp", [128, 4 * NH * DR], BF16)
            w_qbps = sb(ph0, "w_qbps", [128, 4 * NH * DR], BF16)
            qa_sb = sb(ph0, "qa_sb", [128, 4 * OWN], BF16)
            qnB = sb(ph0, "qnB", [128, NHP * OWN], BF16)
            qpB = sb(ph0, "qpB", [DR, NHP * OWN], BF16)
            rq_c = sb(ph0, "rq_c", [DR, OWN], F32)
            rq_s = sb(ph0, "rq_s", [DR, OWN], F32)
            rop = [sb(ph0, f"ph0rop{i}", [DR, 512], F32) for i in range(2)]

            sch.dma("gpsimd", xq[:].rearrange("p (k t) -> p k t", k=KT), xqT.rearrange("(k p) t -> p k t", p=128), sem="ph0w")
            sch.dma("gpsimd", w_qa[:].rearrange("p (k t) -> p k t", k=KT), wqa_d.rearrange("(k p) t -> p k t", p=128), sem="ph0w")
            sch.dma("gpsimd", w_qbn[:].rearrange("p (k t) -> p k t", k=4), wqbn_d.rearrange("(k p) t -> p k t", p=128), sem="ph0w")
            sch.dma("gpsimd", w_qbp[:].rearrange("p (k t) -> p k t", k=4), wqbp_d.rearrange("(k p) t -> p k t", p=128), sem="ph0w")
            sch.dma("gpsimd", w_qbps[:].rearrange("p (k t) -> p k t", k=4), wqbps_d.rearrange("(k p) t -> p k t", p=128), sem="ph0w")
            sch.dma("sync", rq_c[:], ropeq_c, sem="ropeq")
            sch.dma("sync", rq_s[:], ropeq_s, sem="ropeq")
            sch.dwait("scalar", "ph0w")
            sch.dwait("tensor", "ph0w")
            sch.dwait("vector", "ropeq")

            stats_rows(lambda k, half: xq[:, k * OWN + half * 512:
                                          k * OWN + half * 512 + 512],
                       sqb, rows0m, rows0r, rows0_b)
            v_ln1 = ln_apply(
                lambda k, half: xq[:, k * OWN + half * 512: k * OWN + half * 512 + 512],
                lambda k, half: xq[:, k * OWN + half * 512: k * OWN + half * 512 + 512],
                rows0r, rows0_b, tmpf)

            prev_ev = v_ln1
            for m in range(4):
                for half in range(2):
                    ps = PS[(m * 2 + half) % 2]
                    sch.wait("tensor", "vector", prev_ev)
                    for k in range(KT):
                        lt = w_qa[:, k * QL + m * 128: k * QL + m * 128 + 128]
                        rt = xq[:, k * OWN + half * 512: k * OWN + half * 512 + 512]
                        mm(ps[:, 0:512], lt, rt, k == 0, k == KT - 1)
                    sch.wait("vector", "tensor", sch.cnt["tensor"])
                    dst = qa_sb[:, m * OWN + half * 512: m * OWN + half * 512 + 512]
                    prev_ev = sch.op("vector", lambda h, o=dst, i=ps[:, 0:512]:
                                     h.tensor_copy(o, i))

            prev_v = prev_ev
            for h_ in range(NH):
                qn_dst = qnA if h_ < NHP else qnB
                qp_dst = qpA if h_ < NHP else qpB
                hh = h_ % NHP
                for half in range(2):
                    sl = slice(hh * OWN + half * 512, hh * OWN + half * 512 + 512)
                    sch.wait("tensor", "vector", prev_v)
                    for k in range(4):
                        lt = w_qbn[:, k * NH * DN + h_ * DN: k * NH * DN + h_ * DN + 128]
                        rt = qa_sb[:, k * OWN + half * 512: k * OWN + half * 512 + 512]
                        mm(PS[0][:, 0:512], lt, rt, k == 0, k == 3)
                    for k in range(4):
                        lt = w_qbp[:, k * NH * DR + h_ * DR: k * NH * DR + h_ * DR + DR]
                        rt = qa_sb[:, k * OWN + half * 512: k * OWN + half * 512 + 512]
                        mm(PS[2][0:DR, 0:512], lt, rt, k == 0, k == 3)
                    for k in range(4):
                        lt = w_qbps[:, k * NH * DR + h_ * DR: k * NH * DR + h_ * DR + DR]
                        rt = qa_sb[:, k * OWN + half * 512: k * OWN + half * 512 + 512]
                        mm(PS[3][0:DR, 0:512], lt, rt, k == 0, k == 3)
                    sch.wait("vector", "tensor", sch.cnt["tensor"])
                    sch.op("vector", lambda h, o=qn_dst[:, sl], i=PS[0][:, 0:512]:
                           h.tensor_copy(o, i))
                    cs = rq_c[:, half * 512: half * 512 + 512]
                    sn = rq_s[:, half * 512: half * 512 + 512]
                    sch.op("vector", lambda h, o=rop[0][:], a=PS[3][0:DR, 0:512], b=sn:
                           h.tensor_tensor(o, a, b, OP.mult))
                    sch.op("vector", lambda h, o=rop[1][:], a=PS[2][0:DR, 0:512], b=cs:
                           h.tensor_tensor(o, a, b, OP.mult))
                    prev_v = sch.op("vector", lambda h, o=qp_dst[0:DR, sl], a=rop[0][:],
                                    b=rop[1][:]: h.tensor_tensor(o, a, b, OP.add))

            sch.wait("gpsimd", "vector", sch.cnt["vector"])
            sch.dma("gpsimd", sp_qnB, qnB[:], sem="qspill")
            sch.dma("gpsimd", sp_qpB, qpB[:], sem="qspill")
            sch.barrier()

        def attn_pass(pass_id, stack, qn, qp, odst):
            if pass_id == 0:
                xc = sb(stack, "xc", [128, KT * CH], BF16)
                sqb = [sb(stack, f"a0sq{i}", [128, 512], BF16) for i in range(2)]
                w_kva = sb(stack, "w_kva", [128, KT * 640], BF16)
                w_kbk = sb(stack, "w_kbk", [128, 4 * NH * DN], BF16)
                w_kbv = sb(stack, "w_kbv", [128, 4 * NH * DN], BF16)
                cT = sb(stack, "cT", [128, 4 * CH], BF16)
                nw1 = sb(stack, "nw1", [1, 640], BF16)
                rk_c = sb(stack, "rk_c", [DR, CH], F32)
                rk_s = sb(stack, "rk_s", [DR, CH], F32)
                crowm = sb(stack, "crowm", [1, CH], F32)
                crowr = sb(stack, "crowr", [1, CH], F32)
                crow_b = sb(stack, "crow_b", [1, CH], BF16)
                rbc = sb(stack, "rbc", [128, 512], F32)
                ropt = [sb(stack, f"a0rp{i}", [DR, 512], F32) for i in range(2)]
                kn_sb = [sb(stack, "kn_sb", [128, NH * CH], BF16)]
                v_sb = [sb(stack, "v_sb", [128, 4 * NH * 128], BF16)]
                kpe_sb = [sb(stack, "kpe_sb", [DR, CH], BF16)]
                nbuf = 1
            else:
                kn_sb = [sb(stack, f"kn_sb{i}", [128, NH * CH], BF16) for i in range(2)]
                v_sb = [sb(stack, f"v_sb{i}", [128, 4 * NH * 128], BF16) for i in range(2)]
                kpe_sb = [sb(stack, f"kpe_sb{i}", [DR, CH], BF16) for i in range(2)]
                nbuf = 2

            mk_sb = [sb(stack, f"mk{pass_id}_{i}", [128, 512], BF16) for i in range(2)]
            p_sb = [sb(stack, f"p{pass_id}_{i}", [128, 512], BF16) for i in range(2)]
            pt_sb = [sb(stack, f"pv{pass_id}_{i}", [128, 512], BF16) for i in range(2)]
            oacc = sb(stack, f"oacc{pass_id}", [128, NSLOT * NHP * 128], F32)
            sums_vis = sb(stack, f"sumsv{pass_id}", [128, NHP], F32)
            onorm = [sb(stack, f"onorm{pass_id}_{i}", [128, NHP * 128], BF16)
                     for i in range(2)]

            if pass_id == 0:
                sch.dma("gpsimd", w_kva[:].rearrange("p (k t) -> p k t", k=KT), wkva_d.rearrange("(k p) t -> p k t", p=128), sem="kvaw")
                sch.dma("gpsimd", w_kbk[:].rearrange("p (k t) -> p k t", k=4), wkvbk_d.rearrange("(k p) t -> p k t", p=128), sem="kvaw")
                sch.dma("gpsimd", w_kbv[:].rearrange("p (k t) -> p k t", k=4), wkvbv_d.rearrange("(k p) t -> p k t", p=128), sem="kvaw")
                sch.dwait("tensor", "kvaw")
                ones_col_b = c_colsb[:, 1:2]
                for (n0, n1, ps) in ((0, 512, PS[0]), (512, 640, PS[1])):
                    for k in range(KT):
                        mm(ps[0:1, 0:n1 - n0], ones_col_b,
                           w_kva[:, k * 640 + n0: k * 640 + n1], k == 0, k == KT - 1)
                sch.wait("vector", "tensor", sch.cnt["tensor"])
                sch.op("vector", lambda h: h.tensor_scalar(
                    nw1[0:1, 0:512], PS[0][0:1, 0:512], -1.0, None, OP.mult, OP.bypass))
                sch.op("vector", lambda h: h.tensor_scalar(
                    nw1[0:1, 512:640], PS[1][0:1, 0:128], -1.0, None, OP.mult, OP.bypass))
                sch.wait("tensor", "vector", sch.cnt["vector"])

            mask_ld = {}
            kvload = [0] * (NCH + 1)
            if pass_id == 0:
                sch.dma("gpsimd", xc[:].rearrange("p (k t) -> p k t", k=KT), xT[:, 0:CH].rearrange("(k p) t -> p k t", p=128), sem="xc")
                sch.dma("sync", rk_c[:], ropek_c[:, 0:CH], sem="ropek")
                sch.dma("sync", rk_s[:], ropek_s[:, 0:CH], sem="ropek")
            else:
                sch.dwait("gpsimd", "spill")
                sch.dma("gpsimd", kn_sb[0][:], sp_kn[0], sem="kv0")
                sch.dma("gpsimd", v_sb[0][:], sp_v[0], sem="kv0")
                sch.dma("gpsimd", kpe_sb[0][:], sp_kpe[0], sem="kv0")
                kvload[0] = ("kv0", sch.dcnt["kv0"])
            mname = f"mask{pass_id}"
            sch.dma("sync", mk_sb[0][:], masks_d[0], sem=mname + "0")
            mask_ld[(0, 0)] = (mname + "0", sch.dcnt[mname + "0"])
            sch.dma("sync", mk_sb[1][:], masks_d[1], sem=mname + "1")
            mask_ld[(0, 1)] = (mname + "1", sch.dcnt[mname + "1"])

            s_banks = (PS[0], PS[1])
            o_pairs = ((PS[2], PS[3]), (PS[4], PS[5]))
            exp_done = [0, 0]
            ptevac_done = [0, 0]
            pmm_done = [0, 0]
            oadd_done = [0, 0]
            sums_read = 0
            opair_idx = 0
            kvspill_done = 0
            xload_g = sch.dcnt.get("xc", 0)
            rload_s = sch.dcnt.get("ropek", 0)

            for v in range(NCH):
                buf = v % nbuf
                if pass_id == 0:
                    sch.dwait("scalar", "xc", xload_g)
                    sch.dwait("tensor", "xc", xload_g)
                    sch.wait("tensor", "scalar", max(exp_done))
                    sch.dwait("vector", "spill", kvspill_done)
                    sq_guard = [0, 0]
                    for k in range(KT):
                        xt = xc[:, k * CH: k * CH + 512]
                        par = k % 2
                        sch.wait("scalar", "tensor", sq_guard[par])
                        c_sq = sch.op("scalar", lambda h, o=sqb[par][:], i=xt:
                                      h.activation(o, i, AF.Square))
                        mm(PS[0][0:1, 0:512], inv2048_b, xt, k == 0, k == KT - 1)
                        sch.wait("tensor", "scalar", c_sq)
                        mm(PS[1][0:1, 0:512], inv2048_b, sqb[par][:], k == 0, k == KT - 1)
                        sq_guard[par] = sch.cnt["tensor"]
                    sch.wait("vector", "tensor", sch.cnt["tensor"])
                    sch.op("vector", lambda h: h.tensor_copy(crowm[0:1, :], PS[0][0:1, 0:512]))
                    sch.op("vector", lambda h: h.tensor_copy(crow_b[0:1, :], PS[0][0:1, 0:512]))
                    sch.op("vector", lambda h: h.tensor_tensor(
                        crowr[0:1, :], PS[0][0:1, 0:512], crowm[0:1, :], OP.mult))
                    sch.op("vector", lambda h: h.scalar_tensor_tensor(
                        crowr[0:1, :], crowr[0:1, :], -1.0, PS[1][0:1, 0:512], OP.mult, OP.add))
                    sch.wait("scalar", "vector", sch.cnt["vector"])
                    c_sd = sch.op("scalar", lambda h: h.activation(
                        crowr[0:1, :], crowr[0:1, :], AF.Sqrt, bias=EPS_LN))
                    sch.wait("vector", "scalar", c_sd)
                    sch.op("vector", lambda h: h.reciprocal(crowr[0:1, :], crowr[0:1, :]))
                    v_r = sch.cnt["vector"]
                    sch.wait("tensor", "vector", v_r)
                    trb = mm(PS[2][:, 0:512], ones_row_f, crowr[0:1, :], True, True)
                    sch.wait("vector", "tensor", trb)
                    v_rbc = sch.op("vector", lambda h: h.tensor_copy(rbc[:], PS[2][:, 0:512]))

                    mt = [(0, 128), (128, 128), (256, 128), (384, 128),
                          (512, 64), (576, 64)]
                    banks = (PS[3], PS[4], PS[5], PS[3], PS[4], PS[5])
                    evac_cnt = [0, 0, 0]
                    pe_ps = {}
                    for mi, (c0, M) in enumerate(mt):
                        ps = banks[mi]
                        if mi >= 3:
                            sch.wait("tensor", "vector", evac_cnt[mi % 3])
                        if mi == 0:
                            sch.wait("tensor", "vector", v_rbc)
                        for k in range(KT):
                            lt = w_kva[:, k * 640 + c0: k * 640 + c0 + M]
                            rt = xc[:, k * CH: k * CH + 512]
                            mm(ps[0:M, 0:512], lt, rt, k == 0, False)
                        tmm = mm(ps[0:M, 0:512], nw1[0:1, c0:c0 + M],
                                 crow_b[0:1, :], False, True)
                        sch.wait("vector", "tensor", tmm)
                        if mi < 4:
                            dst = cT[:, mi * CH: mi * CH + 512]
                            evac_cnt[mi % 3] = sch.op(
                                "vector", lambda h, o=dst, a=ps[:, 0:512], b=rbc[:]:
                                h.tensor_tensor(o, a, b, OP.mult))
                        else:
                            pe_ps[mi] = ps
                    sch.dwait("vector", "ropek", rload_s)
                    sch.op("vector", lambda h: h.tensor_tensor(
                        ropt[0][:], pe_ps[5][0:DR, 0:512], rk_s[:], OP.mult))
                    sch.op("vector", lambda h: h.tensor_tensor(
                        ropt[1][:], pe_ps[4][0:DR, 0:512], rk_c[:], OP.mult))
                    sch.op("vector", lambda h: h.tensor_tensor(
                        ropt[0][:], ropt[0][:], ropt[1][:], OP.add))
                    sch.op("vector", lambda h, o=kpe_sb[buf][:]: h.tensor_tensor(
                        o, ropt[0][:], rbc[0:DR, 0:512], OP.mult))
                    v_ckv = sch.cnt["vector"]

                    if v + 1 < NCH:
                        sch.wait("gpsimd", "tensor", sch.cnt["tensor"])
                        sch.wait("gpsimd", "scalar", sch.cnt["scalar"])
                        sch.dma("gpsimd", xc[:].rearrange("p (k t) -> p k t", k=KT),
                                xT[:, (v + 1) * CH: (v + 2) * CH].rearrange(
                                    "(k p) t -> p k t", p=128), sem="xc")
                        xload_g = sch.dcnt["xc"]
                        sch.wait("sync", "vector", v_ckv)
                        sch.dma("sync", rk_c[:], ropek_c[:, (v + 1) * CH: (v + 2) * CH], sem="ropek")
                        sch.dma("sync", rk_s[:], ropek_s[:, (v + 1) * CH: (v + 2) * CH], sem="ropek")
                        rload_s = sch.dcnt["ropek"]

                    sch.wait("tensor", "vector", v_ckv)
                    ev = [0, 0]
                    for h_ in range(NH):
                        ps = s_banks[h_ % 2]
                        sch.wait("tensor", "vector", ev[h_ % 2])
                        for k in range(4):
                            lt = w_kbk[:, k * NH * DN + h_ * DN: k * NH * DN + h_ * DN + 128]
                            rt = cT[:, k * CH: k * CH + 512]
                            mm(ps[:, 0:512], lt, rt, k == 0, k == 3)
                        sch.wait("vector", "tensor", sch.cnt["tensor"])
                        dst = kn_sb[buf][:, h_ * CH: h_ * CH + 512]
                        ev[h_ % 2] = sch.op("vector", lambda h, o=dst, i=ps[:, 0:512]:
                                            h.tensor_copy(o, i))
                    evv = [0, 0, 0, 0]
                    for j in range(4):
                        for n in range(4):
                            bi = (j * 4 + n) % 4
                            ps = (PS[2], PS[3], PS[4], PS[5])[bi]
                            sch.wait("tensor", "vector", evv[bi])
                            for k in range(4):
                                lt = cT[:, k * CH + j * 128: k * CH + j * 128 + 128]
                                rt = w_kbv[:, k * NH * DN + n * 512:
                                           k * NH * DN + n * 512 + 512]
                                mm(ps[:, 0:512], lt, rt, k == 0, k == 3)
                            sch.wait("vector", "tensor", sch.cnt["tensor"])
                            dst = v_sb[buf][:, j * NH * 128 + n * 512:
                                            j * NH * 128 + n * 512 + 512]
                            evv[bi] = sch.op("vector", lambda h, o=dst, i=ps[:, 0:512]:
                                             h.tensor_copy(o, i))
                    v_kv = sch.cnt["vector"]
                    sch.wait("gpsimd", "vector", v_kv)
                    sch.dma("gpsimd", sp_kn[v], kn_sb[buf][:], sem="spill")
                    sch.dma("gpsimd", sp_v[v], v_sb[buf][:], sem="spill")
                    sch.dma("gpsimd", sp_kpe[v], kpe_sb[buf][:], sem="spill")
                    kvspill_done = sch.dcnt["spill"]
                    sch.wait("tensor", "vector", v_kv)
                    exp_done = [0, 0]
                else:
                    if v + 1 < NCH:
                        sch.wait("gpsimd", "tensor", sch.cnt["tensor"])
                        kb = f"kv{1 - buf}"
                        sch.dma("gpsimd", kn_sb[1 - buf][:], sp_kn[v + 1], sem=kb)
                        sch.dma("gpsimd", v_sb[1 - buf][:], sp_v[v + 1], sem=kb)
                        sch.dma("gpsimd", kpe_sb[1 - buf][:], sp_kpe[v + 1], sem=kb)
                        kvload[v + 1] = (kb, sch.dcnt[kb])
                    sch.dwait("tensor", kvload[v][0], kvload[v][1])

                for t in range(v // 2, NSLOT):
                    maskable = (t == v // 2)
                    mk_idx = v - 2 * t
                    first_visit = (v == 0)
                    pid = opair_idx % 2
                    op_pair = o_pairs[pid]
                    sch.wait("tensor", "vector", oadd_done[pid])
                    if maskable:
                        sch.dwait("tensor", mask_ld[(t, mk_idx)][0], mask_ld[(t, mk_idx)][1])
                    sch.wait("scalar", "vector", sums_read)
                    for h_ in range(NHP):
                        sbk = h_ % 2
                        ps = s_banks[sbk]
                        sch.wait("tensor", "scalar", exp_done[sbk])
                        hg = h_ + pass_id * NHP
                        lt1 = qn[:, h_ * OWN + t * 128: h_ * OWN + t * 128 + 128]
                        rt1 = kn_sb[buf][:, hg * CH: hg * CH + 512]
                        mm(ps[:, 0:512], lt1, rt1, True, False)
                        lt2 = qp[0:DR, h_ * OWN + t * 128: h_ * OWN + t * 128 + 128]
                        rt2 = kpe_sb[buf][0:DR, 0:512]
                        mm(ps[:, 0:512], lt2, rt2, False, not maskable)
                        if maskable:
                            mm(ps[:, 0:512], c_idb[:], mk_sb[mk_idx][:], False, True)
                        sch.wait("scalar", "tensor", sch.cnt["tensor"])
                        c_e = sch.op("scalar", lambda h, o=p_sb[sbk][:], i=ps[:, 0:512],
                                     a=sums_vis[:, h_: h_ + 1]:
                                     h.activation(o, i, AF.Exp, scale=SCALE, accum_out=a))
                        exp_done[sbk] = c_e
                        sch.wait("tensor", "scalar", c_e)
                        sch.wait("tensor", "vector", ptevac_done[sbk])
                        for j in range(4):
                            sch.op("tensor", lambda h, o=PTb[sbk][:, j * 128: j * 128 + 128],
                                   i=p_sb[sbk][:, j * 128: j * 128 + 128], idm=c_idb[:]:
                                   h.transpose(o, i, idm))
                        pmm_done[sbk] = sch.cnt["tensor"]
                        sch.wait("vector", "tensor", pmm_done[sbk])
                        c_pt = sch.op("vector", lambda h, o=pt_sb[sbk][:],
                                      i=PTb[sbk][:, 0:512]: h.tensor_copy(o, i))
                        ptevac_done[sbk] = c_pt
                        sch.wait("tensor", "vector", c_pt)
                        opsl = op_pair[h_ // 4]
                        col = (h_ % 4) * 128
                        for j in range(4):
                            lt = v_sb[buf][:, j * NH * 128 + hg * 128:
                                           j * NH * 128 + hg * 128 + 128]
                            rt = pt_sb[sbk][:, j * 128: j * 128 + 128]
                            mm(opsl[:, col: col + 128], lt, rt,
                               (h_ % 4 == 0) and (j == 0), (h_ % 4 == 3) and (j == 3))
                    t_pv = sch.cnt["tensor"]
                    sch.wait("vector", "tensor", t_pv)
                    sch.wait("vector", "scalar", max(exp_done))
                    for pi in range(2):
                        dst = oacc[:, t * NHP * 128 + pi * 512:
                                   t * NHP * 128 + pi * 512 + 512]
                        src = op_pair[pi][:, 0:512]
                        if first_visit:
                            sch.op("vector", lambda h, o=dst, i=src: h.tensor_copy(o, i))
                        else:
                            sch.op("vector", lambda h, o=dst, a=dst, b=src:
                                   h.tensor_tensor(o, a, b, OP.add))
                    ssl = sums_acc[:, t * NHP: (t + 1) * NHP]
                    if first_visit:
                        sch.op("vector", lambda h, o=ssl, i=sums_vis[:]: h.tensor_copy(o, i))
                    else:
                        sch.op("vector", lambda h, o=ssl, a=ssl, b=sums_vis[:]:
                               h.tensor_tensor(o, a, b, OP.add))
                    sums_read = sch.cnt["vector"]
                    oadd_done[pid] = sch.cnt["vector"]
                    opair_idx += 1

                if v + 1 < NCH:
                    nt = (v + 1) // 2
                    nidx = (v + 1) - 2 * nt
                    sch.wait("sync", "tensor", sch.cnt["tensor"])
                    mn = f"mask{pass_id}{nidx}"
                    sch.dma("sync", mk_sb[nidx][:], masks_d[2 * nt + nidx], sem=mn)
                    mask_ld[(nt, nidx)] = (mn, sch.dcnt[mn])

            sch.wait("tensor", "vector", sch.cnt["vector"])
            sch.wait("tensor", "scalar", sch.cnt["scalar"])
            ospill = [0, 0]
            for t in range(NSLOT):
                ob = t % 2
                for h_ in range(NHP):
                    ptv = PTf[h_ // 4]
                    sch.op("tensor", lambda h, i=sums_acc[:, t * NHP + h_:
                                                          t * NHP + h_ + 1],
                           o=ptv[0:1, (h_ % 4) * 128: (h_ % 4) * 128 + 128]:
                           h.transpose(o, i, c_idf[:]))
                sch.wait("vector", "tensor", sch.cnt["tensor"])
                sch.op("vector", lambda h: h.reciprocal(
                    rsum_sb[0:1, 0:512], PTf[0][0:1, 0:512]))
                vv = sch.op("vector", lambda h: h.reciprocal(
                    rsum_sb[0:1, 512:1024], PTf[1][0:1, 0:512]))
                sch.dwait("vector", f"onorm{pass_id}{ob}", ospill[ob])
                for hb in range(2):
                    sch.wait("tensor", "vector", vv)
                    for hh in range(4):
                        h_ = hb * 4 + hh
                        mm(PS[0][:, hh * 128: hh * 128 + 128], ones_row_f,
                           rsum_sb[0:1, h_ * 128: h_ * 128 + 128], True, True)
                    sch.wait("vector", "tensor", sch.cnt["tensor"])
                    off = t * NHP * 128 + hb * 512
                    vv = sch.op("vector", lambda h, o=onorm[ob][:, hb * 512: hb * 512 + 512],
                                a=oacc[:, off: off + 512], b=PS[0][:, 0:512]:
                                h.tensor_tensor(o, a, b, OP.mult))
                sch.wait("gpsimd", "vector", vv)
                sch.dma("gpsimd", odst[:, t * NHP * 128: (t + 1) * NHP * 128],
                        onorm[ob][:], sem=f"onorm{pass_id}{ob}")
                ospill[ob] = sch.dcnt[f"onorm{pass_id}{ob}"]
            sch.barrier()

        with contextlib.ExitStack() as pa:
            attn_pass(0, pa, qnA[:], qpA[:], sp_oA)
        with contextlib.ExitStack() as pb:
            sch.dwait("gpsimd", "qspill")
            sch.dma("gpsimd", qnA[:], sp_qnB, sem="qload")
            sch.dma("gpsimd", qpA[:], sp_qpB, sem="qload")
            sch.dwait("tensor", "qload")
            attn_pass(1, pb, qnA[:], qpA[:], sp_oB)
        stack_q.close()

        stack_ln2 = contextlib.ExitStack()
        ln2 = sb(stack_ln2, "ln2", [128, KT * OWN], BF16, side="right")

        with contextlib.ExitStack() as p3:
            xo = sb(p3, "xo", [128, KT * OWN], F32)
            ab = sb(p3, "ab", [128, KT * OWN], BF16)
            wo_sb = [sb(p3, f"wo_sb{i}", [128, KT * 128], BF16) for i in range(2)]
            sqb3 = [sb(p3, f"p3sq{i}", [128, 512], BF16) for i in range(2)]
            rows3m = sb(p3, "p3rowm", [1, OWN], F32)
            rows3r = sb(p3, "p3rowr", [1, OWN], F32)
            rows3_b = sb(p3, "p3rows_b", [1, OWN], BF16)
            tmp3 = [sb(p3, f"p3tmp{i}", [128, 512], F32) for i in range(2)]

            oA = sb(p3, "oA3", [128, NSLOT * NHP * 128], BF16)
            oB = sb(p3, "oB3", [128, NSLOT * NHP * 128], BF16)
            sch.dma("sync", oA[:], sp_oA, sem="p3ld")
            sch.dma("sync", oB[:], sp_oB, sem="p3ld")
            sch.dma("sync", xo[:].rearrange("p (k t) -> p k t", k=KT), xqT.rearrange("(k p) t -> p k t", p=128), sem="p3ld")
            sch.dwait("tensor", "p3ld")
            sch.dma("gpsimd", wo_sb[0][:].rearrange("p (k t) -> p k t", k=KT),
                    wo_d[:, 0:128].rearrange("(k p) m2 -> p k m2", p=128), sem="wo0")
            wo_ld = [16, 0]

            oab = (oA, oB)
            prev_ev = 0
            for m in range(KT):
                if m + 1 < KT:
                    sch.wait("gpsimd", "tensor", sch.cnt["tensor"])
                    wn = f"wo{(m + 1) % 2}"
                    sch.dma("gpsimd", wo_sb[(m + 1) % 2][:].rearrange("p (k t) -> p k t", k=KT),
                            wo_d[:, (m + 1) * 128: (m + 2) * 128].rearrange(
                                "(k p) m2 -> p k m2", p=128), sem=wn)
                    wo_ld[(m + 1) % 2] = sch.dcnt[wn]
                ppair = (PS[2], PS[3]) if m % 2 == 0 else (PS[4], PS[5])
                sch.dwait("tensor", f"wo{m % 2}", wo_ld[m % 2])
                sch.wait("tensor", "vector", prev_ev)
                for half in range(2):
                    ps = ppair[half]
                    for k in range(NH):
                        lt = wo_sb[m % 2][:, k * 128: k * 128 + 128]
                        src = oab[k // NHP]
                        hh = k % NHP
                        rt = src[:].rearrange("p (t x) -> p t x", t=NSLOT)[
                            :, half * 4: half * 4 + 4, hh * 128: hh * 128 + 128]
                        mm(ps[:, 0:512], lt, rt, k == 0, k == NH - 1)
                sch.wait("vector", "tensor", sch.cnt["tensor"])
                for half in range(2):
                    dst = ab[:, m * OWN + half * 512: m * OWN + half * 512 + 512]
                    prev_ev = sch.op("vector", lambda h, o=dst, i=ppair[half][:, 0:512]:
                                     h.tensor_copy(o, i))

            stats_rows(lambda k, half: ab[:, k * OWN + half * 512:
                                          k * OWN + half * 512 + 512],
                       sqb3, rows3m, rows3r, rows3_b)
            sch.dwait("vector", "p3ld")
            v_x1 = ln_apply(
                lambda k, half: ab[:, k * OWN + half * 512: k * OWN + half * 512 + 512],
                lambda k, half: xo[:, k * OWN + half * 512: k * OWN + half * 512 + 512],
                rows3r, rows3_b, tmp3,
                extra_res=lambda k, half: xo[:, k * OWN + half * 512:
                                             k * OWN + half * 512 + 512])
            for k in range(KT):
                for half in range(2):
                    xsl = xo[:, k * OWN + half * 512: k * OWN + half * 512 + 512]
                    bsl = ln2[:, k * OWN + half * 512: k * OWN + half * 512 + 512]
                    sch.op("vector", lambda h, o=bsl, i=xsl: h.tensor_copy(o, i))
            v_x1b = sch.cnt["vector"]
            sch.wait("gpsimd", "vector", v_x1)
            sch.dma("gpsimd", sp_x1, xo[:], sem="x1sp")

            sch.wait("tensor", "vector", v_x1b)
            stats_rows(lambda k, half: ln2[:, k * OWN + half * 512:
                                           k * OWN + half * 512 + 512],
                       sqb3, rows3m, rows3r, rows3_b)
            ln_apply(
                lambda k, half: ln2[:, k * OWN + half * 512: k * OWN + half * 512 + 512],
                lambda k, half: ln2[:, k * OWN + half * 512: k * OWN + half * 512 + 512],
                rows3r, rows3_b, tmp3)
            sch.barrier()

        stack_sg = contextlib.ExitStack()
        sgh = sb(stack_sg, "sgh", [128, 44 * OWN], BF16)

        with contextlib.ExitStack() as p4:
            wraw = [sb(p4, f"wraw{i}", [128, 512], I16) for i in range(2)]
            wstage = [sb(p4, f"wstage{i}", [128, 512], BF16) for i in range(2)]
            wsc = [sb(p4, f"wsc{i}", [128, 512], F16) for i in range(2)]
            dq = [sb(p4, f"dq{i}", [128, 512], I16) for i in range(3)]
            wsl = [sb(p4, f"wsl{i}", [128, 16 * 512], BF16) for i in range(2)]

            NSL = 22
            wsl_free_t = [0, 0]
            wsl_ready = [0, 0]
            raw_free = [0, 0]
            raw_ld = [0, 0]
            evac_v = [0, 0, 0, 0]
            evac_s = [0, 0, 0, 0]
            for s in range(NSL):
                cur = s % 2
                sch.wait("vector", "tensor", wsl_free_t[cur])
                for g in range(16):
                    par = g % 2
                    sch.wait("gpsimd", "vector", raw_free[par])
                    rn = f"raw{par}"
                    sch.dma("gpsimd", wraw[par][:],
                            gu_rep[g * 128: (g + 1) * 128, s * 512: (s + 1) * 512], sem=rn)
                    sch.dma("gpsimd", wsc[par][:],
                            gu_sc_d[g: g + 1, s * 512: (s + 1) * 512].broadcast_to([128, 512]), sem=rn)
                    raw_ld[par] = sch.dcnt[rn]
                    sch.dwait("vector", rn, raw_ld[par])
                    sch.op("vector", lambda h, w=wraw[par][:]: h.tensor_tensor(
                        dq[0][:], w, c_shfull[:], OP.logical_shift_right))
                    raw_free[par] = sch.cnt["vector"]
                    sch.op("vector", lambda h: h.tensor_scalar(
                        dq[1][:], dq[0][:], 3, 1, OP.logical_shift_right, OP.bitwise_and))
                    sch.op("vector", lambda h: h.tensor_scalar(
                        dq[0][:], dq[0][:], 7, None, OP.bitwise_and, OP.bypass))
                    sch.op("vector", lambda h: h.tensor_scalar(
                        dq[2][:], dq[0][:], 252.0, None, OP.add, OP.bypass))
                    sch.op("vector", lambda h: h.scalar_tensor_tensor(
                        dq[0][:], dq[0][:], 252.0, dq[2][:], OP.mult, OP.min))
                    sch.op("vector", lambda h: h.tensor_scalar(
                        dq[0][:], dq[0][:], 6, None, OP.logical_shift_left, OP.bypass))
                    sch.op("vector", lambda h: h.tensor_scalar(
                        dq[1][:], dq[1][:], 15, None, OP.logical_shift_left, OP.bypass))
                    sch.op("vector", lambda h: h.tensor_tensor(
                        dq[2][:], dq[1][:], dq[0][:], OP.bitwise_or))
                    dst = wsl[cur][:, g * 512: (g + 1) * 512]
                    raw_free[par] = sch.op("vector", lambda h, o=dst, w=wsc[par][:]:
                                           h.tensor_tensor(o, dq[2][:].bitcast(BF16),
                                                           w, OP.mult))
                wsl_ready[cur] = sch.cnt["vector"]

                for dt_ in range(s * 8, min(s * 8 + 8, 176)):
                    g5, os5 = dt_ % 44, dt_ // 44
                    par = dt_ % 2
                    sch.wait("gpsimd", "vector", raw_free[par])
                    rn = f"raw{par}"
                    sch.dma("gpsimd", wraw[par][:],
                            dn_rep[g5 * 128: (g5 + 1) * 128, os5 * 512: (os5 + 1) * 512],
                            sem=rn)
                    sch.dma("gpsimd", wsc[par][:],
                            dn_sc_d[g5: g5 + 1, os5 * 512: (os5 + 1) * 512].broadcast_to([128, 512]),
                            sem=rn)
                    raw_ld[par] = sch.dcnt[rn]
                    sch.dwait("vector", rn, raw_ld[par])
                    sch.dwait("vector", f"wdsp{par}")
                    sch.op("vector", lambda h, w=wraw[par][:]: h.tensor_tensor(
                        dq[0][:], w, c_shfull[:], OP.logical_shift_right))
                    raw_free[par] = sch.cnt["vector"]
                    sch.op("vector", lambda h: h.tensor_scalar(
                        dq[1][:], dq[0][:], 3, 1, OP.logical_shift_right, OP.bitwise_and))
                    sch.op("vector", lambda h: h.tensor_scalar(
                        dq[0][:], dq[0][:], 7, None, OP.bitwise_and, OP.bypass))
                    sch.op("vector", lambda h: h.tensor_scalar(
                        dq[2][:], dq[0][:], 252.0, None, OP.add, OP.bypass))
                    sch.op("vector", lambda h: h.scalar_tensor_tensor(
                        dq[0][:], dq[0][:], 252.0, dq[2][:], OP.mult, OP.min))
                    sch.op("vector", lambda h: h.tensor_scalar(
                        dq[0][:], dq[0][:], 6, None, OP.logical_shift_left, OP.bypass))
                    sch.op("vector", lambda h: h.tensor_scalar(
                        dq[1][:], dq[1][:], 15, None, OP.logical_shift_left, OP.bypass))
                    sch.op("vector", lambda h: h.tensor_tensor(
                        dq[2][:], dq[1][:], dq[0][:], OP.bitwise_or))
                    vws = sch.op("vector", lambda h, o=wstage[par][:], w=wsc[par][:]:
                                 h.tensor_tensor(o, dq[2][:].bitcast(BF16), w, OP.mult))
                    sch.wait("gpsimd", "vector", vws)
                    sch.dma("gpsimd", sp_wd[g5, :, os5 * 512: (os5 + 1) * 512],
                            wstage[par][:], sem=f"wdsp{par}")

                sch.wait("tensor", "vector", wsl_ready[cur])
                for grp in range(2):
                    for g in range(16):
                        for m2 in range(2):
                            m4 = grp * 2 + m2
                            for half in range(2):
                                bi = m2 * 2 + half
                                ps = (PS[2], PS[3], PS[4], PS[5])[bi]
                                if g == 0:
                                    sch.wait("tensor", "vector", evac_v[bi])
                                    sch.wait("tensor", "scalar", evac_s[bi])
                                lt = wsl[cur][:, g * 512 + m4 * 128:
                                              g * 512 + m4 * 128 + 128]
                                rt = ln2[:, g * OWN + half * 512:
                                         g * OWN + half * 512 + 512]
                                mm(ps[:, 0:512], lt, rt, g == 0, g == 15)
                    t4 = sch.cnt["tensor"]
                    for m2 in range(2):
                        m4 = grp * 2 + m2
                        for half in range(2):
                            bi = m2 * 2 + half
                            ps = (PS[2], PS[3], PS[4], PS[5])[bi]
                            if s < 11:
                                dst = sgh[:, (s * 4 + m4) * OWN + half * 512:
                                          (s * 4 + m4) * OWN + half * 512 + 512]
                                sch.wait("scalar", "tensor", t4)
                                evac_s[bi] = sch.op(
                                    "scalar", lambda h, o=dst, i=ps[:, 0:512]:
                                    h.activation(o, i, AF.Silu))
                            else:
                                su = s - 11
                                dst = sgh[:, (su * 4 + m4) * OWN + half * 512:
                                          (su * 4 + m4) * OWN + half * 512 + 512]
                                sch.wait("vector", "tensor", t4)
                                evac_v[bi] = sch.op(
                                    "vector", lambda h, o=dst, a=dst, b=ps[:, 0:512]:
                                    h.tensor_tensor(o, a, b, OP.mult))
                wsl_free_t[cur] = sch.cnt["tensor"]
            sch.barrier()
        stack_ln2.close()

        stack_y = contextlib.ExitStack()
        y_sb = sb(stack_y, "y_sb", [128, KT * OWN], F32, side="right")

        with contextlib.ExitStack() as p5:
            wdb = [sb(p5, f"wdb{i}", [128, 8 * 256], BF16) for i in range(2)]

            ev5 = [0, 0, 0, 0]
            batches = []
            for os2 in range(8):
                for b0 in range(0, 44, 8):
                    batches.append((os2, list(range(b0, min(b0 + 8, 44)))))
            wb_ready = [0, 0]
            wb_free = [0, 0]

            def load_batch(bi_, os2, gl):
                cur = bi_ % 2
                wn = f"wdl{cur}"
                sch.dwait("gpsimd", f"wdsp0")
                sch.dwait("gpsimd", f"wdsp1")
                sch.wait("gpsimd", "tensor", wb_free[cur])
                for j, g in enumerate(gl):
                    sch.dma("gpsimd", wdb[cur][:, j * 256: j * 256 + 256],
                            sp_wd[g, :, os2 * 256: (os2 + 1) * 256], sem=wn)
                wb_ready[cur] = sch.dcnt[wn]

            load_batch(0, *batches[0])
            for bi_, (os2, gl) in enumerate(batches):
                cur = bi_ % 2
                if bi_ + 1 < len(batches):
                    load_batch(bi_ + 1, *batches[bi_ + 1])
                sch.dwait("tensor", f"wdl{cur}", wb_ready[cur])
                for j, g in enumerate(gl):
                    for m2 in range(2):
                        for half in range(2):
                            b_ = m2 * 2 + half
                            ps = (PS[2], PS[3], PS[4], PS[5])[b_]
                            if g == 0:
                                sch.wait("tensor", "vector", ev5[b_])
                            lt = wdb[cur][:, j * 256 + m2 * 128: j * 256 + m2 * 128 + 128]
                            rt = sgh[:, g * OWN + half * 512: g * OWN + half * 512 + 512]
                            mm(ps[:, 0:512], lt, rt, g == 0, g == 43)
                wb_free[cur] = sch.cnt["tensor"]
                if gl[-1] == 43:
                    t5 = sch.cnt["tensor"]
                    sch.wait("vector", "tensor", t5)
                    for m2 in range(2):
                        for half in range(2):
                            b_ = m2 * 2 + half
                            ps = (PS[2], PS[3], PS[4], PS[5])[b_]
                            o_tile = os2 * 2 + m2
                            dst = y_sb[:, o_tile * OWN + half * 512:
                                       o_tile * OWN + half * 512 + 512]
                            ev5[b_] = sch.op("vector", lambda h, o=dst, i=ps[:, 0:512]:
                                             h.tensor_copy(o, i))
            sch.barrier()
        stack_sg.close()

        with contextlib.ExitStack() as p6:
            xo2 = sb(p6, "xo2", [128, KT * OWN], F32)
            sqb6 = [sb(p6, f"p6sq{i}", [128, 512], BF16) for i in range(2)]
            rows6m = sb(p6, "p6rowm", [1, OWN], F32)
            rows6r = sb(p6, "p6rowr", [1, OWN], F32)
            rows6_b = sb(p6, "p6rows_b", [1, OWN], BF16)
            tmp6 = [sb(p6, f"p6tmp{i}", [128, 512], F32) for i in range(2)]
            outsb = [sb(p6, f"outsb{i}", [128, D], F32) for i in range(2)]

            sch.dwait("gpsimd", "x1sp")
            sch.dma("gpsimd", xo2[:], sp_x1, sem="x1ld")

            stats_rows(lambda k, half: y_sb[:, k * OWN + half * 512:
                                            k * OWN + half * 512 + 512],
                       sqb6, rows6m, rows6r, rows6_b, src_f32=True)
            sch.dwait("vector", "x1ld")
            v_x2 = ln_apply(
                lambda k, half: y_sb[:, k * OWN + half * 512: k * OWN + half * 512 + 512],
                lambda k, half: xo2[:, k * OWN + half * 512: k * OWN + half * 512 + 512],
                rows6r, rows6_b, tmp6,
                extra_res=lambda k, half: xo2[:, k * OWN + half * 512:
                                              k * OWN + half * 512 + 512])

            tr_roll = [0, 0]
            out_roll = [0, 0]
            for ti in range(NSLOT):
                ob = ti % 2
                sch.dwait("vector", f"out{ob}", out_roll[ob])
                sch.wait("tensor", "vector", v_x2)
                for k in range(KT):
                    par = k % 2
                    ps = (PS[0], PS[1])[par]
                    sch.wait("tensor", "vector", tr_roll[par])
                    sch.op("tensor", lambda h, o=ps[:, 0:128],
                           i=xo2[:, k * OWN + ti * 128: k * OWN + ti * 128 + 128],
                           idm=c_idf[:]: h.transpose(o, i, idm))
                    sch.wait("vector", "tensor", sch.cnt["tensor"])
                    tr_roll[par] = sch.op(
                        "vector", lambda h, o=outsb[ob][:, k * 128: k * 128 + 128],
                        i=ps[:, 0:128]: h.tensor_copy(o, i))
                sch.wait("gpsimd", "vector", sch.cnt["vector"])
                sch.dma("gpsimd", out_tok[ti * 128: (ti + 1) * 128, :], outsb[ob][:],
                        sem=f"out{ob}")
                out_roll[ob] = sch.dcnt[f"out{ob}"]
            sch.dwait("gpsimd", "out0")
            sch.dwait("gpsimd", "out1")
        stack_y.close()

        sch.finalize()
        with nc.Block() as block:
            @block.tensor
            def _(eng):
                for f in sch.ops["tensor"]:
                    f(eng)

            @block.vector
            def _(eng):
                for f in sch.ops["vector"]:
                    f(eng)

            @block.scalar
            def _(eng):
                for f in sch.ops["scalar"]:
                    f(eng)

            @block.sync
            def _(eng):
                for f in sch.ops["sync"]:
                    f(eng)

            @block.gpsimd
            def _(eng):
                for f in sch.ops["gpsimd"]:
                    f(eng)

    return nc



def _rope_tables(pos, n=DR):
    inv = 10000.0 ** (-np.arange(0, n, 2, dtype=np.float64) / n)
    ang = pos[None, :].astype(np.float64) * inv[:, None]
    c = np.cos(ang).astype(np.float32)
    s = np.sin(ang).astype(np.float32)
    C = np.empty((n, pos.shape[0]), np.float32)
    S = np.empty((n, pos.shape[0]), np.float32)
    C[0::2] = c
    C[1::2] = c
    S[0::2] = -s
    S[1::2] = s
    return np.ascontiguousarray(C), np.ascontiguousarray(S)


def _build_masks(core):
    out = np.zeros((NSLOT * 2, 128, CH), np.float32)
    for t in range(NSLOT):
        g = 8 * t + core
        k_cnt = g // 4 + 1
        jdiag = g % 4
        diag = np.zeros((128, CH), np.float32)
        diag[:, (jdiag + 1) * 128:] = NEG_MASK
        tri = np.triu(np.full((128, 128), NEG_MASK, np.float32), 1)
        diag[:, jdiag * 128: (jdiag + 1) * 128] = tri
        if k_cnt == 2 * t + 2:
            out[2 * t + 0] = 0.0
            out[2 * t + 1] = diag
        else:
            out[2 * t + 0] = diag
            out[2 * t + 1] = NEG_MASK
    return np.ascontiguousarray(out.astype(BF))


def kernel(L0_attn_x, L0_attn_Wqa, L0_attn_Wqb, L0_attn_Wkva, L0_attn_Wkvb,
           L0_attn_Wo, L0_ffn_gate_up_packed, L0_ffn_gate_up_scales,
           L0_ffn_down_packed, L0_ffn_down_scales):
    x = np.ascontiguousarray(np.asarray(L0_attn_x)[0])
    xT = np.ascontiguousarray(x.T)
    Wqa = np.ascontiguousarray(np.asarray(L0_attn_Wqa))
    Wqb = np.asarray(L0_attn_Wqb)
    Wkva = np.asarray(L0_attn_Wkva)
    Wkvb = np.asarray(L0_attn_Wkvb)
    Wo = np.ascontiguousarray(np.asarray(L0_attn_Wo))

    perm = np.arange(DR) ^ 1

    q3 = Wqb.reshape(QL, NH, DN + DR)
    wqbn = np.ascontiguousarray(q3[:, :, :DN].reshape(QL, NH * DN))
    wqbp = np.ascontiguousarray(q3[:, :, DN:].reshape(QL, NH * DR))
    wqbps = np.ascontiguousarray(q3[:, :, DN:][:, :, perm].reshape(QL, NH * DR))

    wkva_ext = np.ascontiguousarray(
        np.concatenate([Wkva, Wkva[:, KVL:KVL + DR][:, perm]], axis=1))

    kv3 = Wkvb.reshape(KVL, NH, 2 * DN)
    wkvbk = np.ascontiguousarray(kv3[:, :, :DN].reshape(KVL, NH * DN))
    wkvbv = np.ascontiguousarray(kv3[:, :, DN:].reshape(KVL, NH * DN))

    gu16 = np.asarray(L0_ffn_gate_up_packed).view(np.int16)
    gu_rep_h = np.ascontiguousarray(np.repeat(gu16.T, 4, axis=0))
    dn16 = np.asarray(L0_ffn_down_packed).view(np.int16)
    dn_rep_h = np.ascontiguousarray(np.repeat(dn16.T, 4, axis=0))
    gu_sc = np.ascontiguousarray(np.asarray(L0_ffn_gate_up_scales))
    dn_sc = np.ascontiguousarray(np.asarray(L0_ffn_down_scales))

    identb = np.ascontiguousarray(np.eye(128, dtype=BF))
    identf = np.ascontiguousarray(np.eye(128, dtype=np.float32))
    colsb = np.ascontiguousarray(
        np.stack([np.full(128, 1.0 / 2048.0), np.ones(128)], axis=1).astype(BF))
    colsf = np.ascontiguousarray(
        np.stack([np.full(128, 1.0 / 2048.0), np.ones(128),
                  np.full(128, EPS_LN)], axis=1).astype(np.float32))
    rowb = np.ones((1, 128), BF)
    rowf = np.ones((1, 128), np.float32)
    shf = np.ascontiguousarray(
        np.stack([4 * (np.arange(128) % 4), 4 * (np.arange(128) % 4) + 3],
                 axis=1).astype(np.int16))
    shfull = np.ascontiguousarray(
        np.broadcast_to((4 * (np.arange(128) % 4)).astype(np.int16)[:, None],
                        (128, 512)).copy())

    kc, ks = _rope_tables(np.arange(SEQ))

    nc = build_program()

    in_maps = []
    all_rows = []
    for c in range(N_CORES):
        rows = np.concatenate([128 * (8 * t + c) + np.arange(128)
                               for t in range(NSLOT)])
        all_rows.append(rows)
        xq = np.ascontiguousarray(xT[:, rows])
        qc, qs = _rope_tables(rows.astype(np.float64))
        in_maps.append(dict(
            xT=xT, xqT=xq,
            ropeq_c=qc, ropeq_s=qs, ropek_c=kc, ropek_s=ks,
            masks=_build_masks(c),
            wqa=Wqa, wqbn=wqbn, wqbp=wqbp, wqbps=wqbps,
            wkva=wkva_ext, wkvbk=wkvbk, wkvbv=wkvbv, wo=Wo,
            gu_rep=gu_rep_h, gu_sc=gu_sc, dn_rep=dn_rep_h, dn_sc=dn_sc,
            identb=identb, identf=identf, colsb=colsb, colsf=colsf,
            rowb=rowb, rowf=rowf, shf=shf, shfull=shfull,
        ))

    res = run_bass_kernel_spmd(nc, in_maps, core_ids=list(range(N_CORES)))

    out = np.empty((1, SEQ, D), np.float32)
    for c in range(N_CORES):
        out[0, all_rows[c], :] = np.asarray(res.results[c]["out_tok"])
    return out


# version 05
# speedup vs baseline: 1.1397x; 1.1397x over previous
import contextlib
import numpy as np
import ml_dtypes

import concourse.bass as bass
import concourse.mybir as mybir
from concourse.bass_utils import run_bass_kernel_spmd
from concourse.alu_op_type import AluOpType as OP

F32 = mybir.dt.float32
BF16 = mybir.dt.bfloat16
F16 = mybir.dt.float16
I16 = mybir.dt.int16
AF = mybir.ActivationFunctionType

N_CORES = 8
SEQ, D, NH, DN, DR = 8192, 2048, 16, 128, 64
QL, KVL = 512, 512
CH = 512
NCH = SEQ // CH
NSLOT = 8
OWN = NSLOT * 128
KT = D // 128
SCALE = float(np.float32(1.0) / np.float32(np.sqrt(np.float32(192.0))))
NEG_MASK = -1e30
NHP = 8
EPS_LN = 1e-5
I_FF = 5632

ENGS = ("tensor", "vector", "scalar", "sync", "gpsimd")
BF = ml_dtypes.bfloat16


class Sched:
    def __init__(self, nc):
        self.nc = nc
        self.ops = {e: [] for e in ENGS}
        self.cnt = {e: 0 for e in ENGS}
        self.sems = {}
        self.last_wait = {}
        self.dsems = {}
        self.dcnt = {}
        self._stack = None
        self.waited = {e: set() for e in ENGS}
        self.inc_at = {e: {} for e in ENGS}

    def setup(self, stack):
        self._stack = stack
        for e in ENGS:
            self.sems[e] = stack.enter_context(self.nc.semaphore("sem_" + e))

    def dsem(self, name):
        if name not in self.dsems:
            self.dsems[name] = self._stack.enter_context(
                self.nc.semaphore("d_" + name))
            self.dcnt[name] = 0
        return self.dsems[name]

    def op(self, eng, fn, inc=1):
        sem = self.sems[eng]
        self.cnt[eng] += inc
        ordinal = self.cnt[eng]

        def run(h, fn=fn, sem=sem, eng=eng, ordinal=ordinal):
            inst = fn(h)
            d = self.inc_at[eng].get(ordinal)
            if d:
                inst.then_inc(sem, d)

        self.ops[eng].append(run)
        return ordinal

    def dma(self, eng, out, in_, sem=None):
        assert sem is not None
        s = self.dsem(sem)

        def run(h, o=out, i=in_, s=s):
            h.dma_start(out=o, in_=i).then_inc(s, 16)

        self.ops[eng].append(run)
        self.dcnt[sem] += 16
        return self.dcnt[sem]

    def dwait(self, eng, sem, cnt=None):
        s = self.dsem(sem)
        if cnt is None:
            cnt = self.dcnt[sem]
        if cnt <= 0:
            return
        key = (eng, "d_" + sem)
        if self.last_wait.get(key, -1) >= cnt:
            return
        self.last_wait[key] = cnt
        self.ops[eng].append(lambda h, s=s, c=cnt: h.wait_ge(s, c))

    def wait(self, eng, on, cnt=None):
        if cnt is None:
            cnt = self.cnt[on]
        if cnt <= 0:
            return
        key = (eng, on)
        if self.last_wait.get(key, -1) >= cnt:
            return
        self.last_wait[key] = cnt
        self.waited[on].add(cnt)
        sem = self.sems[on]
        self.ops[eng].append(lambda h, s=sem, c=cnt: h.wait_ge(s, c))

    def finalize(self):
        for e in ENGS:
            prev = 0
            for c in sorted(self.waited[e]):
                if c > self.cnt[e]:
                    continue
                self.inc_at[e][c] = c - prev
                prev = c

    def barrier(self):
        snap = dict(self.cnt)
        dsnap = dict(self.dcnt)
        for e in ENGS:
            for o in ENGS:
                if o != e:
                    self.wait(e, o, snap[o])
            for name, c in dsnap.items():
                self.dwait(e, name, c)


def build_program():
    nc = bass.Bass("TRN2", target_bir_lowering=False, debug=False,
                   detect_race_conditions=False)

    def din(name, shape, dt):
        return nc.dram_tensor(name, shape, dt, kind="ExternalInput").ap()

    def dint(name, shape, dt):
        return nc.dram_tensor(name, shape, dt, kind="Internal").ap()

    xT = din("xT", [D, SEQ], F32)
    xqT = din("xqT", [D, OWN], F32)
    ropeq_c = din("ropeq_c", [DR, OWN], F32)
    ropeq_s = din("ropeq_s", [DR, OWN], F32)
    ropek_c = din("ropek_c", [DR, SEQ], F32)
    ropek_s = din("ropek_s", [DR, SEQ], F32)
    masks_d = din("masks", [NSLOT * 2, 128, CH], BF16)
    wqa_d = din("wqa", [D, QL], F32)
    wqbn_d = din("wqbn", [QL, NH * DN], F32)
    wqbp_d = din("wqbp", [QL, NH * DR], F32)
    wqbps_d = din("wqbps", [QL, NH * DR], F32)
    wkva_d = din("wkva", [D, 640], F32)
    wkvbk_d = din("wkvbk", [KVL, NH * DN], F32)
    wkvbv_d = din("wkvbv", [KVL, NH * DN], F32)
    wo_d = din("wo", [D, D], F32)
    gu_rep = din("gu_rep", [D, 2 * I_FF], I16)
    gu_sc_d = din("gu_sc", [16, 2 * I_FF], F16)
    dn_rep = din("dn_rep", [I_FF, D], I16)
    dn_sc_d = din("dn_sc", [44, D], F16)
    identb_d = din("identb", [128, 128], BF16)
    identf_d = din("identf", [128, 128], F32)
    colsb_d = din("colsb", [128, 2], BF16)
    colsf_d = din("colsf", [128, 3], F32)
    rowb_d = din("rowb", [1, 128], BF16)
    rowf_d = din("rowf", [1, 128], F32)
    shf_d = din("shf", [128, 2], I16)
    shfull_d = din("shfull", [128, 512], I16)

    out_tok = nc.dram_tensor("out_tok", [OWN, D], F32, kind="ExternalOutput").ap()

    sp_kn = dint("sp_kn", [NCH, 128, NH * CH], BF16)
    sp_v = dint("sp_v", [NCH, 128, 4 * NH * 128], BF16)
    sp_kpe = dint("sp_kpe", [NCH, DR, CH], BF16)
    sp_qnB = dint("sp_qnB", [128, NHP * OWN], BF16)
    sp_qpB = dint("sp_qpB", [DR, NHP * OWN], BF16)
    sp_x1 = dint("sp_x1", [128, KT * OWN], F32)
    sp_wd = dint("sp_wd", [44, 128, D], BF16)
    sp_oA = dint("sp_oA", [128, NSLOT * NHP * 128], BF16)
    sp_oB = dint("sp_oB", [128, NSLOT * NHP * 128], BF16)

    sch = Sched(nc)

    def mm(out, lhsT, rhs, start, stop):
        return sch.op("tensor", lambda h, o=out, l=lhsT, r=rhs, s=start, e=stop:
                      h.matmul(o, l, r, start=s, stop=e))

    with contextlib.ExitStack() as root:
        sch.setup(root)

        def sb(st, name, shape, dt, side=None):
            if side is None:
                return st.enter_context(nc.sbuf_tensor(name, shape, dt))
            return st.enter_context(nc.sbuf_tensor(name, shape, dt, side=side))

        c_idb = sb(root, "c_idb", [128, 128], BF16)
        c_idf = sb(root, "c_idf", [128, 128], F32)
        c_colsb = sb(root, "c_colsb", [128, 2], BF16)
        c_colsf = sb(root, "c_colsf", [128, 3], F32)
        c_rowb = sb(root, "c_rowb", [1, 128], BF16)
        c_rowf = sb(root, "c_rowf", [1, 128], F32)
        c_shf = sb(root, "c_shf", [128, 2], I16)
        c_shfull = sb(root, "c_shfull", [128, 512], I16)
        sums_acc = sb(root, "sums_acc", [128, NSLOT * NHP], F32)
        bfc = [sb(root, f"bfc{i}", [128, 512], BF16) for i in range(2)]
        rsum_sb = sb(root, "rsum_sb", [1, NHP * 128], F32)

        inv2048_b = c_colsb[:, 0:1]
        inv2048_f = c_colsf[:, 0:1]
        nc.const_aps.register(F32, EPS_LN, c_colsf[:, 2:3])
        ones_row_b = c_rowb[:, :]
        ones_row_f = c_rowf[:, :]
        shifts = c_shf[:, 0:1]
        shifts3 = c_shf[:, 1:2]

        PS = [root.enter_context(nc.psum_tensor(f"ps{i}", [128, 512], F32))
              for i in range(6)]
        PTb = [root.enter_context(nc.psum_tensor(f"pt{i}", [128, 1024], BF16))
               for i in range(2)]
        PTf = [p.ap().bitcast(F32) for p in PTb]

        sch.dma("sync", c_idb[:], identb_d, sem="consts")
        sch.dma("sync", c_idf[:], identf_d, sem="consts")
        sch.dma("sync", c_colsb[:], colsb_d, sem="consts")
        sch.dma("sync", c_colsf[:], colsf_d, sem="consts")
        sch.dma("sync", c_rowb[:], rowb_d, sem="consts")
        sch.dma("sync", c_rowf[:], rowf_d, sem="consts")
        sch.dma("sync", c_shf[:], shf_d, sem="consts")
        sch.dma("sync", c_shfull[:], shfull_d, sem="consts")
        for e in ("tensor", "vector", "scalar", "gpsimd", "sync"):
            sch.dwait(e, "consts")

        def stats_rows(get_tile, sqbufs, rows_m, rows_r, rows_b, width=512, src_f32=False):
            sch.wait("scalar", "vector", sch.cnt["vector"])
            sch.wait("tensor", "vector", sch.cnt["vector"])
            for half in range(2):
                mrow = (PS[0] if half == 0 else PS[1])[0:1, 0:width]
                s2row = (PTf[0] if half == 0 else PTf[1])[0:1, 0:width]
                sq_guard = [0, 0]
                bf_guard = [0, 0]
                for k in range(KT):
                    xt = get_tile(k, half)
                    par = k % 2
                    sch.wait("scalar", "tensor", sq_guard[par])
                    c_sq = sch.op("scalar", lambda h, o=sqbufs[par][:, 0:width], i=xt:
                                  h.activation(o, i, AF.Square))
                    if src_f32:
                        btile = bfc[par][:, 0:width]
                        sch.wait("vector", "tensor", bf_guard[par])
                        vb = sch.op("vector", lambda h, o=btile, i=xt: h.tensor_copy(o, i))
                        sch.wait("tensor", "vector", vb)
                        mm(mrow, inv2048_b, btile, k == 0, k == KT - 1)
                        bf_guard[par] = sch.cnt["tensor"]
                    else:
                        mm(mrow, inv2048_b, xt, k == 0, k == KT - 1)
                    sch.wait("tensor", "scalar", c_sq)
                    mm(s2row, inv2048_b, sqbufs[par][:, 0:width], k == 0, k == KT - 1)
                    sq_guard[par] = sch.cnt["tensor"]
                sch.wait("vector", "tensor", sch.cnt["tensor"])
                m_f = rows_m[0:1, half * width: (half + 1) * width]
                m_b = rows_b[0:1, half * width: (half + 1) * width]
                r_f = rows_r[0:1, half * width: (half + 1) * width]
                sch.op("vector", lambda h, o=m_f, i=mrow: h.tensor_copy(o, i))
                sch.op("vector", lambda h, o=m_b, i=mrow: h.tensor_copy(o, i))
                sch.op("vector", lambda h, o=r_f, a=mrow, b=m_f:
                       h.tensor_tensor(o, a, b, OP.mult))
                sch.op("vector", lambda h, o=r_f, a=r_f, b=s2row:
                       h.scalar_tensor_tensor(o, a, -1.0, b, OP.mult, OP.add))
                sch.wait("scalar", "vector", sch.cnt["vector"])
                c_sd = sch.op("scalar", lambda h, o=r_f, i=r_f:
                              h.activation(o, i, AF.Sqrt, bias=EPS_LN))
                sch.wait("vector", "scalar", c_sd)
                sch.op("vector", lambda h, o=r_f, i=r_f: h.reciprocal(o, i))
            return sch.cnt["vector"]

        def ln_apply(get_in, get_out, rows_r, rows_b, tmp2, extra_res=None):
            for half in range(2):
                sch.wait("tensor", "vector", sch.cnt["vector"])
                mm(PS[2][:, 0:512], ones_row_b,
                   rows_b[0:1, half * 512: half * 512 + 512], True, True)
                tb = mm(PS[3][:, 0:512], ones_row_f,
                        rows_r[0:1, half * 512: half * 512 + 512], True, True)
                sch.wait("vector", "tensor", tb)
                for k in range(KT):
                    it = get_in(k, half)
                    ot = get_out(k, half)
                    tf = tmp2[k % 2][:]
                    sch.op("vector", lambda h, o=tf, a=it, b=PS[2][:, 0:512]:
                           h.tensor_tensor(o, a, b, OP.subtract))
                    if extra_res is None:
                        sch.op("vector", lambda h, o=ot, a=tf, b=PS[3][:, 0:512]:
                               h.tensor_tensor(o, a, b, OP.mult))
                    else:
                        rt = extra_res(k, half)
                        sch.op("vector", lambda h, o=tf, a=tf, b=PS[3][:, 0:512]:
                               h.tensor_tensor(o, a, b, OP.mult))
                        sch.op("vector", lambda h, o=ot, a=rt, b=tf:
                               h.tensor_tensor(o, a, b, OP.add))
            return sch.cnt["vector"]

        stack_q = contextlib.ExitStack()
        qnA = sb(stack_q, "qnA", [128, NHP * OWN], BF16, side="right")
        qpA = sb(stack_q, "qpA", [DR, NHP * OWN], BF16, side="right")

        with contextlib.ExitStack() as ph0:
            xq = sb(ph0, "xq", [128, KT * OWN], BF16)
            sqb = [sb(ph0, f"ph0sq{i}", [128, 512], BF16) for i in range(2)]
            rows0m = sb(ph0, "ph0rowm", [1, OWN], F32)
            rows0r = sb(ph0, "ph0rowr", [1, OWN], F32)
            rows0_b = sb(ph0, "ph0rows_b", [1, OWN], BF16)
            tmpf = [sb(ph0, f"ph0tmp{i}", [128, 512], F32) for i in range(2)]
            w_qa = sb(ph0, "w_qa", [128, KT * QL], BF16)
            w_qbn = sb(ph0, "w_qbn", [128, 4 * NH * DN], BF16)
            w_qbp = sb(ph0, "w_qbp", [128, 4 * NH * DR], BF16)
            w_qbps = sb(ph0, "w_qbps", [128, 4 * NH * DR], BF16)
            qa_sb = sb(ph0, "qa_sb", [128, 4 * OWN], BF16)
            qnB = sb(ph0, "qnB", [128, NHP * OWN], BF16)
            qpB = sb(ph0, "qpB", [DR, NHP * OWN], BF16)
            rq_c = sb(ph0, "rq_c", [DR, OWN], F32)
            rq_s = sb(ph0, "rq_s", [DR, OWN], F32)
            rop = [sb(ph0, f"ph0rop{i}", [DR, 512], F32) for i in range(2)]

            sch.dma("gpsimd", xq[:].rearrange("p (k t) -> p k t", k=KT), xqT.rearrange("(k p) t -> p k t", p=128), sem="ph0w")
            sch.dma("gpsimd", w_qa[:].rearrange("p (k t) -> p k t", k=KT), wqa_d.rearrange("(k p) t -> p k t", p=128), sem="ph0w")
            sch.dma("gpsimd", w_qbn[:].rearrange("p (k t) -> p k t", k=4), wqbn_d.rearrange("(k p) t -> p k t", p=128), sem="ph0w")
            sch.dma("gpsimd", w_qbp[:].rearrange("p (k t) -> p k t", k=4), wqbp_d.rearrange("(k p) t -> p k t", p=128), sem="ph0w")
            sch.dma("gpsimd", w_qbps[:].rearrange("p (k t) -> p k t", k=4), wqbps_d.rearrange("(k p) t -> p k t", p=128), sem="ph0w")
            sch.dma("sync", rq_c[:], ropeq_c, sem="ropeq")
            sch.dma("sync", rq_s[:], ropeq_s, sem="ropeq")
            sch.dwait("scalar", "ph0w")
            sch.dwait("tensor", "ph0w")
            sch.dwait("vector", "ropeq")

            stats_rows(lambda k, half: xq[:, k * OWN + half * 512:
                                          k * OWN + half * 512 + 512],
                       sqb, rows0m, rows0r, rows0_b)
            v_ln1 = ln_apply(
                lambda k, half: xq[:, k * OWN + half * 512: k * OWN + half * 512 + 512],
                lambda k, half: xq[:, k * OWN + half * 512: k * OWN + half * 512 + 512],
                rows0r, rows0_b, tmpf)

            prev_ev = v_ln1
            for m in range(4):
                for half in range(2):
                    ps = PS[(m * 2 + half) % 2]
                    sch.wait("tensor", "vector", prev_ev)
                    for k in range(KT):
                        lt = w_qa[:, k * QL + m * 128: k * QL + m * 128 + 128]
                        rt = xq[:, k * OWN + half * 512: k * OWN + half * 512 + 512]
                        mm(ps[:, 0:512], lt, rt, k == 0, k == KT - 1)
                    sch.wait("vector", "tensor", sch.cnt["tensor"])
                    dst = qa_sb[:, m * OWN + half * 512: m * OWN + half * 512 + 512]
                    prev_ev = sch.op("vector", lambda h, o=dst, i=ps[:, 0:512]:
                                     h.tensor_copy(o, i))

            prev_v = prev_ev
            for h_ in range(NH):
                qn_dst = qnA if h_ < NHP else qnB
                qp_dst = qpA if h_ < NHP else qpB
                hh = h_ % NHP
                for half in range(2):
                    sl = slice(hh * OWN + half * 512, hh * OWN + half * 512 + 512)
                    sch.wait("tensor", "vector", prev_v)
                    for k in range(4):
                        lt = w_qbn[:, k * NH * DN + h_ * DN: k * NH * DN + h_ * DN + 128]
                        rt = qa_sb[:, k * OWN + half * 512: k * OWN + half * 512 + 512]
                        mm(PS[0][:, 0:512], lt, rt, k == 0, k == 3)
                    for k in range(4):
                        lt = w_qbp[:, k * NH * DR + h_ * DR: k * NH * DR + h_ * DR + DR]
                        rt = qa_sb[:, k * OWN + half * 512: k * OWN + half * 512 + 512]
                        mm(PS[2][0:DR, 0:512], lt, rt, k == 0, k == 3)
                    for k in range(4):
                        lt = w_qbps[:, k * NH * DR + h_ * DR: k * NH * DR + h_ * DR + DR]
                        rt = qa_sb[:, k * OWN + half * 512: k * OWN + half * 512 + 512]
                        mm(PS[3][0:DR, 0:512], lt, rt, k == 0, k == 3)
                    sch.wait("vector", "tensor", sch.cnt["tensor"])
                    sch.op("vector", lambda h, o=qn_dst[:, sl], i=PS[0][:, 0:512]:
                           h.tensor_copy(o, i))
                    cs = rq_c[:, half * 512: half * 512 + 512]
                    sn = rq_s[:, half * 512: half * 512 + 512]
                    sch.op("vector", lambda h, o=rop[0][:], a=PS[3][0:DR, 0:512], b=sn:
                           h.tensor_tensor(o, a, b, OP.mult))
                    sch.op("vector", lambda h, o=rop[1][:], a=PS[2][0:DR, 0:512], b=cs:
                           h.tensor_tensor(o, a, b, OP.mult))
                    prev_v = sch.op("vector", lambda h, o=qp_dst[0:DR, sl], a=rop[0][:],
                                    b=rop[1][:]: h.tensor_tensor(o, a, b, OP.add))

            sch.wait("gpsimd", "vector", sch.cnt["vector"])
            sch.dma("gpsimd", sp_qnB, qnB[:], sem="qspill")
            sch.dma("gpsimd", sp_qpB, qpB[:], sem="qspill")
            sch.barrier()

        def attn_pass(pass_id, stack, qn, qp, odst):
            if pass_id == 0:
                xc = sb(stack, "xc", [128, KT * CH], BF16)
                sqb = [sb(stack, f"a0sq{i}", [128, 512], BF16) for i in range(2)]
                w_kva = sb(stack, "w_kva", [128, KT * 640], BF16)
                w_kbk = sb(stack, "w_kbk", [128, 4 * NH * DN], BF16)
                w_kbv = sb(stack, "w_kbv", [128, 4 * NH * DN], BF16)
                cT = sb(stack, "cT", [128, 4 * CH], BF16)
                nw1 = sb(stack, "nw1", [1, 640], BF16)
                rk_c = sb(stack, "rk_c", [DR, CH], F32)
                rk_s = sb(stack, "rk_s", [DR, CH], F32)
                crowm = sb(stack, "crowm", [1, CH], F32)
                crowr = sb(stack, "crowr", [1, CH], F32)
                crow_b = sb(stack, "crow_b", [1, CH], BF16)
                rbc = sb(stack, "rbc", [128, 512], F32)
                ropt = [sb(stack, f"a0rp{i}", [DR, 512], F32) for i in range(2)]
                kn_sb = [sb(stack, "kn_sb", [128, NH * CH], BF16)]
                v_sb = [sb(stack, "v_sb", [128, 4 * NH * 128], BF16)]
                kpe_sb = [sb(stack, "kpe_sb", [DR, CH], BF16)]
                nbuf = 1
            else:
                kn_sb = [sb(stack, f"kn_sb{i}", [128, NH * CH], BF16) for i in range(2)]
                v_sb = [sb(stack, f"v_sb{i}", [128, 4 * NH * 128], BF16) for i in range(2)]
                kpe_sb = [sb(stack, f"kpe_sb{i}", [DR, CH], BF16) for i in range(2)]
                nbuf = 2

            mk_sb = [sb(stack, f"mk{pass_id}_{i}", [128, 512], BF16) for i in range(2)]
            p_sb = [sb(stack, f"p{pass_id}_{i}", [128, 512], BF16) for i in range(2)]
            pt_sb = [sb(stack, f"pv{pass_id}_{i}", [128, 512], BF16) for i in range(2)]
            oacc = sb(stack, f"oacc{pass_id}", [128, NSLOT * NHP * 128], F32)
            sums_vis = sb(stack, f"sumsv{pass_id}", [128, NHP], F32)
            onorm = [sb(stack, f"onorm{pass_id}_{i}", [128, NHP * 128], BF16)
                     for i in range(2)]

            if pass_id == 0:
                sch.dma("gpsimd", w_kva[:].rearrange("p (k t) -> p k t", k=KT), wkva_d.rearrange("(k p) t -> p k t", p=128), sem="kvaw")
                sch.dma("gpsimd", w_kbk[:].rearrange("p (k t) -> p k t", k=4), wkvbk_d.rearrange("(k p) t -> p k t", p=128), sem="kvaw")
                sch.dma("gpsimd", w_kbv[:].rearrange("p (k t) -> p k t", k=4), wkvbv_d.rearrange("(k p) t -> p k t", p=128), sem="kvaw")
                sch.dwait("tensor", "kvaw")
                ones_col_b = c_colsb[:, 1:2]
                for (n0, n1, ps) in ((0, 512, PS[0]), (512, 640, PS[1])):
                    for k in range(KT):
                        mm(ps[0:1, 0:n1 - n0], ones_col_b,
                           w_kva[:, k * 640 + n0: k * 640 + n1], k == 0, k == KT - 1)
                sch.wait("vector", "tensor", sch.cnt["tensor"])
                sch.op("vector", lambda h: h.tensor_scalar(
                    nw1[0:1, 0:512], PS[0][0:1, 0:512], -1.0, None, OP.mult, OP.bypass))
                sch.op("vector", lambda h: h.tensor_scalar(
                    nw1[0:1, 512:640], PS[1][0:1, 0:128], -1.0, None, OP.mult, OP.bypass))
                sch.wait("tensor", "vector", sch.cnt["vector"])

            mask_ld = {}
            kvload = [0] * (NCH + 1)
            if pass_id == 0:
                sch.dma("gpsimd", xc[:].rearrange("p (k t) -> p k t", k=KT), xT[:, 0:CH].rearrange("(k p) t -> p k t", p=128), sem="xc")
                sch.dma("sync", rk_c[:], ropek_c[:, 0:CH], sem="ropek")
                sch.dma("sync", rk_s[:], ropek_s[:, 0:CH], sem="ropek")
            else:
                sch.dwait("gpsimd", "spill")
                sch.dma("gpsimd", kn_sb[0][:], sp_kn[0], sem="kv0")
                sch.dma("gpsimd", v_sb[0][:], sp_v[0], sem="kv0")
                sch.dma("gpsimd", kpe_sb[0][:], sp_kpe[0], sem="kv0")
                kvload[0] = ("kv0", sch.dcnt["kv0"])
            mname = f"mask{pass_id}"
            sch.dma("sync", mk_sb[0][:], masks_d[0], sem=mname + "0")
            mask_ld[(0, 0)] = (mname + "0", sch.dcnt[mname + "0"])
            sch.dma("sync", mk_sb[1][:], masks_d[1], sem=mname + "1")
            mask_ld[(0, 1)] = (mname + "1", sch.dcnt[mname + "1"])

            s_banks = (PS[0], PS[1])
            o_pairs = ((PS[2], PS[3]), (PS[4], PS[5]))
            exp_done = [0, 0]
            ptevac_done = [0, 0]
            pmm_done = [0, 0]
            oadd_done = [0, 0]
            sums_read = 0
            opair_idx = 0
            kvspill_done = 0
            xload_g = sch.dcnt.get("xc", 0)
            rload_s = sch.dcnt.get("ropek", 0)

            for v in range(NCH):
                buf = v % nbuf
                if pass_id == 0:
                    sch.dwait("scalar", "xc", xload_g)
                    sch.dwait("tensor", "xc", xload_g)
                    sch.wait("tensor", "scalar", max(exp_done))
                    sch.dwait("vector", "spill", kvspill_done)
                    sq_guard = [0, 0]
                    for k in range(KT):
                        xt = xc[:, k * CH: k * CH + 512]
                        par = k % 2
                        sch.wait("scalar", "tensor", sq_guard[par])
                        c_sq = sch.op("scalar", lambda h, o=sqb[par][:], i=xt:
                                      h.activation(o, i, AF.Square))
                        mm(PS[0][0:1, 0:512], inv2048_b, xt, k == 0, k == KT - 1)
                        sch.wait("tensor", "scalar", c_sq)
                        mm(PS[1][0:1, 0:512], inv2048_b, sqb[par][:], k == 0, k == KT - 1)
                        sq_guard[par] = sch.cnt["tensor"]
                    sch.wait("vector", "tensor", sch.cnt["tensor"])
                    sch.op("vector", lambda h: h.tensor_copy(crowm[0:1, :], PS[0][0:1, 0:512]))
                    sch.op("vector", lambda h: h.tensor_copy(crow_b[0:1, :], PS[0][0:1, 0:512]))
                    sch.op("vector", lambda h: h.tensor_tensor(
                        crowr[0:1, :], PS[0][0:1, 0:512], crowm[0:1, :], OP.mult))
                    sch.op("vector", lambda h: h.scalar_tensor_tensor(
                        crowr[0:1, :], crowr[0:1, :], -1.0, PS[1][0:1, 0:512], OP.mult, OP.add))
                    sch.wait("scalar", "vector", sch.cnt["vector"])
                    c_sd = sch.op("scalar", lambda h: h.activation(
                        crowr[0:1, :], crowr[0:1, :], AF.Sqrt, bias=EPS_LN))
                    sch.wait("vector", "scalar", c_sd)
                    sch.op("vector", lambda h: h.reciprocal(crowr[0:1, :], crowr[0:1, :]))
                    v_r = sch.cnt["vector"]
                    sch.wait("tensor", "vector", v_r)
                    trb = mm(PS[2][:, 0:512], ones_row_f, crowr[0:1, :], True, True)
                    sch.wait("vector", "tensor", trb)
                    v_rbc = sch.op("vector", lambda h: h.tensor_copy(rbc[:], PS[2][:, 0:512]))

                    mt = [(0, 128), (128, 128), (256, 128), (384, 128),
                          (512, 64), (576, 64)]
                    banks = (PS[3], PS[4], PS[5], PS[3], PS[4], PS[5])
                    evac_cnt = [0, 0, 0]
                    pe_ps = {}
                    for mi, (c0, M) in enumerate(mt):
                        ps = banks[mi]
                        if mi >= 3:
                            sch.wait("tensor", "vector", evac_cnt[mi % 3])
                        if mi == 0:
                            sch.wait("tensor", "vector", v_rbc)
                        for k in range(KT):
                            lt = w_kva[:, k * 640 + c0: k * 640 + c0 + M]
                            rt = xc[:, k * CH: k * CH + 512]
                            mm(ps[0:M, 0:512], lt, rt, k == 0, False)
                        tmm = mm(ps[0:M, 0:512], nw1[0:1, c0:c0 + M],
                                 crow_b[0:1, :], False, True)
                        sch.wait("vector", "tensor", tmm)
                        if mi < 4:
                            dst = cT[:, mi * CH: mi * CH + 512]
                            evac_cnt[mi % 3] = sch.op(
                                "vector", lambda h, o=dst, a=ps[:, 0:512], b=rbc[:]:
                                h.tensor_tensor(o, a, b, OP.mult))
                        else:
                            pe_ps[mi] = ps
                    sch.dwait("vector", "ropek", rload_s)
                    sch.op("vector", lambda h: h.tensor_tensor(
                        ropt[0][:], pe_ps[5][0:DR, 0:512], rk_s[:], OP.mult))
                    sch.op("vector", lambda h: h.tensor_tensor(
                        ropt[1][:], pe_ps[4][0:DR, 0:512], rk_c[:], OP.mult))
                    sch.op("vector", lambda h: h.tensor_tensor(
                        ropt[0][:], ropt[0][:], ropt[1][:], OP.add))
                    sch.op("vector", lambda h, o=kpe_sb[buf][:]: h.tensor_tensor(
                        o, ropt[0][:], rbc[0:DR, 0:512], OP.mult))
                    v_ckv = sch.cnt["vector"]

                    if v + 1 < NCH:
                        sch.wait("gpsimd", "tensor", sch.cnt["tensor"])
                        sch.wait("gpsimd", "scalar", sch.cnt["scalar"])
                        sch.dma("gpsimd", xc[:].rearrange("p (k t) -> p k t", k=KT),
                                xT[:, (v + 1) * CH: (v + 2) * CH].rearrange(
                                    "(k p) t -> p k t", p=128), sem="xc")
                        xload_g = sch.dcnt["xc"]
                        sch.wait("sync", "vector", v_ckv)
                        sch.dma("sync", rk_c[:], ropek_c[:, (v + 1) * CH: (v + 2) * CH], sem="ropek")
                        sch.dma("sync", rk_s[:], ropek_s[:, (v + 1) * CH: (v + 2) * CH], sem="ropek")
                        rload_s = sch.dcnt["ropek"]

                    sch.wait("tensor", "vector", v_ckv)
                    ev = [0, 0]
                    for h_ in range(NH):
                        ps = s_banks[h_ % 2]
                        sch.wait("tensor", "vector", ev[h_ % 2])
                        for k in range(4):
                            lt = w_kbk[:, k * NH * DN + h_ * DN: k * NH * DN + h_ * DN + 128]
                            rt = cT[:, k * CH: k * CH + 512]
                            mm(ps[:, 0:512], lt, rt, k == 0, k == 3)
                        sch.wait("vector", "tensor", sch.cnt["tensor"])
                        dst = kn_sb[buf][:, h_ * CH: h_ * CH + 512]
                        ev[h_ % 2] = sch.op("vector", lambda h, o=dst, i=ps[:, 0:512]:
                                            h.tensor_copy(o, i))
                    evv = [0, 0, 0, 0]
                    for j in range(4):
                        for n in range(4):
                            bi = (j * 4 + n) % 4
                            ps = (PS[2], PS[3], PS[4], PS[5])[bi]
                            sch.wait("tensor", "vector", evv[bi])
                            for k in range(4):
                                lt = cT[:, k * CH + j * 128: k * CH + j * 128 + 128]
                                rt = w_kbv[:, k * NH * DN + n * 512:
                                           k * NH * DN + n * 512 + 512]
                                mm(ps[:, 0:512], lt, rt, k == 0, k == 3)
                            sch.wait("vector", "tensor", sch.cnt["tensor"])
                            dst = v_sb[buf][:, j * NH * 128 + n * 512:
                                            j * NH * 128 + n * 512 + 512]
                            evv[bi] = sch.op("vector", lambda h, o=dst, i=ps[:, 0:512]:
                                             h.tensor_copy(o, i))
                    v_kv = sch.cnt["vector"]
                    sch.wait("gpsimd", "vector", v_kv)
                    sch.dma("gpsimd", sp_kn[v], kn_sb[buf][:], sem="spill")
                    sch.dma("gpsimd", sp_v[v], v_sb[buf][:], sem="spill")
                    sch.dma("gpsimd", sp_kpe[v], kpe_sb[buf][:], sem="spill")
                    kvspill_done = sch.dcnt["spill"]
                    sch.wait("tensor", "vector", v_kv)
                    exp_done = [0, 0]
                else:
                    if v + 1 < NCH:
                        sch.wait("gpsimd", "tensor", sch.cnt["tensor"])
                        kb = f"kv{1 - buf}"
                        sch.dma("gpsimd", kn_sb[1 - buf][:], sp_kn[v + 1], sem=kb)
                        sch.dma("gpsimd", v_sb[1 - buf][:], sp_v[v + 1], sem=kb)
                        sch.dma("gpsimd", kpe_sb[1 - buf][:], sp_kpe[v + 1], sem=kb)
                        kvload[v + 1] = (kb, sch.dcnt[kb])
                    sch.dwait("tensor", kvload[v][0], kvload[v][1])

                for t in range(v // 2, NSLOT):
                    maskable = (t == v // 2)
                    mk_idx = v - 2 * t
                    first_visit = (v == 0)
                    pid = opair_idx % 2
                    op_pair = o_pairs[pid]
                    sch.wait("tensor", "vector", oadd_done[pid])
                    if maskable:
                        sch.dwait("tensor", mask_ld[(t, mk_idx)][0], mask_ld[(t, mk_idx)][1])
                    sch.wait("scalar", "vector", sums_read)

                    def emit_s(h_):
                        sbk = h_ % 2
                        ps = s_banks[sbk]
                        sch.wait("tensor", "scalar", exp_done[sbk])
                        hg = h_ + pass_id * NHP
                        lt1 = qn[:, h_ * OWN + t * 128: h_ * OWN + t * 128 + 128]
                        rt1 = kn_sb[buf][:, hg * CH: hg * CH + 512]
                        mm(ps[:, 0:512], lt1, rt1, True, False)
                        lt2 = qp[0:DR, h_ * OWN + t * 128: h_ * OWN + t * 128 + 128]
                        rt2 = kpe_sb[buf][0:DR, 0:512]
                        mm(ps[:, 0:512], lt2, rt2, False, not maskable)
                        if maskable:
                            mm(ps[:, 0:512], c_idb[:], mk_sb[mk_idx][:], False, True)
                        s_done[sbk] = sch.cnt["tensor"]

                    s_done = [0, 0]
                    emit_s(0)
                    for h_ in range(NHP):
                        sbk = h_ % 2
                        ps = s_banks[sbk]
                        hg = h_ + pass_id * NHP
                        sch.wait("scalar", "tensor", s_done[sbk])
                        c_e = sch.op("scalar", lambda h, o=p_sb[sbk][:], i=ps[:, 0:512],
                                     a=sums_vis[:, h_: h_ + 1]:
                                     h.activation(o, i, AF.Exp, scale=SCALE, accum_out=a))
                        exp_done[sbk] = c_e
                        if h_ + 1 < NHP:
                            emit_s(h_ + 1)
                        sch.wait("tensor", "scalar", c_e)
                        sch.wait("tensor", "vector", ptevac_done[sbk])
                        for j in range(4):
                            sch.op("tensor", lambda h, o=PTb[sbk][:, j * 128: j * 128 + 128],
                                   i=p_sb[sbk][:, j * 128: j * 128 + 128], idm=c_idb[:]:
                                   h.transpose(o, i, idm))
                        pmm_done[sbk] = sch.cnt["tensor"]
                        sch.wait("vector", "tensor", pmm_done[sbk])
                        c_pt = sch.op("vector", lambda h, o=pt_sb[sbk][:],
                                      i=PTb[sbk][:, 0:512]: h.tensor_copy(o, i))
                        ptevac_done[sbk] = c_pt
                        sch.wait("tensor", "vector", c_pt)
                        opsl = op_pair[h_ // 4]
                        col = (h_ % 4) * 128
                        for j in range(4):
                            lt = v_sb[buf][:, j * NH * 128 + hg * 128:
                                           j * NH * 128 + hg * 128 + 128]
                            rt = pt_sb[sbk][:, j * 128: j * 128 + 128]
                            mm(opsl[:, col: col + 128], lt, rt,
                               (h_ % 4 == 0) and (j == 0), (h_ % 4 == 3) and (j == 3))
                    t_pv = sch.cnt["tensor"]
                    sch.wait("vector", "tensor", t_pv)
                    sch.wait("vector", "scalar", max(exp_done))
                    for pi in range(2):
                        dst = oacc[:, t * NHP * 128 + pi * 512:
                                   t * NHP * 128 + pi * 512 + 512]
                        src = op_pair[pi][:, 0:512]
                        if first_visit:
                            sch.op("vector", lambda h, o=dst, i=src: h.tensor_copy(o, i))
                        else:
                            sch.op("vector", lambda h, o=dst, a=dst, b=src:
                                   h.tensor_tensor(o, a, b, OP.add))
                    ssl = sums_acc[:, t * NHP: (t + 1) * NHP]
                    if first_visit:
                        sch.op("vector", lambda h, o=ssl, i=sums_vis[:]: h.tensor_copy(o, i))
                    else:
                        sch.op("vector", lambda h, o=ssl, a=ssl, b=sums_vis[:]:
                               h.tensor_tensor(o, a, b, OP.add))
                    sums_read = sch.cnt["vector"]
                    oadd_done[pid] = sch.cnt["vector"]
                    opair_idx += 1

                if v + 1 < NCH:
                    nt = (v + 1) // 2
                    nidx = (v + 1) - 2 * nt
                    sch.wait("sync", "tensor", sch.cnt["tensor"])
                    mn = f"mask{pass_id}{nidx}"
                    sch.dma("sync", mk_sb[nidx][:], masks_d[2 * nt + nidx], sem=mn)
                    mask_ld[(nt, nidx)] = (mn, sch.dcnt[mn])

            sch.wait("tensor", "vector", sch.cnt["vector"])
            sch.wait("tensor", "scalar", sch.cnt["scalar"])
            ospill = [0, 0]
            for t in range(NSLOT):
                ob = t % 2
                for h_ in range(NHP):
                    ptv = PTf[h_ // 4]
                    sch.op("tensor", lambda h, i=sums_acc[:, t * NHP + h_:
                                                          t * NHP + h_ + 1],
                           o=ptv[0:1, (h_ % 4) * 128: (h_ % 4) * 128 + 128]:
                           h.transpose(o, i, c_idf[:]))
                sch.wait("vector", "tensor", sch.cnt["tensor"])
                sch.op("vector", lambda h: h.reciprocal(
                    rsum_sb[0:1, 0:512], PTf[0][0:1, 0:512]))
                vv = sch.op("vector", lambda h: h.reciprocal(
                    rsum_sb[0:1, 512:1024], PTf[1][0:1, 0:512]))
                sch.dwait("vector", f"onorm{pass_id}{ob}", ospill[ob])
                for hb in range(2):
                    sch.wait("tensor", "vector", vv)
                    for hh in range(4):
                        h_ = hb * 4 + hh
                        mm(PS[0][:, hh * 128: hh * 128 + 128], ones_row_f,
                           rsum_sb[0:1, h_ * 128: h_ * 128 + 128], True, True)
                    sch.wait("vector", "tensor", sch.cnt["tensor"])
                    off = t * NHP * 128 + hb * 512
                    vv = sch.op("vector", lambda h, o=onorm[ob][:, hb * 512: hb * 512 + 512],
                                a=oacc[:, off: off + 512], b=PS[0][:, 0:512]:
                                h.tensor_tensor(o, a, b, OP.mult))
                sch.wait("gpsimd", "vector", vv)
                sch.dma("gpsimd", odst[:, t * NHP * 128: (t + 1) * NHP * 128],
                        onorm[ob][:], sem=f"onorm{pass_id}{ob}")
                ospill[ob] = sch.dcnt[f"onorm{pass_id}{ob}"]
            sch.barrier()

        with contextlib.ExitStack() as pa:
            attn_pass(0, pa, qnA[:], qpA[:], sp_oA)
        with contextlib.ExitStack() as pb:
            sch.dwait("gpsimd", "qspill")
            sch.dma("gpsimd", qnA[:], sp_qnB, sem="qload")
            sch.dma("gpsimd", qpA[:], sp_qpB, sem="qload")
            sch.dwait("tensor", "qload")
            attn_pass(1, pb, qnA[:], qpA[:], sp_oB)
        stack_q.close()

        stack_ln2 = contextlib.ExitStack()
        ln2 = sb(stack_ln2, "ln2", [128, KT * OWN], BF16, side="right")

        with contextlib.ExitStack() as p3:
            xo = sb(p3, "xo", [128, KT * OWN], F32)
            ab = sb(p3, "ab", [128, KT * OWN], BF16)
            wo_sb = [sb(p3, f"wo_sb{i}", [128, KT * 128], BF16) for i in range(2)]
            sqb3 = [sb(p3, f"p3sq{i}", [128, 512], BF16) for i in range(2)]
            rows3m = sb(p3, "p3rowm", [1, OWN], F32)
            rows3r = sb(p3, "p3rowr", [1, OWN], F32)
            rows3_b = sb(p3, "p3rows_b", [1, OWN], BF16)
            tmp3 = [sb(p3, f"p3tmp{i}", [128, 512], F32) for i in range(2)]

            oA = sb(p3, "oA3", [128, NSLOT * NHP * 128], BF16)
            oB = sb(p3, "oB3", [128, NSLOT * NHP * 128], BF16)
            sch.dma("sync", oA[:], sp_oA, sem="p3ld")
            sch.dma("sync", oB[:], sp_oB, sem="p3ld")
            sch.dma("sync", xo[:].rearrange("p (k t) -> p k t", k=KT), xqT.rearrange("(k p) t -> p k t", p=128), sem="p3ld")
            sch.dwait("tensor", "p3ld")
            sch.dma("gpsimd", wo_sb[0][:].rearrange("p (k t) -> p k t", k=KT),
                    wo_d[:, 0:128].rearrange("(k p) m2 -> p k m2", p=128), sem="wo0")
            wo_ld = [16, 0]

            oab = (oA, oB)
            prev_ev = 0
            for m in range(KT):
                if m + 1 < KT:
                    sch.wait("gpsimd", "tensor", sch.cnt["tensor"])
                    wn = f"wo{(m + 1) % 2}"
                    sch.dma("gpsimd", wo_sb[(m + 1) % 2][:].rearrange("p (k t) -> p k t", k=KT),
                            wo_d[:, (m + 1) * 128: (m + 2) * 128].rearrange(
                                "(k p) m2 -> p k m2", p=128), sem=wn)
                    wo_ld[(m + 1) % 2] = sch.dcnt[wn]
                ppair = (PS[2], PS[3]) if m % 2 == 0 else (PS[4], PS[5])
                sch.dwait("tensor", f"wo{m % 2}", wo_ld[m % 2])
                sch.wait("tensor", "vector", prev_ev)
                for half in range(2):
                    ps = ppair[half]
                    for k in range(NH):
                        lt = wo_sb[m % 2][:, k * 128: k * 128 + 128]
                        src = oab[k // NHP]
                        hh = k % NHP
                        rt = src[:].rearrange("p (t x) -> p t x", t=NSLOT)[
                            :, half * 4: half * 4 + 4, hh * 128: hh * 128 + 128]
                        mm(ps[:, 0:512], lt, rt, k == 0, k == NH - 1)
                sch.wait("vector", "tensor", sch.cnt["tensor"])
                for half in range(2):
                    dst = ab[:, m * OWN + half * 512: m * OWN + half * 512 + 512]
                    prev_ev = sch.op("vector", lambda h, o=dst, i=ppair[half][:, 0:512]:
                                     h.tensor_copy(o, i))

            stats_rows(lambda k, half: ab[:, k * OWN + half * 512:
                                          k * OWN + half * 512 + 512],
                       sqb3, rows3m, rows3r, rows3_b)
            sch.dwait("vector", "p3ld")
            v_x1 = ln_apply(
                lambda k, half: ab[:, k * OWN + half * 512: k * OWN + half * 512 + 512],
                lambda k, half: xo[:, k * OWN + half * 512: k * OWN + half * 512 + 512],
                rows3r, rows3_b, tmp3,
                extra_res=lambda k, half: xo[:, k * OWN + half * 512:
                                             k * OWN + half * 512 + 512])
            for k in range(KT):
                for half in range(2):
                    xsl = xo[:, k * OWN + half * 512: k * OWN + half * 512 + 512]
                    bsl = ln2[:, k * OWN + half * 512: k * OWN + half * 512 + 512]
                    sch.op("vector", lambda h, o=bsl, i=xsl: h.tensor_copy(o, i))
            v_x1b = sch.cnt["vector"]
            sch.wait("gpsimd", "vector", v_x1)
            sch.dma("gpsimd", sp_x1, xo[:], sem="x1sp")

            sch.wait("tensor", "vector", v_x1b)
            stats_rows(lambda k, half: ln2[:, k * OWN + half * 512:
                                           k * OWN + half * 512 + 512],
                       sqb3, rows3m, rows3r, rows3_b)
            ln_apply(
                lambda k, half: ln2[:, k * OWN + half * 512: k * OWN + half * 512 + 512],
                lambda k, half: ln2[:, k * OWN + half * 512: k * OWN + half * 512 + 512],
                rows3r, rows3_b, tmp3)
            sch.barrier()

        stack_sg = contextlib.ExitStack()
        sgh = sb(stack_sg, "sgh", [128, 44 * OWN], BF16)

        with contextlib.ExitStack() as p4:
            wraw = [sb(p4, f"wraw{i}", [128, 512], I16) for i in range(2)]
            wstage = [sb(p4, f"wstage{i}", [128, 512], BF16) for i in range(2)]
            wsc = [sb(p4, f"wsc{i}", [128, 512], F16) for i in range(2)]
            dq = [sb(p4, f"dq{i}", [128, 512], I16) for i in range(3)]
            wsl = [sb(p4, f"wsl{i}", [128, 16 * 512], BF16) for i in range(2)]

            NSL = 22
            wsl_free_t = [0, 0]
            wsl_ready = [0, 0]
            raw_free = [0, 0]
            raw_ld = [0, 0]
            evac_v = [0, 0, 0, 0]
            evac_s = [0, 0, 0, 0]
            for s in range(NSL):
                cur = s % 2
                sch.wait("vector", "tensor", wsl_free_t[cur])
                for g in range(16):
                    par = g % 2
                    sch.wait("gpsimd", "vector", raw_free[par])
                    rn = f"raw{par}"
                    sch.dma("gpsimd", wraw[par][:],
                            gu_rep[g * 128: (g + 1) * 128, s * 512: (s + 1) * 512], sem=rn)
                    sch.dma("gpsimd", wsc[par][:],
                            gu_sc_d[g: g + 1, s * 512: (s + 1) * 512].broadcast_to([128, 512]), sem=rn)
                    raw_ld[par] = sch.dcnt[rn]
                    sch.dwait("vector", rn, raw_ld[par])
                    sch.op("vector", lambda h, w=wraw[par][:]: h.tensor_tensor(
                        dq[0][:], w, c_shfull[:], OP.logical_shift_right))
                    raw_free[par] = sch.cnt["vector"]
                    sch.op("vector", lambda h: h.tensor_scalar(
                        dq[1][:], dq[0][:], 3, 1, OP.logical_shift_right, OP.bitwise_and))
                    sch.op("vector", lambda h: h.tensor_scalar(
                        dq[0][:], dq[0][:], 7, None, OP.bitwise_and, OP.bypass))
                    sch.op("vector", lambda h: h.tensor_scalar(
                        dq[2][:], dq[0][:], 252.0, None, OP.add, OP.bypass))
                    sch.op("vector", lambda h: h.scalar_tensor_tensor(
                        dq[0][:], dq[0][:], 252.0, dq[2][:], OP.mult, OP.min))
                    sch.op("vector", lambda h: h.tensor_scalar(
                        dq[0][:], dq[0][:], 6, None, OP.logical_shift_left, OP.bypass))
                    sch.op("vector", lambda h: h.tensor_scalar(
                        dq[1][:], dq[1][:], 15, None, OP.logical_shift_left, OP.bypass))
                    sch.op("vector", lambda h: h.tensor_tensor(
                        dq[2][:], dq[1][:], dq[0][:], OP.bitwise_or))
                    dst = wsl[cur][:, g * 512: (g + 1) * 512]
                    raw_free[par] = sch.op("vector", lambda h, o=dst, w=wsc[par][:]:
                                           h.tensor_tensor(o, dq[2][:].bitcast(BF16),
                                                           w, OP.mult))
                wsl_ready[cur] = sch.cnt["vector"]

                for dt_ in range(s * 8, min(s * 8 + 8, 176)):
                    g5, os5 = dt_ % 44, dt_ // 44
                    par = dt_ % 2
                    sch.wait("gpsimd", "vector", raw_free[par])
                    rn = f"raw{par}"
                    sch.dma("gpsimd", wraw[par][:],
                            dn_rep[g5 * 128: (g5 + 1) * 128, os5 * 512: (os5 + 1) * 512],
                            sem=rn)
                    sch.dma("gpsimd", wsc[par][:],
                            dn_sc_d[g5: g5 + 1, os5 * 512: (os5 + 1) * 512].broadcast_to([128, 512]),
                            sem=rn)
                    raw_ld[par] = sch.dcnt[rn]
                    sch.dwait("vector", rn, raw_ld[par])
                    sch.dwait("vector", f"wdsp{par}")
                    sch.op("vector", lambda h, w=wraw[par][:]: h.tensor_tensor(
                        dq[0][:], w, c_shfull[:], OP.logical_shift_right))
                    raw_free[par] = sch.cnt["vector"]
                    sch.op("vector", lambda h: h.tensor_scalar(
                        dq[1][:], dq[0][:], 3, 1, OP.logical_shift_right, OP.bitwise_and))
                    sch.op("vector", lambda h: h.tensor_scalar(
                        dq[0][:], dq[0][:], 7, None, OP.bitwise_and, OP.bypass))
                    sch.op("vector", lambda h: h.tensor_scalar(
                        dq[2][:], dq[0][:], 252.0, None, OP.add, OP.bypass))
                    sch.op("vector", lambda h: h.scalar_tensor_tensor(
                        dq[0][:], dq[0][:], 252.0, dq[2][:], OP.mult, OP.min))
                    sch.op("vector", lambda h: h.tensor_scalar(
                        dq[0][:], dq[0][:], 6, None, OP.logical_shift_left, OP.bypass))
                    sch.op("vector", lambda h: h.tensor_scalar(
                        dq[1][:], dq[1][:], 15, None, OP.logical_shift_left, OP.bypass))
                    sch.op("vector", lambda h: h.tensor_tensor(
                        dq[2][:], dq[1][:], dq[0][:], OP.bitwise_or))
                    vws = sch.op("vector", lambda h, o=wstage[par][:], w=wsc[par][:]:
                                 h.tensor_tensor(o, dq[2][:].bitcast(BF16), w, OP.mult))
                    sch.wait("gpsimd", "vector", vws)
                    sch.dma("gpsimd", sp_wd[g5, :, os5 * 512: (os5 + 1) * 512],
                            wstage[par][:], sem=f"wdsp{par}")

                sch.wait("tensor", "vector", wsl_ready[cur])
                for grp in range(2):
                    for g in range(16):
                        for m2 in range(2):
                            m4 = grp * 2 + m2
                            for half in range(2):
                                bi = m2 * 2 + half
                                ps = (PS[2], PS[3], PS[4], PS[5])[bi]
                                if g == 0:
                                    sch.wait("tensor", "vector", evac_v[bi])
                                    sch.wait("tensor", "scalar", evac_s[bi])
                                lt = wsl[cur][:, g * 512 + m4 * 128:
                                              g * 512 + m4 * 128 + 128]
                                rt = ln2[:, g * OWN + half * 512:
                                         g * OWN + half * 512 + 512]
                                mm(ps[:, 0:512], lt, rt, g == 0, g == 15)
                    t4 = sch.cnt["tensor"]
                    for m2 in range(2):
                        m4 = grp * 2 + m2
                        for half in range(2):
                            bi = m2 * 2 + half
                            ps = (PS[2], PS[3], PS[4], PS[5])[bi]
                            if s < 11:
                                dst = sgh[:, (s * 4 + m4) * OWN + half * 512:
                                          (s * 4 + m4) * OWN + half * 512 + 512]
                                sch.wait("scalar", "tensor", t4)
                                evac_s[bi] = sch.op(
                                    "scalar", lambda h, o=dst, i=ps[:, 0:512]:
                                    h.activation(o, i, AF.Silu))
                            else:
                                su = s - 11
                                dst = sgh[:, (su * 4 + m4) * OWN + half * 512:
                                          (su * 4 + m4) * OWN + half * 512 + 512]
                                sch.wait("vector", "tensor", t4)
                                evac_v[bi] = sch.op(
                                    "vector", lambda h, o=dst, a=dst, b=ps[:, 0:512]:
                                    h.tensor_tensor(o, a, b, OP.mult))
                wsl_free_t[cur] = sch.cnt["tensor"]
            sch.barrier()
        stack_ln2.close()

        stack_y = contextlib.ExitStack()
        y_sb = sb(stack_y, "y_sb", [128, KT * OWN], F32, side="right")

        with contextlib.ExitStack() as p5:
            wdb = [sb(p5, f"wdb{i}", [128, 8 * 256], BF16) for i in range(2)]

            ev5 = [0, 0, 0, 0]
            batches = []
            for os2 in range(8):
                for b0 in range(0, 44, 8):
                    batches.append((os2, list(range(b0, min(b0 + 8, 44)))))
            wb_ready = [0, 0]
            wb_free = [0, 0]

            def load_batch(bi_, os2, gl):
                cur = bi_ % 2
                wn = f"wdl{cur}"
                sch.dwait("gpsimd", f"wdsp0")
                sch.dwait("gpsimd", f"wdsp1")
                sch.wait("gpsimd", "tensor", wb_free[cur])
                for j, g in enumerate(gl):
                    sch.dma("gpsimd", wdb[cur][:, j * 256: j * 256 + 256],
                            sp_wd[g, :, os2 * 256: (os2 + 1) * 256], sem=wn)
                wb_ready[cur] = sch.dcnt[wn]

            load_batch(0, *batches[0])
            for bi_, (os2, gl) in enumerate(batches):
                cur = bi_ % 2
                if bi_ + 1 < len(batches):
                    load_batch(bi_ + 1, *batches[bi_ + 1])
                sch.dwait("tensor", f"wdl{cur}", wb_ready[cur])
                for j, g in enumerate(gl):
                    for m2 in range(2):
                        for half in range(2):
                            b_ = m2 * 2 + half
                            ps = (PS[2], PS[3], PS[4], PS[5])[b_]
                            if g == 0:
                                sch.wait("tensor", "vector", ev5[b_])
                            lt = wdb[cur][:, j * 256 + m2 * 128: j * 256 + m2 * 128 + 128]
                            rt = sgh[:, g * OWN + half * 512: g * OWN + half * 512 + 512]
                            mm(ps[:, 0:512], lt, rt, g == 0, g == 43)
                wb_free[cur] = sch.cnt["tensor"]
                if gl[-1] == 43:
                    t5 = sch.cnt["tensor"]
                    sch.wait("vector", "tensor", t5)
                    for m2 in range(2):
                        for half in range(2):
                            b_ = m2 * 2 + half
                            ps = (PS[2], PS[3], PS[4], PS[5])[b_]
                            o_tile = os2 * 2 + m2
                            dst = y_sb[:, o_tile * OWN + half * 512:
                                       o_tile * OWN + half * 512 + 512]
                            ev5[b_] = sch.op("vector", lambda h, o=dst, i=ps[:, 0:512]:
                                             h.tensor_copy(o, i))
            sch.barrier()
        stack_sg.close()

        with contextlib.ExitStack() as p6:
            xo2 = sb(p6, "xo2", [128, KT * OWN], F32)
            sqb6 = [sb(p6, f"p6sq{i}", [128, 512], BF16) for i in range(2)]
            rows6m = sb(p6, "p6rowm", [1, OWN], F32)
            rows6r = sb(p6, "p6rowr", [1, OWN], F32)
            rows6_b = sb(p6, "p6rows_b", [1, OWN], BF16)
            tmp6 = [sb(p6, f"p6tmp{i}", [128, 512], F32) for i in range(2)]
            outsb = [sb(p6, f"outsb{i}", [128, D], F32) for i in range(2)]

            sch.dwait("gpsimd", "x1sp")
            sch.dma("gpsimd", xo2[:], sp_x1, sem="x1ld")

            stats_rows(lambda k, half: y_sb[:, k * OWN + half * 512:
                                            k * OWN + half * 512 + 512],
                       sqb6, rows6m, rows6r, rows6_b, src_f32=True)
            sch.dwait("vector", "x1ld")
            v_x2 = ln_apply(
                lambda k, half: y_sb[:, k * OWN + half * 512: k * OWN + half * 512 + 512],
                lambda k, half: xo2[:, k * OWN + half * 512: k * OWN + half * 512 + 512],
                rows6r, rows6_b, tmp6,
                extra_res=lambda k, half: xo2[:, k * OWN + half * 512:
                                              k * OWN + half * 512 + 512])

            tr_roll = [0, 0]
            out_roll = [0, 0]
            for ti in range(NSLOT):
                ob = ti % 2
                sch.dwait("vector", f"out{ob}", out_roll[ob])
                sch.wait("tensor", "vector", v_x2)
                for k in range(KT):
                    par = k % 2
                    ps = (PS[0], PS[1])[par]
                    sch.wait("tensor", "vector", tr_roll[par])
                    sch.op("tensor", lambda h, o=ps[:, 0:128],
                           i=xo2[:, k * OWN + ti * 128: k * OWN + ti * 128 + 128],
                           idm=c_idf[:]: h.transpose(o, i, idm))
                    sch.wait("vector", "tensor", sch.cnt["tensor"])
                    tr_roll[par] = sch.op(
                        "vector", lambda h, o=outsb[ob][:, k * 128: k * 128 + 128],
                        i=ps[:, 0:128]: h.tensor_copy(o, i))
                sch.wait("gpsimd", "vector", sch.cnt["vector"])
                sch.dma("gpsimd", out_tok[ti * 128: (ti + 1) * 128, :], outsb[ob][:],
                        sem=f"out{ob}")
                out_roll[ob] = sch.dcnt[f"out{ob}"]
            sch.dwait("gpsimd", "out0")
            sch.dwait("gpsimd", "out1")
        stack_y.close()

        sch.finalize()
        with nc.Block() as block:
            @block.tensor
            def _(eng):
                for f in sch.ops["tensor"]:
                    f(eng)

            @block.vector
            def _(eng):
                for f in sch.ops["vector"]:
                    f(eng)

            @block.scalar
            def _(eng):
                for f in sch.ops["scalar"]:
                    f(eng)

            @block.sync
            def _(eng):
                for f in sch.ops["sync"]:
                    f(eng)

            @block.gpsimd
            def _(eng):
                for f in sch.ops["gpsimd"]:
                    f(eng)

    return nc



def _rope_tables(pos, n=DR):
    inv = 10000.0 ** (-np.arange(0, n, 2, dtype=np.float64) / n)
    ang = pos[None, :].astype(np.float64) * inv[:, None]
    c = np.cos(ang).astype(np.float32)
    s = np.sin(ang).astype(np.float32)
    C = np.empty((n, pos.shape[0]), np.float32)
    S = np.empty((n, pos.shape[0]), np.float32)
    C[0::2] = c
    C[1::2] = c
    S[0::2] = -s
    S[1::2] = s
    return np.ascontiguousarray(C), np.ascontiguousarray(S)


def _build_masks(core):
    out = np.zeros((NSLOT * 2, 128, CH), np.float32)
    for t in range(NSLOT):
        g = 8 * t + core
        k_cnt = g // 4 + 1
        jdiag = g % 4
        diag = np.zeros((128, CH), np.float32)
        diag[:, (jdiag + 1) * 128:] = NEG_MASK
        tri = np.triu(np.full((128, 128), NEG_MASK, np.float32), 1)
        diag[:, jdiag * 128: (jdiag + 1) * 128] = tri
        if k_cnt == 2 * t + 2:
            out[2 * t + 0] = 0.0
            out[2 * t + 1] = diag
        else:
            out[2 * t + 0] = diag
            out[2 * t + 1] = NEG_MASK
    return np.ascontiguousarray(out.astype(BF))


def kernel(L0_attn_x, L0_attn_Wqa, L0_attn_Wqb, L0_attn_Wkva, L0_attn_Wkvb,
           L0_attn_Wo, L0_ffn_gate_up_packed, L0_ffn_gate_up_scales,
           L0_ffn_down_packed, L0_ffn_down_scales):
    x = np.ascontiguousarray(np.asarray(L0_attn_x)[0])
    xT = np.ascontiguousarray(x.T)
    Wqa = np.ascontiguousarray(np.asarray(L0_attn_Wqa))
    Wqb = np.asarray(L0_attn_Wqb)
    Wkva = np.asarray(L0_attn_Wkva)
    Wkvb = np.asarray(L0_attn_Wkvb)
    Wo = np.ascontiguousarray(np.asarray(L0_attn_Wo))

    perm = np.arange(DR) ^ 1

    q3 = Wqb.reshape(QL, NH, DN + DR)
    wqbn = np.ascontiguousarray(q3[:, :, :DN].reshape(QL, NH * DN))
    wqbp = np.ascontiguousarray(q3[:, :, DN:].reshape(QL, NH * DR))
    wqbps = np.ascontiguousarray(q3[:, :, DN:][:, :, perm].reshape(QL, NH * DR))

    wkva_ext = np.ascontiguousarray(
        np.concatenate([Wkva, Wkva[:, KVL:KVL + DR][:, perm]], axis=1))

    kv3 = Wkvb.reshape(KVL, NH, 2 * DN)
    wkvbk = np.ascontiguousarray(kv3[:, :, :DN].reshape(KVL, NH * DN))
    wkvbv = np.ascontiguousarray(kv3[:, :, DN:].reshape(KVL, NH * DN))

    gu16 = np.asarray(L0_ffn_gate_up_packed).view(np.int16)
    gu_rep_h = np.ascontiguousarray(np.repeat(gu16.T, 4, axis=0))
    dn16 = np.asarray(L0_ffn_down_packed).view(np.int16)
    dn_rep_h = np.ascontiguousarray(np.repeat(dn16.T, 4, axis=0))
    gu_sc = np.ascontiguousarray(np.asarray(L0_ffn_gate_up_scales))
    dn_sc = np.ascontiguousarray(np.asarray(L0_ffn_down_scales))

    identb = np.ascontiguousarray(np.eye(128, dtype=BF))
    identf = np.ascontiguousarray(np.eye(128, dtype=np.float32))
    colsb = np.ascontiguousarray(
        np.stack([np.full(128, 1.0 / 2048.0), np.ones(128)], axis=1).astype(BF))
    colsf = np.ascontiguousarray(
        np.stack([np.full(128, 1.0 / 2048.0), np.ones(128),
                  np.full(128, EPS_LN)], axis=1).astype(np.float32))
    rowb = np.ones((1, 128), BF)
    rowf = np.ones((1, 128), np.float32)
    shf = np.ascontiguousarray(
        np.stack([4 * (np.arange(128) % 4), 4 * (np.arange(128) % 4) + 3],
                 axis=1).astype(np.int16))
    shfull = np.ascontiguousarray(
        np.broadcast_to((4 * (np.arange(128) % 4)).astype(np.int16)[:, None],
                        (128, 512)).copy())

    kc, ks = _rope_tables(np.arange(SEQ))

    nc = build_program()

    in_maps = []
    all_rows = []
    for c in range(N_CORES):
        rows = np.concatenate([128 * (8 * t + c) + np.arange(128)
                               for t in range(NSLOT)])
        all_rows.append(rows)
        xq = np.ascontiguousarray(xT[:, rows])
        qc, qs = _rope_tables(rows.astype(np.float64))
        in_maps.append(dict(
            xT=xT, xqT=xq,
            ropeq_c=qc, ropeq_s=qs, ropek_c=kc, ropek_s=ks,
            masks=_build_masks(c),
            wqa=Wqa, wqbn=wqbn, wqbp=wqbp, wqbps=wqbps,
            wkva=wkva_ext, wkvbk=wkvbk, wkvbv=wkvbv, wo=Wo,
            gu_rep=gu_rep_h, gu_sc=gu_sc, dn_rep=dn_rep_h, dn_sc=dn_sc,
            identb=identb, identf=identf, colsb=colsb, colsf=colsf,
            rowb=rowb, rowf=rowf, shf=shf, shfull=shfull,
        ))

    res = run_bass_kernel_spmd(nc, in_maps, core_ids=list(range(N_CORES)))

    out = np.empty((1, SEQ, D), np.float32)
    for c in range(N_CORES):
        out[0, all_rows[c], :] = np.asarray(res.results[c]["out_tok"])
    return out


# version 06
# speedup vs baseline: 1.3431x; 1.1784x over previous
import contextlib
import numpy as np
import ml_dtypes

import concourse.bass as bass
import concourse.mybir as mybir
from concourse.bass_utils import run_bass_kernel_spmd
from concourse.alu_op_type import AluOpType as OP

F32 = mybir.dt.float32
BF16 = mybir.dt.bfloat16
F16 = mybir.dt.float16
I16 = mybir.dt.int16
AF = mybir.ActivationFunctionType

N_CORES = 8
SEQ, D, NH, DN, DR = 8192, 2048, 16, 128, 64
QL, KVL = 512, 512
CH = 512
NCH = SEQ // CH
NSLOT = 8
OWN = NSLOT * 128
KT = D // 128
SCALE = float(np.float32(1.0) / np.float32(np.sqrt(np.float32(192.0))))
NEG_MASK = -1e30
NHP = 8
EPS_LN = 1e-5
I_FF = 5632

ENGS = ("tensor", "vector", "scalar", "sync", "gpsimd")
BF = ml_dtypes.bfloat16


class Sched:
    def __init__(self, nc):
        self.nc = nc
        self.ops = {e: [] for e in ENGS}
        self.cnt = {e: 0 for e in ENGS}
        self.sems = {}
        self.last_wait = {}
        self.dsems = {}
        self.dcnt = {}
        self._stack = None
        self.waited = {e: set() for e in ENGS}
        self.inc_at = {e: {} for e in ENGS}

    def setup(self, stack):
        self._stack = stack
        for e in ENGS:
            self.sems[e] = stack.enter_context(self.nc.semaphore("sem_" + e))

    def dsem(self, name):
        if name not in self.dsems:
            self.dsems[name] = self._stack.enter_context(
                self.nc.semaphore("d_" + name))
            self.dcnt[name] = 0
        return self.dsems[name]

    def op(self, eng, fn, inc=1):
        sem = self.sems[eng]
        self.cnt[eng] += inc
        ordinal = self.cnt[eng]

        def run(h, fn=fn, sem=sem, eng=eng, ordinal=ordinal):
            inst = fn(h)
            d = self.inc_at[eng].get(ordinal)
            if d:
                inst.then_inc(sem, d)

        self.ops[eng].append(run)
        return ordinal

    def dma(self, eng, out, in_, sem=None):
        assert sem is not None
        s = self.dsem(sem)

        def run(h, o=out, i=in_, s=s):
            h.dma_start(out=o, in_=i).then_inc(s, 16)

        self.ops[eng].append(run)
        self.dcnt[sem] += 16
        return self.dcnt[sem]

    def dwait(self, eng, sem, cnt=None):
        s = self.dsem(sem)
        if cnt is None:
            cnt = self.dcnt[sem]
        if cnt <= 0:
            return
        key = (eng, "d_" + sem)
        if self.last_wait.get(key, -1) >= cnt:
            return
        self.last_wait[key] = cnt
        self.ops[eng].append(lambda h, s=s, c=cnt: h.wait_ge(s, c))

    def wait(self, eng, on, cnt=None):
        if cnt is None:
            cnt = self.cnt[on]
        if cnt <= 0:
            return
        key = (eng, on)
        if self.last_wait.get(key, -1) >= cnt:
            return
        self.last_wait[key] = cnt
        self.waited[on].add(cnt)
        sem = self.sems[on]
        self.ops[eng].append(lambda h, s=sem, c=cnt: h.wait_ge(s, c))

    def finalize(self):
        for e in ENGS:
            prev = 0
            for c in sorted(self.waited[e]):
                if c > self.cnt[e]:
                    continue
                self.inc_at[e][c] = c - prev
                prev = c

    def barrier(self):
        snap = dict(self.cnt)
        dsnap = dict(self.dcnt)
        for e in ENGS:
            for o in ENGS:
                if o != e:
                    self.wait(e, o, snap[o])
            for name, c in dsnap.items():
                self.dwait(e, name, c)


def build_program():
    nc = bass.Bass("TRN2", target_bir_lowering=False, debug=False,
                   detect_race_conditions=False)

    def din(name, shape, dt):
        return nc.dram_tensor(name, shape, dt, kind="ExternalInput").ap()

    def dint(name, shape, dt):
        return nc.dram_tensor(name, shape, dt, kind="Internal").ap()

    xT = din("xT", [D, SEQ], F32)
    xqT = din("xqT", [D, OWN], F32)
    ropeq_c = din("ropeq_c", [DR, OWN], F32)
    ropeq_s = din("ropeq_s", [DR, OWN], F32)
    ropek_c = din("ropek_c", [DR, SEQ], F32)
    ropek_s = din("ropek_s", [DR, SEQ], F32)
    masks_d = din("masks", [NSLOT * 2, 128, CH], BF16)
    wqa_d = din("wqa", [D, QL], F32)
    wqbn_d = din("wqbn", [QL, NH * DN], F32)
    wqbp_d = din("wqbp", [QL, NH * DR], F32)
    wqbps_d = din("wqbps", [QL, NH * DR], F32)
    wkva_d = din("wkva", [D, 640], F32)
    wkvbk_d = din("wkvbk", [KVL, NH * DN], F32)
    wkvbv_d = din("wkvbv", [KVL, NH * DN], F32)
    wo_d = din("wo", [D, D], F32)
    gu_rep = din("gu_rep", [D, 2 * I_FF], I16)
    gu_sc_d = din("gu_sc", [16, 2 * I_FF], F16)
    dn_rep = din("dn_rep", [I_FF, D], I16)
    dn_sc_d = din("dn_sc", [44, D], F16)
    identb_d = din("identb", [128, 128], BF16)
    identf_d = din("identf", [128, 128], F32)
    colsb_d = din("colsb", [128, 2], BF16)
    colsf_d = din("colsf", [128, 3], F32)
    rowb_d = din("rowb", [1, 128], BF16)
    rowf_d = din("rowf", [1, 128], F32)
    shf_d = din("shf", [128, 2], I16)
    shfull_d = din("shfull", [128, 512], I16)

    out_tok = nc.dram_tensor("out_tok", [OWN, D], F32, kind="ExternalOutput").ap()

    sp_kn = dint("sp_kn", [NCH, 128, NH * CH], BF16)
    sp_v = dint("sp_v", [NCH, 128, 4 * NH * 128], BF16)
    sp_kpe = dint("sp_kpe", [NCH, DR, CH], BF16)
    sp_qnB = dint("sp_qnB", [128, NHP * OWN], BF16)
    sp_qpB = dint("sp_qpB", [DR, NHP * OWN], BF16)
    sp_x1 = dint("sp_x1", [128, KT * OWN], F32)
    sp_wd = dint("sp_wd", [44, 128, D], BF16)
    sp_oA = dint("sp_oA", [128, NSLOT * NHP * 128], BF16)
    sp_oB = dint("sp_oB", [128, NSLOT * NHP * 128], BF16)

    sch = Sched(nc)

    def mm(out, lhsT, rhs, start, stop):
        return sch.op("tensor", lambda h, o=out, l=lhsT, r=rhs, s=start, e=stop:
                      h.matmul(o, l, r, start=s, stop=e))

    with contextlib.ExitStack() as root:
        sch.setup(root)

        def sb(st, name, shape, dt, side=None):
            if side is None:
                return st.enter_context(nc.sbuf_tensor(name, shape, dt))
            return st.enter_context(nc.sbuf_tensor(name, shape, dt, side=side))

        c_idb = sb(root, "c_idb", [128, 128], BF16)
        c_idf = sb(root, "c_idf", [128, 128], F32)
        c_colsb = sb(root, "c_colsb", [128, 2], BF16)
        c_colsf = sb(root, "c_colsf", [128, 3], F32)
        c_rowb = sb(root, "c_rowb", [1, 128], BF16)
        c_rowf = sb(root, "c_rowf", [1, 128], F32)
        c_shf = sb(root, "c_shf", [128, 2], I16)
        c_shfull = sb(root, "c_shfull", [128, 512], I16)
        sums_acc = sb(root, "sums_acc", [128, NSLOT * NHP], F32)
        bfc = [sb(root, f"bfc{i}", [128, 512], BF16) for i in range(2)]
        rsum_sb = sb(root, "rsum_sb", [1, NHP * 128], F32)

        inv2048_b = c_colsb[:, 0:1]
        inv2048_f = c_colsf[:, 0:1]
        nc.const_aps.register(F32, EPS_LN, c_colsf[:, 2:3])
        ones_row_b = c_rowb[:, :]
        ones_row_f = c_rowf[:, :]
        shifts = c_shf[:, 0:1]
        shifts3 = c_shf[:, 1:2]

        PS = [root.enter_context(nc.psum_tensor(f"ps{i}", [128, 512], F32))
              for i in range(6)]
        PTb = [root.enter_context(nc.psum_tensor(f"pt{i}", [128, 1024], BF16))
               for i in range(2)]
        PTf = [p.ap().bitcast(F32) for p in PTb]

        sch.dma("sync", c_idb[:], identb_d, sem="consts")
        sch.dma("sync", c_idf[:], identf_d, sem="consts")
        sch.dma("sync", c_colsb[:], colsb_d, sem="consts")
        sch.dma("sync", c_colsf[:], colsf_d, sem="consts")
        sch.dma("sync", c_rowb[:], rowb_d, sem="consts")
        sch.dma("sync", c_rowf[:], rowf_d, sem="consts")
        sch.dma("sync", c_shf[:], shf_d, sem="consts")
        sch.dma("sync", c_shfull[:], shfull_d, sem="consts")
        for e in ("tensor", "vector", "scalar", "gpsimd", "sync"):
            sch.dwait(e, "consts")

        def stats_rows(get_tile, sqbufs, rows_m, rows_r, rows_b, width=512, src_f32=False):
            sch.wait("scalar", "vector", sch.cnt["vector"])
            sch.wait("tensor", "vector", sch.cnt["vector"])
            for half in range(2):
                mrow = (PS[0] if half == 0 else PS[1])[0:1, 0:width]
                s2row = (PTf[0] if half == 0 else PTf[1])[0:1, 0:width]
                sq_guard = [0, 0]
                bf_guard = [0, 0]
                for k in range(KT):
                    xt = get_tile(k, half)
                    par = k % 2
                    sch.wait("scalar", "tensor", sq_guard[par])
                    c_sq = sch.op("scalar", lambda h, o=sqbufs[par][:, 0:width], i=xt:
                                  h.activation(o, i, AF.Square))
                    if src_f32:
                        btile = bfc[par][:, 0:width]
                        sch.wait("vector", "tensor", bf_guard[par])
                        vb = sch.op("vector", lambda h, o=btile, i=xt: h.tensor_copy(o, i))
                        sch.wait("tensor", "vector", vb)
                        mm(mrow, inv2048_b, btile, k == 0, k == KT - 1)
                        bf_guard[par] = sch.cnt["tensor"]
                    else:
                        mm(mrow, inv2048_b, xt, k == 0, k == KT - 1)
                    sch.wait("tensor", "scalar", c_sq)
                    mm(s2row, inv2048_b, sqbufs[par][:, 0:width], k == 0, k == KT - 1)
                    sq_guard[par] = sch.cnt["tensor"]
                sch.wait("vector", "tensor", sch.cnt["tensor"])
                m_f = rows_m[0:1, half * width: (half + 1) * width]
                m_b = rows_b[0:1, half * width: (half + 1) * width]
                r_f = rows_r[0:1, half * width: (half + 1) * width]
                sch.op("vector", lambda h, o=m_f, i=mrow: h.tensor_copy(o, i))
                sch.op("vector", lambda h, o=m_b, i=mrow: h.tensor_copy(o, i))
                sch.op("vector", lambda h, o=r_f, a=mrow, b=m_f:
                       h.tensor_tensor(o, a, b, OP.mult))
                sch.op("vector", lambda h, o=r_f, a=r_f, b=s2row:
                       h.scalar_tensor_tensor(o, a, -1.0, b, OP.mult, OP.add))
                sch.wait("scalar", "vector", sch.cnt["vector"])
                c_sd = sch.op("scalar", lambda h, o=r_f, i=r_f:
                              h.activation(o, i, AF.Sqrt, bias=EPS_LN))
                sch.wait("vector", "scalar", c_sd)
                sch.op("vector", lambda h, o=r_f, i=r_f: h.reciprocal(o, i))
            return sch.cnt["vector"]

        def ln_apply(get_in, get_out, rows_r, rows_b, tmp2, extra_res=None):
            for half in range(2):
                sch.wait("tensor", "vector", sch.cnt["vector"])
                mm(PS[2][:, 0:512], ones_row_b,
                   rows_b[0:1, half * 512: half * 512 + 512], True, True)
                tb = mm(PS[3][:, 0:512], ones_row_f,
                        rows_r[0:1, half * 512: half * 512 + 512], True, True)
                sch.wait("vector", "tensor", tb)
                for k in range(KT):
                    it = get_in(k, half)
                    ot = get_out(k, half)
                    tf = tmp2[k % 2][:]
                    sch.op("vector", lambda h, o=tf, a=it, b=PS[2][:, 0:512]:
                           h.tensor_tensor(o, a, b, OP.subtract))
                    if extra_res is None:
                        sch.op("vector", lambda h, o=ot, a=tf, b=PS[3][:, 0:512]:
                               h.tensor_tensor(o, a, b, OP.mult))
                    else:
                        rt = extra_res(k, half)
                        sch.op("vector", lambda h, o=tf, a=tf, b=PS[3][:, 0:512]:
                               h.tensor_tensor(o, a, b, OP.mult))
                        sch.op("vector", lambda h, o=ot, a=rt, b=tf:
                               h.tensor_tensor(o, a, b, OP.add))
            return sch.cnt["vector"]

        stack_q = contextlib.ExitStack()
        qnA = sb(stack_q, "qnA", [128, NHP * OWN], BF16, side="right")
        qpA = sb(stack_q, "qpA", [DR, NHP * OWN], BF16, side="right")

        with contextlib.ExitStack() as ph0:
            xq = sb(ph0, "xq", [128, KT * OWN], BF16)
            sqb = [sb(ph0, f"ph0sq{i}", [128, 512], BF16) for i in range(2)]
            rows0m = sb(ph0, "ph0rowm", [1, OWN], F32)
            rows0r = sb(ph0, "ph0rowr", [1, OWN], F32)
            rows0_b = sb(ph0, "ph0rows_b", [1, OWN], BF16)
            tmpf = [sb(ph0, f"ph0tmp{i}", [128, 512], F32) for i in range(2)]
            w_qa = sb(ph0, "w_qa", [128, KT * QL], BF16)
            w_qbn = sb(ph0, "w_qbn", [128, 4 * NH * DN], BF16)
            w_qbp = sb(ph0, "w_qbp", [128, 4 * NH * DR], BF16)
            w_qbps = sb(ph0, "w_qbps", [128, 4 * NH * DR], BF16)
            qa_sb = sb(ph0, "qa_sb", [128, 4 * OWN], BF16)
            qnB = sb(ph0, "qnB", [128, NHP * OWN], BF16)
            qpB = sb(ph0, "qpB", [DR, NHP * OWN], BF16)
            rq_c = sb(ph0, "rq_c", [DR, OWN], F32)
            rq_s = sb(ph0, "rq_s", [DR, OWN], F32)
            rop = [sb(ph0, f"ph0rop{i}", [DR, 512], F32) for i in range(2)]

            sch.dma("gpsimd", xq[:].rearrange("p (k t) -> p k t", k=KT), xqT.rearrange("(k p) t -> p k t", p=128), sem="ph0w")
            sch.dma("gpsimd", w_qa[:].rearrange("p (k t) -> p k t", k=KT), wqa_d.rearrange("(k p) t -> p k t", p=128), sem="ph0w")
            sch.dma("gpsimd", w_qbn[:].rearrange("p (k t) -> p k t", k=4), wqbn_d.rearrange("(k p) t -> p k t", p=128), sem="ph0w")
            sch.dma("gpsimd", w_qbp[:].rearrange("p (k t) -> p k t", k=4), wqbp_d.rearrange("(k p) t -> p k t", p=128), sem="ph0w")
            sch.dma("gpsimd", w_qbps[:].rearrange("p (k t) -> p k t", k=4), wqbps_d.rearrange("(k p) t -> p k t", p=128), sem="ph0w")
            sch.dma("sync", rq_c[:], ropeq_c, sem="ropeq")
            sch.dma("sync", rq_s[:], ropeq_s, sem="ropeq")
            sch.dwait("scalar", "ph0w")
            sch.dwait("tensor", "ph0w")
            sch.dwait("vector", "ropeq")

            stats_rows(lambda k, half: xq[:, k * OWN + half * 512:
                                          k * OWN + half * 512 + 512],
                       sqb, rows0m, rows0r, rows0_b)
            v_ln1 = ln_apply(
                lambda k, half: xq[:, k * OWN + half * 512: k * OWN + half * 512 + 512],
                lambda k, half: xq[:, k * OWN + half * 512: k * OWN + half * 512 + 512],
                rows0r, rows0_b, tmpf)

            prev_ev = v_ln1
            for m in range(4):
                for half in range(2):
                    ps = PS[(m * 2 + half) % 2]
                    sch.wait("tensor", "vector", prev_ev)
                    for k in range(KT):
                        lt = w_qa[:, k * QL + m * 128: k * QL + m * 128 + 128]
                        rt = xq[:, k * OWN + half * 512: k * OWN + half * 512 + 512]
                        mm(ps[:, 0:512], lt, rt, k == 0, k == KT - 1)
                    sch.wait("vector", "tensor", sch.cnt["tensor"])
                    dst = qa_sb[:, m * OWN + half * 512: m * OWN + half * 512 + 512]
                    prev_ev = sch.op("vector", lambda h, o=dst, i=ps[:, 0:512]:
                                     h.tensor_copy(o, i))

            prev_v = prev_ev
            for h_ in range(NH):
                qn_dst = qnA if h_ < NHP else qnB
                qp_dst = qpA if h_ < NHP else qpB
                hh = h_ % NHP
                for half in range(2):
                    sl = slice(hh * OWN + half * 512, hh * OWN + half * 512 + 512)
                    sch.wait("tensor", "vector", prev_v)
                    for k in range(4):
                        lt = w_qbn[:, k * NH * DN + h_ * DN: k * NH * DN + h_ * DN + 128]
                        rt = qa_sb[:, k * OWN + half * 512: k * OWN + half * 512 + 512]
                        mm(PS[0][:, 0:512], lt, rt, k == 0, k == 3)
                    for k in range(4):
                        lt = w_qbp[:, k * NH * DR + h_ * DR: k * NH * DR + h_ * DR + DR]
                        rt = qa_sb[:, k * OWN + half * 512: k * OWN + half * 512 + 512]
                        mm(PS[2][0:DR, 0:512], lt, rt, k == 0, k == 3)
                    for k in range(4):
                        lt = w_qbps[:, k * NH * DR + h_ * DR: k * NH * DR + h_ * DR + DR]
                        rt = qa_sb[:, k * OWN + half * 512: k * OWN + half * 512 + 512]
                        mm(PS[3][0:DR, 0:512], lt, rt, k == 0, k == 3)
                    sch.wait("vector", "tensor", sch.cnt["tensor"])
                    sch.op("vector", lambda h, o=qn_dst[:, sl], i=PS[0][:, 0:512]:
                           h.tensor_copy(o, i))
                    cs = rq_c[:, half * 512: half * 512 + 512]
                    sn = rq_s[:, half * 512: half * 512 + 512]
                    sch.op("vector", lambda h, o=rop[0][:], a=PS[3][0:DR, 0:512], b=sn:
                           h.tensor_tensor(o, a, b, OP.mult))
                    sch.op("vector", lambda h, o=rop[1][:], a=PS[2][0:DR, 0:512], b=cs:
                           h.tensor_tensor(o, a, b, OP.mult))
                    prev_v = sch.op("vector", lambda h, o=qp_dst[0:DR, sl], a=rop[0][:],
                                    b=rop[1][:]: h.tensor_tensor(o, a, b, OP.add))

            sch.wait("gpsimd", "vector", sch.cnt["vector"])
            sch.dma("gpsimd", sp_qnB, qnB[:], sem="qspill")
            sch.dma("gpsimd", sp_qpB, qpB[:], sem="qspill")
            sch.barrier()

        def attn_pass(pass_id, stack, qn, qp, odst):
            if pass_id == 0:
                xc = sb(stack, "xc", [128, KT * CH], BF16)
                sqb = [sb(stack, f"a0sq{i}", [128, 512], BF16) for i in range(2)]
                w_kva = sb(stack, "w_kva", [128, KT * 640], BF16)
                w_kbk = sb(stack, "w_kbk", [128, 4 * NH * DN], BF16)
                w_kbv = sb(stack, "w_kbv", [128, 4 * NH * DN], BF16)
                cT = sb(stack, "cT", [128, 4 * CH], BF16)
                nw1 = sb(stack, "nw1", [1, 640], BF16)
                rk_c = sb(stack, "rk_c", [DR, CH], F32)
                rk_s = sb(stack, "rk_s", [DR, CH], F32)
                crowm = sb(stack, "crowm", [1, CH], F32)
                crowr = sb(stack, "crowr", [1, CH], F32)
                crow_b = sb(stack, "crow_b", [1, CH], BF16)
                rbc = sb(stack, "rbc", [128, 512], F32)
                ropt = [sb(stack, f"a0rp{i}", [DR, 512], F32) for i in range(2)]
                kn_sb = [sb(stack, "kn_sb", [128, NH * CH], BF16)]
                v_sb = [sb(stack, "v_sb", [128, 4 * NH * 128], BF16)]
                kpe_sb = [sb(stack, "kpe_sb", [DR, CH], BF16)]
                nbuf = 1
            else:
                kn_sb = [sb(stack, f"kn_sb{i}", [128, NH * CH], BF16) for i in range(2)]
                v_sb = [sb(stack, f"v_sb{i}", [128, 4 * NH * 128], BF16) for i in range(2)]
                kpe_sb = [sb(stack, f"kpe_sb{i}", [DR, CH], BF16) for i in range(2)]
                nbuf = 2

            mk_sb = [sb(stack, f"mk{pass_id}_{i}", [128, 512], BF16) for i in range(2)]
            p_sb = [sb(stack, f"p{pass_id}_{i}", [128, 512], BF16) for i in range(2)]
            pt_sb = [sb(stack, f"pv{pass_id}_{i}", [128, 512], BF16) for i in range(2)]
            oacc = sb(stack, f"oacc{pass_id}", [128, NSLOT * NHP * 128], F32)
            sums_vis = sb(stack, f"sumsv{pass_id}", [128, NHP], F32)
            onorm = [sb(stack, f"onorm{pass_id}_{i}", [128, NHP * 128], BF16)
                     for i in range(2)]

            if pass_id == 0:
                sch.dma("gpsimd", w_kva[:].rearrange("p (k t) -> p k t", k=KT), wkva_d.rearrange("(k p) t -> p k t", p=128), sem="kvaw")
                sch.dma("gpsimd", w_kbk[:].rearrange("p (k t) -> p k t", k=4), wkvbk_d.rearrange("(k p) t -> p k t", p=128), sem="kvaw")
                sch.dma("gpsimd", w_kbv[:].rearrange("p (k t) -> p k t", k=4), wkvbv_d.rearrange("(k p) t -> p k t", p=128), sem="kvaw")
                sch.dwait("tensor", "kvaw")
                ones_col_b = c_colsb[:, 1:2]
                for (n0, n1, ps) in ((0, 512, PS[0]), (512, 640, PS[1])):
                    for k in range(KT):
                        mm(ps[0:1, 0:n1 - n0], ones_col_b,
                           w_kva[:, k * 640 + n0: k * 640 + n1], k == 0, k == KT - 1)
                sch.wait("vector", "tensor", sch.cnt["tensor"])
                sch.op("vector", lambda h: h.tensor_scalar(
                    nw1[0:1, 0:512], PS[0][0:1, 0:512], -1.0, None, OP.mult, OP.bypass))
                sch.op("vector", lambda h: h.tensor_scalar(
                    nw1[0:1, 512:640], PS[1][0:1, 0:128], -1.0, None, OP.mult, OP.bypass))
                sch.wait("tensor", "vector", sch.cnt["vector"])

            mask_ld = {}
            kvload = [0] * (NCH + 1)
            if pass_id == 0:
                sch.dma("gpsimd", xc[:].rearrange("p (k t) -> p k t", k=KT), xT[:, 0:CH].rearrange("(k p) t -> p k t", p=128), sem="xc")
                sch.dma("sync", rk_c[:], ropek_c[:, 0:CH], sem="ropek")
                sch.dma("sync", rk_s[:], ropek_s[:, 0:CH], sem="ropek")
            else:
                sch.dwait("gpsimd", "spill")
                sch.dma("gpsimd", kn_sb[0][:], sp_kn[0], sem="kv0")
                sch.dma("gpsimd", v_sb[0][:], sp_v[0], sem="kv0")
                sch.dma("gpsimd", kpe_sb[0][:], sp_kpe[0], sem="kv0")
                kvload[0] = ("kv0", sch.dcnt["kv0"])
            mname = f"mask{pass_id}"
            sch.dma("sync", mk_sb[0][:], masks_d[0], sem=mname + "0")
            mask_ld[(0, 0)] = (mname + "0", sch.dcnt[mname + "0"])
            sch.dma("sync", mk_sb[1][:], masks_d[1], sem=mname + "1")
            mask_ld[(0, 1)] = (mname + "1", sch.dcnt[mname + "1"])

            s_banks = (PS[0], PS[1])
            o_pairs = ((PS[2], PS[3]), (PS[4], PS[5]))
            exp_done = [0, 0]
            ptevac_done = [0, 0]
            pmm_done = [0, 0]
            oadd_done = [0, 0]
            sums_read = 0
            opair_idx = 0
            kvspill_done = 0
            xload_g = sch.dcnt.get("xc", 0)
            rload_s = sch.dcnt.get("ropek", 0)

            for v in range(NCH):
                buf = v % nbuf
                if pass_id == 0:
                    sch.dwait("scalar", "xc", xload_g)
                    sch.dwait("tensor", "xc", xload_g)
                    sch.wait("tensor", "scalar", max(exp_done))
                    sch.dwait("vector", "spill", kvspill_done)
                    sq_guard = [0, 0]
                    for k in range(KT):
                        xt = xc[:, k * CH: k * CH + 512]
                        par = k % 2
                        sch.wait("scalar", "tensor", sq_guard[par])
                        c_sq = sch.op("scalar", lambda h, o=sqb[par][:], i=xt:
                                      h.activation(o, i, AF.Square))
                        mm(PS[0][0:1, 0:512], inv2048_b, xt, k == 0, k == KT - 1)
                        sch.wait("tensor", "scalar", c_sq)
                        mm(PS[1][0:1, 0:512], inv2048_b, sqb[par][:], k == 0, k == KT - 1)
                        sq_guard[par] = sch.cnt["tensor"]
                    sch.wait("vector", "tensor", sch.cnt["tensor"])
                    sch.op("vector", lambda h: h.tensor_copy(crowm[0:1, :], PS[0][0:1, 0:512]))
                    sch.op("vector", lambda h: h.tensor_copy(crow_b[0:1, :], PS[0][0:1, 0:512]))
                    sch.op("vector", lambda h: h.tensor_tensor(
                        crowr[0:1, :], PS[0][0:1, 0:512], crowm[0:1, :], OP.mult))
                    sch.op("vector", lambda h: h.scalar_tensor_tensor(
                        crowr[0:1, :], crowr[0:1, :], -1.0, PS[1][0:1, 0:512], OP.mult, OP.add))
                    sch.wait("scalar", "vector", sch.cnt["vector"])
                    c_sd = sch.op("scalar", lambda h: h.activation(
                        crowr[0:1, :], crowr[0:1, :], AF.Sqrt, bias=EPS_LN))
                    sch.wait("vector", "scalar", c_sd)
                    sch.op("vector", lambda h: h.reciprocal(crowr[0:1, :], crowr[0:1, :]))
                    v_r = sch.cnt["vector"]
                    sch.wait("tensor", "vector", v_r)
                    trb = mm(PS[2][:, 0:512], ones_row_f, crowr[0:1, :], True, True)
                    sch.wait("vector", "tensor", trb)
                    v_rbc = sch.op("vector", lambda h: h.tensor_copy(rbc[:], PS[2][:, 0:512]))

                    mt = [(0, 128), (128, 128), (256, 128), (384, 128),
                          (512, 64), (576, 64)]
                    banks = (PS[3], PS[4], PS[5], PS[3], PS[4], PS[5])
                    evac_cnt = [0, 0, 0]
                    pe_ps = {}
                    for mi, (c0, M) in enumerate(mt):
                        ps = banks[mi]
                        if mi >= 3:
                            sch.wait("tensor", "vector", evac_cnt[mi % 3])
                        if mi == 0:
                            sch.wait("tensor", "vector", v_rbc)
                        for k in range(KT):
                            lt = w_kva[:, k * 640 + c0: k * 640 + c0 + M]
                            rt = xc[:, k * CH: k * CH + 512]
                            mm(ps[0:M, 0:512], lt, rt, k == 0, False)
                        tmm = mm(ps[0:M, 0:512], nw1[0:1, c0:c0 + M],
                                 crow_b[0:1, :], False, True)
                        sch.wait("vector", "tensor", tmm)
                        if mi < 4:
                            dst = cT[:, mi * CH: mi * CH + 512]
                            evac_cnt[mi % 3] = sch.op(
                                "vector", lambda h, o=dst, a=ps[:, 0:512], b=rbc[:]:
                                h.tensor_tensor(o, a, b, OP.mult))
                        else:
                            pe_ps[mi] = ps
                    sch.dwait("vector", "ropek", rload_s)
                    sch.op("vector", lambda h: h.tensor_tensor(
                        ropt[0][:], pe_ps[5][0:DR, 0:512], rk_s[:], OP.mult))
                    sch.op("vector", lambda h: h.tensor_tensor(
                        ropt[1][:], pe_ps[4][0:DR, 0:512], rk_c[:], OP.mult))
                    sch.op("vector", lambda h: h.tensor_tensor(
                        ropt[0][:], ropt[0][:], ropt[1][:], OP.add))
                    sch.op("vector", lambda h, o=kpe_sb[buf][:]: h.tensor_tensor(
                        o, ropt[0][:], rbc[0:DR, 0:512], OP.mult))
                    v_ckv = sch.cnt["vector"]

                    if v + 1 < NCH:
                        sch.wait("gpsimd", "tensor", sch.cnt["tensor"])
                        sch.wait("gpsimd", "scalar", sch.cnt["scalar"])
                        sch.dma("gpsimd", xc[:].rearrange("p (k t) -> p k t", k=KT),
                                xT[:, (v + 1) * CH: (v + 2) * CH].rearrange(
                                    "(k p) t -> p k t", p=128), sem="xc")
                        xload_g = sch.dcnt["xc"]
                        sch.wait("sync", "vector", v_ckv)
                        sch.dma("sync", rk_c[:], ropek_c[:, (v + 1) * CH: (v + 2) * CH], sem="ropek")
                        sch.dma("sync", rk_s[:], ropek_s[:, (v + 1) * CH: (v + 2) * CH], sem="ropek")
                        rload_s = sch.dcnt["ropek"]

                    sch.wait("tensor", "vector", v_ckv)
                    ev = [0, 0]
                    for h_ in range(NH):
                        ps = s_banks[h_ % 2]
                        sch.wait("tensor", "vector", ev[h_ % 2])
                        for k in range(4):
                            lt = w_kbk[:, k * NH * DN + h_ * DN: k * NH * DN + h_ * DN + 128]
                            rt = cT[:, k * CH: k * CH + 512]
                            mm(ps[:, 0:512], lt, rt, k == 0, k == 3)
                        sch.wait("vector", "tensor", sch.cnt["tensor"])
                        dst = kn_sb[buf][:, h_ * CH: h_ * CH + 512]
                        ev[h_ % 2] = sch.op("vector", lambda h, o=dst, i=ps[:, 0:512]:
                                            h.tensor_copy(o, i))
                    evv = [0, 0, 0, 0]
                    for j in range(4):
                        for n in range(4):
                            bi = (j * 4 + n) % 4
                            ps = (PS[2], PS[3], PS[4], PS[5])[bi]
                            sch.wait("tensor", "vector", evv[bi])
                            for k in range(4):
                                lt = cT[:, k * CH + j * 128: k * CH + j * 128 + 128]
                                rt = w_kbv[:, k * NH * DN + n * 512:
                                           k * NH * DN + n * 512 + 512]
                                mm(ps[:, 0:512], lt, rt, k == 0, k == 3)
                            sch.wait("vector", "tensor", sch.cnt["tensor"])
                            dst = v_sb[buf][:, j * NH * 128 + n * 512:
                                            j * NH * 128 + n * 512 + 512]
                            evv[bi] = sch.op("vector", lambda h, o=dst, i=ps[:, 0:512]:
                                             h.tensor_copy(o, i))
                    v_kv = sch.cnt["vector"]
                    sch.wait("gpsimd", "vector", v_kv)
                    sch.dma("gpsimd", sp_kn[v], kn_sb[buf][:], sem="spill")
                    sch.dma("gpsimd", sp_v[v], v_sb[buf][:], sem="spill")
                    sch.dma("gpsimd", sp_kpe[v], kpe_sb[buf][:], sem="spill")
                    kvspill_done = sch.dcnt["spill"]
                    sch.wait("tensor", "vector", v_kv)
                    exp_done = [0, 0]
                else:
                    if v + 1 < NCH:
                        sch.wait("gpsimd", "tensor", sch.cnt["tensor"])
                        kb = f"kv{1 - buf}"
                        sch.dma("gpsimd", kn_sb[1 - buf][:], sp_kn[v + 1], sem=kb)
                        sch.dma("gpsimd", v_sb[1 - buf][:], sp_v[v + 1], sem=kb)
                        sch.dma("gpsimd", kpe_sb[1 - buf][:], sp_kpe[v + 1], sem=kb)
                        kvload[v + 1] = (kb, sch.dcnt[kb])
                    sch.dwait("tensor", kvload[v][0], kvload[v][1])

                for t in range(v // 2, NSLOT):
                    maskable = (t == v // 2)
                    mk_idx = v - 2 * t
                    first_visit = (v == 0)
                    pid = opair_idx % 2
                    op_pair = o_pairs[pid]
                    sch.wait("tensor", "vector", oadd_done[pid])
                    if maskable:
                        sch.dwait("tensor", mask_ld[(t, mk_idx)][0], mask_ld[(t, mk_idx)][1])
                    sch.wait("scalar", "vector", sums_read)

                    def emit_s(h_):
                        sbk = h_ % 2
                        ps = s_banks[sbk]
                        sch.wait("tensor", "scalar", exp_done[sbk])
                        hg = h_ + pass_id * NHP
                        lt1 = qn[:, h_ * OWN + t * 128: h_ * OWN + t * 128 + 128]
                        rt1 = kn_sb[buf][:, hg * CH: hg * CH + 512]
                        mm(ps[:, 0:512], lt1, rt1, True, False)
                        lt2 = qp[0:DR, h_ * OWN + t * 128: h_ * OWN + t * 128 + 128]
                        rt2 = kpe_sb[buf][0:DR, 0:512]
                        mm(ps[:, 0:512], lt2, rt2, False, not maskable)
                        if maskable:
                            mm(ps[:, 0:512], c_idb[:], mk_sb[mk_idx][:], False, True)
                        s_done[sbk] = sch.cnt["tensor"]

                    s_done = [0, 0]
                    pt_ready = [0, 0]

                    def emit_pv(h_):
                        sbk = h_ % 2
                        hg = h_ + pass_id * NHP
                        sch.wait("tensor", "vector", pt_ready[sbk])
                        opsl = op_pair[h_ // 4]
                        col = (h_ % 4) * 128
                        for j in range(4):
                            lt = v_sb[buf][:, j * NH * 128 + hg * 128:
                                           j * NH * 128 + hg * 128 + 128]
                            rt = pt_sb[sbk][:, j * 128: j * 128 + 128]
                            mm(opsl[:, col: col + 128], lt, rt,
                               (h_ % 4 == 0) and (j == 0), (h_ % 4 == 3) and (j == 3))
                        pv_done[sbk] = sch.cnt["tensor"]

                    pv_done = [0, 0]
                    emit_s(0)
                    for h_ in range(NHP):
                        sbk = h_ % 2
                        ps = s_banks[sbk]
                        sch.wait("scalar", "tensor", s_done[sbk])
                        c_e = sch.op("scalar", lambda h, o=p_sb[sbk][:], i=ps[:, 0:512],
                                     a=sums_vis[:, h_: h_ + 1]:
                                     h.activation(o, i, AF.Exp, scale=SCALE, accum_out=a))
                        exp_done[sbk] = c_e
                        if h_ + 1 < NHP:
                            emit_s(h_ + 1)
                        sch.wait("tensor", "scalar", c_e)
                        sch.wait("tensor", "vector", ptevac_done[sbk])
                        for j in range(4):
                            sch.op("tensor", lambda h, o=PTb[sbk][:, j * 128: j * 128 + 128],
                                   i=p_sb[sbk][:, j * 128: j * 128 + 128], idm=c_idb[:]:
                                   h.transpose(o, i, idm))
                        pmm_done[sbk] = sch.cnt["tensor"]
                        if h_ >= 1:
                            emit_pv(h_ - 1)
                        sch.wait("vector", "tensor", pmm_done[sbk])
                        sch.wait("vector", "tensor", pv_done[sbk])
                        c_pt = sch.op("vector", lambda h, o=pt_sb[sbk][:],
                                      i=PTb[sbk][:, 0:512]: h.tensor_copy(o, i))
                        ptevac_done[sbk] = c_pt
                        pt_ready[sbk] = c_pt
                    emit_pv(NHP - 1)
                    t_pv = sch.cnt["tensor"]
                    sch.wait("vector", "tensor", t_pv)
                    sch.wait("vector", "scalar", max(exp_done))
                    for pi in range(2):
                        dst = oacc[:, t * NHP * 128 + pi * 512:
                                   t * NHP * 128 + pi * 512 + 512]
                        src = op_pair[pi][:, 0:512]
                        if first_visit:
                            sch.op("vector", lambda h, o=dst, i=src: h.tensor_copy(o, i))
                        else:
                            sch.op("vector", lambda h, o=dst, a=dst, b=src:
                                   h.tensor_tensor(o, a, b, OP.add))
                    ssl = sums_acc[:, t * NHP: (t + 1) * NHP]
                    if first_visit:
                        sch.op("vector", lambda h, o=ssl, i=sums_vis[:]: h.tensor_copy(o, i))
                    else:
                        sch.op("vector", lambda h, o=ssl, a=ssl, b=sums_vis[:]:
                               h.tensor_tensor(o, a, b, OP.add))
                    sums_read = sch.cnt["vector"]
                    oadd_done[pid] = sch.cnt["vector"]
                    opair_idx += 1

                if v + 1 < NCH:
                    nt = (v + 1) // 2
                    nidx = (v + 1) - 2 * nt
                    sch.wait("sync", "tensor", sch.cnt["tensor"])
                    mn = f"mask{pass_id}{nidx}"
                    sch.dma("sync", mk_sb[nidx][:], masks_d[2 * nt + nidx], sem=mn)
                    mask_ld[(nt, nidx)] = (mn, sch.dcnt[mn])

            sch.wait("tensor", "vector", sch.cnt["vector"])
            sch.wait("tensor", "scalar", sch.cnt["scalar"])
            ospill = [0, 0]
            for t in range(NSLOT):
                ob = t % 2
                for h_ in range(NHP):
                    ptv = PTf[h_ // 4]
                    sch.op("tensor", lambda h, i=sums_acc[:, t * NHP + h_:
                                                          t * NHP + h_ + 1],
                           o=ptv[0:1, (h_ % 4) * 128: (h_ % 4) * 128 + 128]:
                           h.transpose(o, i, c_idf[:]))
                sch.wait("vector", "tensor", sch.cnt["tensor"])
                sch.op("vector", lambda h: h.reciprocal(
                    rsum_sb[0:1, 0:512], PTf[0][0:1, 0:512]))
                vv = sch.op("vector", lambda h: h.reciprocal(
                    rsum_sb[0:1, 512:1024], PTf[1][0:1, 0:512]))
                sch.dwait("vector", f"onorm{pass_id}{ob}", ospill[ob])
                for hb in range(2):
                    sch.wait("tensor", "vector", vv)
                    for hh in range(4):
                        h_ = hb * 4 + hh
                        mm(PS[0][:, hh * 128: hh * 128 + 128], ones_row_f,
                           rsum_sb[0:1, h_ * 128: h_ * 128 + 128], True, True)
                    sch.wait("vector", "tensor", sch.cnt["tensor"])
                    off = t * NHP * 128 + hb * 512
                    vv = sch.op("vector", lambda h, o=onorm[ob][:, hb * 512: hb * 512 + 512],
                                a=oacc[:, off: off + 512], b=PS[0][:, 0:512]:
                                h.tensor_tensor(o, a, b, OP.mult))
                sch.wait("gpsimd", "vector", vv)
                sch.dma("gpsimd", odst[:, t * NHP * 128: (t + 1) * NHP * 128],
                        onorm[ob][:], sem=f"onorm{pass_id}{ob}")
                ospill[ob] = sch.dcnt[f"onorm{pass_id}{ob}"]
            sch.barrier()

        with contextlib.ExitStack() as pa:
            attn_pass(0, pa, qnA[:], qpA[:], sp_oA)
        with contextlib.ExitStack() as pb:
            sch.dwait("gpsimd", "qspill")
            sch.dma("gpsimd", qnA[:], sp_qnB, sem="qload")
            sch.dma("gpsimd", qpA[:], sp_qpB, sem="qload")
            sch.dwait("tensor", "qload")
            attn_pass(1, pb, qnA[:], qpA[:], sp_oB)
        stack_q.close()

        stack_ln2 = contextlib.ExitStack()
        ln2 = sb(stack_ln2, "ln2", [128, KT * OWN], BF16, side="right")

        with contextlib.ExitStack() as p3:
            xo = sb(p3, "xo", [128, KT * OWN], F32)
            ab = sb(p3, "ab", [128, KT * OWN], BF16)
            wo_sb = [sb(p3, f"wo_sb{i}", [128, KT * 128], BF16) for i in range(2)]
            sqb3 = [sb(p3, f"p3sq{i}", [128, 512], BF16) for i in range(2)]
            rows3m = sb(p3, "p3rowm", [1, OWN], F32)
            rows3r = sb(p3, "p3rowr", [1, OWN], F32)
            rows3_b = sb(p3, "p3rows_b", [1, OWN], BF16)
            tmp3 = [sb(p3, f"p3tmp{i}", [128, 512], F32) for i in range(2)]

            oA = sb(p3, "oA3", [128, NSLOT * NHP * 128], BF16)
            oB = sb(p3, "oB3", [128, NSLOT * NHP * 128], BF16)
            sch.dma("sync", oA[:], sp_oA, sem="p3ld")
            sch.dma("sync", oB[:], sp_oB, sem="p3ld")
            sch.dma("sync", xo[:].rearrange("p (k t) -> p k t", k=KT), xqT.rearrange("(k p) t -> p k t", p=128), sem="p3ld")
            sch.dwait("tensor", "p3ld")
            sch.dma("gpsimd", wo_sb[0][:].rearrange("p (k t) -> p k t", k=KT),
                    wo_d[:, 0:128].rearrange("(k p) m2 -> p k m2", p=128), sem="wo0")
            wo_ld = [16, 0]

            oab = (oA, oB)
            prev_ev = 0
            for m in range(KT):
                if m + 1 < KT:
                    sch.wait("gpsimd", "tensor", sch.cnt["tensor"])
                    wn = f"wo{(m + 1) % 2}"
                    sch.dma("gpsimd", wo_sb[(m + 1) % 2][:].rearrange("p (k t) -> p k t", k=KT),
                            wo_d[:, (m + 1) * 128: (m + 2) * 128].rearrange(
                                "(k p) m2 -> p k m2", p=128), sem=wn)
                    wo_ld[(m + 1) % 2] = sch.dcnt[wn]
                ppair = (PS[2], PS[3]) if m % 2 == 0 else (PS[4], PS[5])
                sch.dwait("tensor", f"wo{m % 2}", wo_ld[m % 2])
                sch.wait("tensor", "vector", prev_ev)
                for half in range(2):
                    ps = ppair[half]
                    for k in range(NH):
                        lt = wo_sb[m % 2][:, k * 128: k * 128 + 128]
                        src = oab[k // NHP]
                        hh = k % NHP
                        rt = src[:].rearrange("p (t x) -> p t x", t=NSLOT)[
                            :, half * 4: half * 4 + 4, hh * 128: hh * 128 + 128]
                        mm(ps[:, 0:512], lt, rt, k == 0, k == NH - 1)
                sch.wait("vector", "tensor", sch.cnt["tensor"])
                for half in range(2):
                    dst = ab[:, m * OWN + half * 512: m * OWN + half * 512 + 512]
                    prev_ev = sch.op("vector", lambda h, o=dst, i=ppair[half][:, 0:512]:
                                     h.tensor_copy(o, i))

            stats_rows(lambda k, half: ab[:, k * OWN + half * 512:
                                          k * OWN + half * 512 + 512],
                       sqb3, rows3m, rows3r, rows3_b)
            sch.dwait("vector", "p3ld")
            v_x1 = ln_apply(
                lambda k, half: ab[:, k * OWN + half * 512: k * OWN + half * 512 + 512],
                lambda k, half: xo[:, k * OWN + half * 512: k * OWN + half * 512 + 512],
                rows3r, rows3_b, tmp3,
                extra_res=lambda k, half: xo[:, k * OWN + half * 512:
                                             k * OWN + half * 512 + 512])
            for k in range(KT):
                for half in range(2):
                    xsl = xo[:, k * OWN + half * 512: k * OWN + half * 512 + 512]
                    bsl = ln2[:, k * OWN + half * 512: k * OWN + half * 512 + 512]
                    sch.op("vector", lambda h, o=bsl, i=xsl: h.tensor_copy(o, i))
            v_x1b = sch.cnt["vector"]
            sch.wait("gpsimd", "vector", v_x1)
            sch.dma("gpsimd", sp_x1, xo[:], sem="x1sp")

            sch.wait("tensor", "vector", v_x1b)
            stats_rows(lambda k, half: ln2[:, k * OWN + half * 512:
                                           k * OWN + half * 512 + 512],
                       sqb3, rows3m, rows3r, rows3_b)
            ln_apply(
                lambda k, half: ln2[:, k * OWN + half * 512: k * OWN + half * 512 + 512],
                lambda k, half: ln2[:, k * OWN + half * 512: k * OWN + half * 512 + 512],
                rows3r, rows3_b, tmp3)
            sch.barrier()

        stack_sg = contextlib.ExitStack()
        sgh = sb(stack_sg, "sgh", [128, 44 * OWN], BF16)

        with contextlib.ExitStack() as p4:
            wraw = [sb(p4, f"wraw{i}", [128, 512], I16) for i in range(2)]
            wstage = [sb(p4, f"wstage{i}", [128, 512], BF16) for i in range(2)]
            wsc = [sb(p4, f"wsc{i}", [128, 512], F16) for i in range(2)]
            dq = [sb(p4, f"dq{i}", [128, 512], I16) for i in range(3)]
            wsl = [sb(p4, f"wsl{i}", [128, 16 * 512], BF16) for i in range(2)]

            NSL = 22
            wsl_free_t = [0, 0]
            wsl_ready = [0, 0]
            raw_free = [0, 0]
            raw_ld = [0, 0]
            evac_v = [0, 0, 0, 0]
            evac_s = [0, 0, 0, 0]
            for s in range(NSL):
                cur = s % 2
                sch.wait("vector", "tensor", wsl_free_t[cur])
                for g in range(16):
                    par = g % 2
                    sch.wait("gpsimd", "vector", raw_free[par])
                    rn = f"raw{par}"
                    sch.dma("gpsimd", wraw[par][:],
                            gu_rep[g * 128: (g + 1) * 128, s * 512: (s + 1) * 512], sem=rn)
                    sch.dma("gpsimd", wsc[par][:],
                            gu_sc_d[g: g + 1, s * 512: (s + 1) * 512].broadcast_to([128, 512]), sem=rn)
                    raw_ld[par] = sch.dcnt[rn]
                    sch.dwait("vector", rn, raw_ld[par])
                    sch.op("vector", lambda h, w=wraw[par][:]: h.tensor_tensor(
                        dq[0][:], w, c_shfull[:], OP.logical_shift_right))
                    raw_free[par] = sch.cnt["vector"]
                    sch.op("vector", lambda h: h.tensor_scalar(
                        dq[1][:], dq[0][:], 3, 1, OP.logical_shift_right, OP.bitwise_and))
                    sch.op("vector", lambda h: h.tensor_scalar(
                        dq[0][:], dq[0][:], 7, None, OP.bitwise_and, OP.bypass))
                    sch.op("vector", lambda h: h.tensor_scalar(
                        dq[2][:], dq[0][:], 252.0, None, OP.add, OP.bypass))
                    sch.op("vector", lambda h: h.scalar_tensor_tensor(
                        dq[0][:], dq[0][:], 252.0, dq[2][:], OP.mult, OP.min))
                    sch.op("vector", lambda h: h.tensor_scalar(
                        dq[0][:], dq[0][:], 6, None, OP.logical_shift_left, OP.bypass))
                    sch.op("vector", lambda h: h.tensor_scalar(
                        dq[1][:], dq[1][:], 15, None, OP.logical_shift_left, OP.bypass))
                    sch.op("vector", lambda h: h.tensor_tensor(
                        dq[2][:], dq[1][:], dq[0][:], OP.bitwise_or))
                    dst = wsl[cur][:, g * 512: (g + 1) * 512]
                    raw_free[par] = sch.op("vector", lambda h, o=dst, w=wsc[par][:]:
                                           h.tensor_tensor(o, dq[2][:].bitcast(BF16),
                                                           w, OP.mult))
                wsl_ready[cur] = sch.cnt["vector"]

                for dt_ in range(s * 8, min(s * 8 + 8, 176)):
                    g5, os5 = dt_ % 44, dt_ // 44
                    par = dt_ % 2
                    sch.wait("gpsimd", "vector", raw_free[par])
                    rn = f"raw{par}"
                    sch.dma("gpsimd", wraw[par][:],
                            dn_rep[g5 * 128: (g5 + 1) * 128, os5 * 512: (os5 + 1) * 512],
                            sem=rn)
                    sch.dma("gpsimd", wsc[par][:],
                            dn_sc_d[g5: g5 + 1, os5 * 512: (os5 + 1) * 512].broadcast_to([128, 512]),
                            sem=rn)
                    raw_ld[par] = sch.dcnt[rn]
                    sch.dwait("vector", rn, raw_ld[par])
                    sch.dwait("vector", f"wdsp{par}")
                    sch.op("vector", lambda h, w=wraw[par][:]: h.tensor_tensor(
                        dq[0][:], w, c_shfull[:], OP.logical_shift_right))
                    raw_free[par] = sch.cnt["vector"]
                    sch.op("vector", lambda h: h.tensor_scalar(
                        dq[1][:], dq[0][:], 3, 1, OP.logical_shift_right, OP.bitwise_and))
                    sch.op("vector", lambda h: h.tensor_scalar(
                        dq[0][:], dq[0][:], 7, None, OP.bitwise_and, OP.bypass))
                    sch.op("vector", lambda h: h.tensor_scalar(
                        dq[2][:], dq[0][:], 252.0, None, OP.add, OP.bypass))
                    sch.op("vector", lambda h: h.scalar_tensor_tensor(
                        dq[0][:], dq[0][:], 252.0, dq[2][:], OP.mult, OP.min))
                    sch.op("vector", lambda h: h.tensor_scalar(
                        dq[0][:], dq[0][:], 6, None, OP.logical_shift_left, OP.bypass))
                    sch.op("vector", lambda h: h.tensor_scalar(
                        dq[1][:], dq[1][:], 15, None, OP.logical_shift_left, OP.bypass))
                    sch.op("vector", lambda h: h.tensor_tensor(
                        dq[2][:], dq[1][:], dq[0][:], OP.bitwise_or))
                    vws = sch.op("vector", lambda h, o=wstage[par][:], w=wsc[par][:]:
                                 h.tensor_tensor(o, dq[2][:].bitcast(BF16), w, OP.mult))
                    sch.wait("gpsimd", "vector", vws)
                    sch.dma("gpsimd", sp_wd[g5, :, os5 * 512: (os5 + 1) * 512],
                            wstage[par][:], sem=f"wdsp{par}")

                sch.wait("tensor", "vector", wsl_ready[cur])
                for grp in range(2):
                    for g in range(16):
                        for m2 in range(2):
                            m4 = grp * 2 + m2
                            for half in range(2):
                                bi = m2 * 2 + half
                                ps = (PS[2], PS[3], PS[4], PS[5])[bi]
                                if g == 0:
                                    sch.wait("tensor", "vector", evac_v[bi])
                                    sch.wait("tensor", "scalar", evac_s[bi])
                                lt = wsl[cur][:, g * 512 + m4 * 128:
                                              g * 512 + m4 * 128 + 128]
                                rt = ln2[:, g * OWN + half * 512:
                                         g * OWN + half * 512 + 512]
                                mm(ps[:, 0:512], lt, rt, g == 0, g == 15)
                    t4 = sch.cnt["tensor"]
                    for m2 in range(2):
                        m4 = grp * 2 + m2
                        for half in range(2):
                            bi = m2 * 2 + half
                            ps = (PS[2], PS[3], PS[4], PS[5])[bi]
                            if s < 11:
                                dst = sgh[:, (s * 4 + m4) * OWN + half * 512:
                                          (s * 4 + m4) * OWN + half * 512 + 512]
                                sch.wait("scalar", "tensor", t4)
                                evac_s[bi] = sch.op(
                                    "scalar", lambda h, o=dst, i=ps[:, 0:512]:
                                    h.activation(o, i, AF.Silu))
                            else:
                                su = s - 11
                                dst = sgh[:, (su * 4 + m4) * OWN + half * 512:
                                          (su * 4 + m4) * OWN + half * 512 + 512]
                                sch.wait("vector", "tensor", t4)
                                evac_v[bi] = sch.op(
                                    "vector", lambda h, o=dst, a=dst, b=ps[:, 0:512]:
                                    h.tensor_tensor(o, a, b, OP.mult))
                wsl_free_t[cur] = sch.cnt["tensor"]
            sch.barrier()
        stack_ln2.close()

        stack_y = contextlib.ExitStack()
        y_sb = sb(stack_y, "y_sb", [128, KT * OWN], F32, side="right")

        with contextlib.ExitStack() as p5:
            wdb = [sb(p5, f"wdb{i}", [128, 8 * 256], BF16) for i in range(2)]

            ev5 = [0, 0, 0, 0]
            batches = []
            for os2 in range(8):
                for b0 in range(0, 44, 8):
                    batches.append((os2, list(range(b0, min(b0 + 8, 44)))))
            wb_ready = [0, 0]
            wb_free = [0, 0]

            def load_batch(bi_, os2, gl):
                cur = bi_ % 2
                wn = f"wdl{cur}"
                sch.dwait("gpsimd", f"wdsp0")
                sch.dwait("gpsimd", f"wdsp1")
                sch.wait("gpsimd", "tensor", wb_free[cur])
                for j, g in enumerate(gl):
                    sch.dma("gpsimd", wdb[cur][:, j * 256: j * 256 + 256],
                            sp_wd[g, :, os2 * 256: (os2 + 1) * 256], sem=wn)
                wb_ready[cur] = sch.dcnt[wn]

            load_batch(0, *batches[0])
            for bi_, (os2, gl) in enumerate(batches):
                cur = bi_ % 2
                if bi_ + 1 < len(batches):
                    load_batch(bi_ + 1, *batches[bi_ + 1])
                sch.dwait("tensor", f"wdl{cur}", wb_ready[cur])
                for j, g in enumerate(gl):
                    for m2 in range(2):
                        for half in range(2):
                            b_ = m2 * 2 + half
                            ps = (PS[2], PS[3], PS[4], PS[5])[b_]
                            if g == 0:
                                sch.wait("tensor", "vector", ev5[b_])
                            lt = wdb[cur][:, j * 256 + m2 * 128: j * 256 + m2 * 128 + 128]
                            rt = sgh[:, g * OWN + half * 512: g * OWN + half * 512 + 512]
                            mm(ps[:, 0:512], lt, rt, g == 0, g == 43)
                wb_free[cur] = sch.cnt["tensor"]
                if gl[-1] == 43:
                    t5 = sch.cnt["tensor"]
                    sch.wait("vector", "tensor", t5)
                    for m2 in range(2):
                        for half in range(2):
                            b_ = m2 * 2 + half
                            ps = (PS[2], PS[3], PS[4], PS[5])[b_]
                            o_tile = os2 * 2 + m2
                            dst = y_sb[:, o_tile * OWN + half * 512:
                                       o_tile * OWN + half * 512 + 512]
                            ev5[b_] = sch.op("vector", lambda h, o=dst, i=ps[:, 0:512]:
                                             h.tensor_copy(o, i))
            sch.barrier()
        stack_sg.close()

        with contextlib.ExitStack() as p6:
            xo2 = sb(p6, "xo2", [128, KT * OWN], F32)
            sqb6 = [sb(p6, f"p6sq{i}", [128, 512], BF16) for i in range(2)]
            rows6m = sb(p6, "p6rowm", [1, OWN], F32)
            rows6r = sb(p6, "p6rowr", [1, OWN], F32)
            rows6_b = sb(p6, "p6rows_b", [1, OWN], BF16)
            tmp6 = [sb(p6, f"p6tmp{i}", [128, 512], F32) for i in range(2)]
            outsb = [sb(p6, f"outsb{i}", [128, D], F32) for i in range(2)]

            sch.dwait("gpsimd", "x1sp")
            sch.dma("gpsimd", xo2[:], sp_x1, sem="x1ld")

            stats_rows(lambda k, half: y_sb[:, k * OWN + half * 512:
                                            k * OWN + half * 512 + 512],
                       sqb6, rows6m, rows6r, rows6_b, src_f32=True)
            sch.dwait("vector", "x1ld")
            v_x2 = ln_apply(
                lambda k, half: y_sb[:, k * OWN + half * 512: k * OWN + half * 512 + 512],
                lambda k, half: xo2[:, k * OWN + half * 512: k * OWN + half * 512 + 512],
                rows6r, rows6_b, tmp6,
                extra_res=lambda k, half: xo2[:, k * OWN + half * 512:
                                              k * OWN + half * 512 + 512])

            tr_roll = [0, 0]
            out_roll = [0, 0]
            for ti in range(NSLOT):
                ob = ti % 2
                sch.dwait("vector", f"out{ob}", out_roll[ob])
                sch.wait("tensor", "vector", v_x2)
                for k in range(KT):
                    par = k % 2
                    ps = (PS[0], PS[1])[par]
                    sch.wait("tensor", "vector", tr_roll[par])
                    sch.op("tensor", lambda h, o=ps[:, 0:128],
                           i=xo2[:, k * OWN + ti * 128: k * OWN + ti * 128 + 128],
                           idm=c_idf[:]: h.transpose(o, i, idm))
                    sch.wait("vector", "tensor", sch.cnt["tensor"])
                    tr_roll[par] = sch.op(
                        "vector", lambda h, o=outsb[ob][:, k * 128: k * 128 + 128],
                        i=ps[:, 0:128]: h.tensor_copy(o, i))
                sch.wait("gpsimd", "vector", sch.cnt["vector"])
                sch.dma("gpsimd", out_tok[ti * 128: (ti + 1) * 128, :], outsb[ob][:],
                        sem=f"out{ob}")
                out_roll[ob] = sch.dcnt[f"out{ob}"]
            sch.dwait("gpsimd", "out0")
            sch.dwait("gpsimd", "out1")
        stack_y.close()

        sch.finalize()
        with nc.Block() as block:
            @block.tensor
            def _(eng):
                for f in sch.ops["tensor"]:
                    f(eng)

            @block.vector
            def _(eng):
                for f in sch.ops["vector"]:
                    f(eng)

            @block.scalar
            def _(eng):
                for f in sch.ops["scalar"]:
                    f(eng)

            @block.sync
            def _(eng):
                for f in sch.ops["sync"]:
                    f(eng)

            @block.gpsimd
            def _(eng):
                for f in sch.ops["gpsimd"]:
                    f(eng)

    return nc



def _rope_tables(pos, n=DR):
    inv = 10000.0 ** (-np.arange(0, n, 2, dtype=np.float64) / n)
    ang = pos[None, :].astype(np.float64) * inv[:, None]
    c = np.cos(ang).astype(np.float32)
    s = np.sin(ang).astype(np.float32)
    C = np.empty((n, pos.shape[0]), np.float32)
    S = np.empty((n, pos.shape[0]), np.float32)
    C[0::2] = c
    C[1::2] = c
    S[0::2] = -s
    S[1::2] = s
    return np.ascontiguousarray(C), np.ascontiguousarray(S)


def _build_masks(core):
    out = np.zeros((NSLOT * 2, 128, CH), np.float32)
    for t in range(NSLOT):
        g = 8 * t + core
        k_cnt = g // 4 + 1
        jdiag = g % 4
        diag = np.zeros((128, CH), np.float32)
        diag[:, (jdiag + 1) * 128:] = NEG_MASK
        tri = np.triu(np.full((128, 128), NEG_MASK, np.float32), 1)
        diag[:, jdiag * 128: (jdiag + 1) * 128] = tri
        if k_cnt == 2 * t + 2:
            out[2 * t + 0] = 0.0
            out[2 * t + 1] = diag
        else:
            out[2 * t + 0] = diag
            out[2 * t + 1] = NEG_MASK
    return np.ascontiguousarray(out.astype(BF))


def kernel(L0_attn_x, L0_attn_Wqa, L0_attn_Wqb, L0_attn_Wkva, L0_attn_Wkvb,
           L0_attn_Wo, L0_ffn_gate_up_packed, L0_ffn_gate_up_scales,
           L0_ffn_down_packed, L0_ffn_down_scales):
    x = np.ascontiguousarray(np.asarray(L0_attn_x)[0])
    xT = np.ascontiguousarray(x.T)
    Wqa = np.ascontiguousarray(np.asarray(L0_attn_Wqa))
    Wqb = np.asarray(L0_attn_Wqb)
    Wkva = np.asarray(L0_attn_Wkva)
    Wkvb = np.asarray(L0_attn_Wkvb)
    Wo = np.ascontiguousarray(np.asarray(L0_attn_Wo))

    perm = np.arange(DR) ^ 1

    q3 = Wqb.reshape(QL, NH, DN + DR)
    wqbn = np.ascontiguousarray(q3[:, :, :DN].reshape(QL, NH * DN))
    wqbp = np.ascontiguousarray(q3[:, :, DN:].reshape(QL, NH * DR))
    wqbps = np.ascontiguousarray(q3[:, :, DN:][:, :, perm].reshape(QL, NH * DR))

    wkva_ext = np.ascontiguousarray(
        np.concatenate([Wkva, Wkva[:, KVL:KVL + DR][:, perm]], axis=1))

    kv3 = Wkvb.reshape(KVL, NH, 2 * DN)
    wkvbk = np.ascontiguousarray(kv3[:, :, :DN].reshape(KVL, NH * DN))
    wkvbv = np.ascontiguousarray(kv3[:, :, DN:].reshape(KVL, NH * DN))

    gu16 = np.asarray(L0_ffn_gate_up_packed).view(np.int16)
    gu_rep_h = np.ascontiguousarray(np.repeat(gu16.T, 4, axis=0))
    dn16 = np.asarray(L0_ffn_down_packed).view(np.int16)
    dn_rep_h = np.ascontiguousarray(np.repeat(dn16.T, 4, axis=0))
    gu_sc = np.ascontiguousarray(np.asarray(L0_ffn_gate_up_scales))
    dn_sc = np.ascontiguousarray(np.asarray(L0_ffn_down_scales))

    identb = np.ascontiguousarray(np.eye(128, dtype=BF))
    identf = np.ascontiguousarray(np.eye(128, dtype=np.float32))
    colsb = np.ascontiguousarray(
        np.stack([np.full(128, 1.0 / 2048.0), np.ones(128)], axis=1).astype(BF))
    colsf = np.ascontiguousarray(
        np.stack([np.full(128, 1.0 / 2048.0), np.ones(128),
                  np.full(128, EPS_LN)], axis=1).astype(np.float32))
    rowb = np.ones((1, 128), BF)
    rowf = np.ones((1, 128), np.float32)
    shf = np.ascontiguousarray(
        np.stack([4 * (np.arange(128) % 4), 4 * (np.arange(128) % 4) + 3],
                 axis=1).astype(np.int16))
    shfull = np.ascontiguousarray(
        np.broadcast_to((4 * (np.arange(128) % 4)).astype(np.int16)[:, None],
                        (128, 512)).copy())

    kc, ks = _rope_tables(np.arange(SEQ))

    nc = build_program()

    in_maps = []
    all_rows = []
    for c in range(N_CORES):
        rows = np.concatenate([128 * (8 * t + c) + np.arange(128)
                               for t in range(NSLOT)])
        all_rows.append(rows)
        xq = np.ascontiguousarray(xT[:, rows])
        qc, qs = _rope_tables(rows.astype(np.float64))
        in_maps.append(dict(
            xT=xT, xqT=xq,
            ropeq_c=qc, ropeq_s=qs, ropek_c=kc, ropek_s=ks,
            masks=_build_masks(c),
            wqa=Wqa, wqbn=wqbn, wqbp=wqbp, wqbps=wqbps,
            wkva=wkva_ext, wkvbk=wkvbk, wkvbv=wkvbv, wo=Wo,
            gu_rep=gu_rep_h, gu_sc=gu_sc, dn_rep=dn_rep_h, dn_sc=dn_sc,
            identb=identb, identf=identf, colsb=colsb, colsf=colsf,
            rowb=rowb, rowf=rowf, shf=shf, shfull=shfull,
        ))

    res = run_bass_kernel_spmd(nc, in_maps, core_ids=list(range(N_CORES)))

    out = np.empty((1, SEQ, D), np.float32)
    for c in range(N_CORES):
        out[0, all_rows[c], :] = np.asarray(res.results[c]["out_tok"])
    return out


# version 07 (final)
# speedup vs baseline: 1.3682x; 1.0188x over previous
import contextlib
import numpy as np
import ml_dtypes

import concourse.bass as bass
import concourse.mybir as mybir
from concourse.bass_utils import run_bass_kernel_spmd
from concourse.alu_op_type import AluOpType as OP

F32 = mybir.dt.float32
BF16 = mybir.dt.bfloat16
F16 = mybir.dt.float16
I16 = mybir.dt.int16
AF = mybir.ActivationFunctionType

N_CORES = 8
SEQ, D, NH, DN, DR = 8192, 2048, 16, 128, 64
QL, KVL = 512, 512
CH = 512
NCH = SEQ // CH
NSLOT = 8
OWN = NSLOT * 128
KT = D // 128
SCALE = float(np.float32(1.0) / np.float32(np.sqrt(np.float32(192.0))))
NEG_MASK = -1e30
NHP = 8
EPS_LN = 1e-5
I_FF = 5632

ENGS = ("tensor", "vector", "scalar", "sync", "gpsimd")
BF = ml_dtypes.bfloat16


class Sched:
    def __init__(self, nc):
        self.nc = nc
        self.ops = {e: [] for e in ENGS}
        self.cnt = {e: 0 for e in ENGS}
        self.sems = {}
        self.last_wait = {}
        self.dsems = {}
        self.dcnt = {}
        self._stack = None
        self.waited = {e: set() for e in ENGS}
        self.inc_at = {e: {} for e in ENGS}

    def setup(self, stack):
        self._stack = stack
        for e in ENGS:
            self.sems[e] = stack.enter_context(self.nc.semaphore("sem_" + e))

    def dsem(self, name):
        if name not in self.dsems:
            self.dsems[name] = self._stack.enter_context(
                self.nc.semaphore("d_" + name))
            self.dcnt[name] = 0
        return self.dsems[name]

    def op(self, eng, fn, inc=1):
        sem = self.sems[eng]
        self.cnt[eng] += inc
        ordinal = self.cnt[eng]

        def run(h, fn=fn, sem=sem, eng=eng, ordinal=ordinal):
            inst = fn(h)
            d = self.inc_at[eng].get(ordinal)
            if d:
                inst.then_inc(sem, d)

        self.ops[eng].append(run)
        return ordinal

    def dma(self, eng, out, in_, sem=None):
        assert sem is not None
        s = self.dsem(sem)

        def run(h, o=out, i=in_, s=s):
            h.dma_start(out=o, in_=i).then_inc(s, 16)

        self.ops[eng].append(run)
        self.dcnt[sem] += 16
        return self.dcnt[sem]

    def dwait(self, eng, sem, cnt=None):
        s = self.dsem(sem)
        if cnt is None:
            cnt = self.dcnt[sem]
        if cnt <= 0:
            return
        key = (eng, "d_" + sem)
        if self.last_wait.get(key, -1) >= cnt:
            return
        self.last_wait[key] = cnt
        self.ops[eng].append(lambda h, s=s, c=cnt: h.wait_ge(s, c))

    def wait(self, eng, on, cnt=None):
        if cnt is None:
            cnt = self.cnt[on]
        if cnt <= 0:
            return
        key = (eng, on)
        if self.last_wait.get(key, -1) >= cnt:
            return
        self.last_wait[key] = cnt
        self.waited[on].add(cnt)
        sem = self.sems[on]
        self.ops[eng].append(lambda h, s=sem, c=cnt: h.wait_ge(s, c))

    def finalize(self):
        for e in ENGS:
            prev = 0
            for c in sorted(self.waited[e]):
                if c > self.cnt[e]:
                    continue
                self.inc_at[e][c] = c - prev
                prev = c

    def barrier(self):
        snap = dict(self.cnt)
        dsnap = dict(self.dcnt)
        for e in ENGS:
            for o in ENGS:
                if o != e:
                    self.wait(e, o, snap[o])
            for name, c in dsnap.items():
                self.dwait(e, name, c)


def build_program():
    nc = bass.Bass("TRN2", target_bir_lowering=False, debug=False,
                   detect_race_conditions=False)

    def din(name, shape, dt):
        return nc.dram_tensor(name, shape, dt, kind="ExternalInput").ap()

    def dint(name, shape, dt):
        return nc.dram_tensor(name, shape, dt, kind="Internal").ap()

    xT = din("xT", [D, SEQ], F32)
    xqT = din("xqT", [D, OWN], F32)
    ropeq_c = din("ropeq_c", [DR, OWN], F32)
    ropeq_s = din("ropeq_s", [DR, OWN], F32)
    ropek_c = din("ropek_c", [DR, SEQ], F32)
    ropek_s = din("ropek_s", [DR, SEQ], F32)
    masks_d = din("masks", [NSLOT * 2, 128, CH], BF16)
    wqa_d = din("wqa", [D, QL], F32)
    wqbn_d = din("wqbn", [QL, NH * DN], F32)
    wqbp_d = din("wqbp", [QL, NH * DR], F32)
    wqbps_d = din("wqbps", [QL, NH * DR], F32)
    wkva_d = din("wkva", [D, 640], F32)
    wkvbk_d = din("wkvbk", [KVL, NH * DN], F32)
    wkvbv_d = din("wkvbv", [KVL, NH * DN], F32)
    wo_d = din("wo", [D, D], F32)
    gu_rep = din("gu_rep", [D, 2 * I_FF], I16)
    gu_sc_d = din("gu_sc", [16, 2 * I_FF], F16)
    dn_rep = din("dn_rep", [I_FF, D], I16)
    dn_sc_d = din("dn_sc", [44, D], F16)
    identb_d = din("identb", [128, 128], BF16)
    identf_d = din("identf", [128, 128], F32)
    colsb_d = din("colsb", [128, 2], BF16)
    colsf_d = din("colsf", [128, 3], F32)
    rowb_d = din("rowb", [1, 128], BF16)
    rowf_d = din("rowf", [1, 128], F32)
    shf_d = din("shf", [128, 2], I16)
    shfull_d = din("shfull", [128, 512], I16)

    out_tok = nc.dram_tensor("out_tok", [OWN, D], F32, kind="ExternalOutput").ap()

    sp_kn = dint("sp_kn", [NCH, 128, NH * CH], BF16)
    sp_v = dint("sp_v", [NCH, 128, 4 * NH * 128], BF16)
    sp_kpe = dint("sp_kpe", [NCH, DR, CH], BF16)
    sp_qnB = dint("sp_qnB", [128, NHP * OWN], BF16)
    sp_qpB = dint("sp_qpB", [DR, NHP * OWN], BF16)
    sp_x1 = dint("sp_x1", [128, KT * OWN], F32)
    sp_wd = dint("sp_wd", [44, 128, D], BF16)
    sp_oA = dint("sp_oA", [128, NSLOT * NHP * 128], BF16)
    sp_oB = dint("sp_oB", [128, NSLOT * NHP * 128], BF16)

    sch = Sched(nc)

    def mm(out, lhsT, rhs, start, stop):
        return sch.op("tensor", lambda h, o=out, l=lhsT, r=rhs, s=start, e=stop:
                      h.matmul(o, l, r, start=s, stop=e))

    with contextlib.ExitStack() as root:
        sch.setup(root)

        def sb(st, name, shape, dt, side=None):
            if side is None:
                return st.enter_context(nc.sbuf_tensor(name, shape, dt))
            return st.enter_context(nc.sbuf_tensor(name, shape, dt, side=side))

        c_idb = sb(root, "c_idb", [128, 128], BF16)
        c_idf = sb(root, "c_idf", [128, 128], F32)
        c_colsb = sb(root, "c_colsb", [128, 2], BF16)
        c_colsf = sb(root, "c_colsf", [128, 3], F32)
        c_rowb = sb(root, "c_rowb", [1, 128], BF16)
        c_rowf = sb(root, "c_rowf", [1, 128], F32)
        c_shf = sb(root, "c_shf", [128, 2], I16)
        c_shfull = sb(root, "c_shfull", [128, 512], I16)
        sums_acc = sb(root, "sums_acc", [128, NSLOT * NHP], F32)
        bfc = [sb(root, f"bfc{i}", [128, 512], BF16) for i in range(2)]
        rsum_sb = sb(root, "rsum_sb", [1, NHP * 128], F32)

        inv2048_b = c_colsb[:, 0:1]
        inv2048_f = c_colsf[:, 0:1]
        nc.const_aps.register(F32, EPS_LN, c_colsf[:, 2:3])
        ones_row_b = c_rowb[:, :]
        ones_row_f = c_rowf[:, :]
        shifts = c_shf[:, 0:1]
        shifts3 = c_shf[:, 1:2]

        PS = [root.enter_context(nc.psum_tensor(f"ps{i}", [128, 512], F32))
              for i in range(6)]
        PTb = [root.enter_context(nc.psum_tensor(f"pt{i}", [128, 1024], BF16))
               for i in range(2)]
        PTf = [p.ap().bitcast(F32) for p in PTb]

        sch.dma("sync", c_idb[:], identb_d, sem="consts")
        sch.dma("sync", c_idf[:], identf_d, sem="consts")
        sch.dma("sync", c_colsb[:], colsb_d, sem="consts")
        sch.dma("sync", c_colsf[:], colsf_d, sem="consts")
        sch.dma("sync", c_rowb[:], rowb_d, sem="consts")
        sch.dma("sync", c_rowf[:], rowf_d, sem="consts")
        sch.dma("sync", c_shf[:], shf_d, sem="consts")
        sch.dma("sync", c_shfull[:], shfull_d, sem="consts")
        for e in ("tensor", "vector", "scalar", "gpsimd", "sync"):
            sch.dwait(e, "consts")

        def stats_rows(get_tile, sqbufs, rows_m, rows_r, rows_b, width=512, src_f32=False):
            sch.wait("scalar", "vector", sch.cnt["vector"])
            sch.wait("tensor", "vector", sch.cnt["vector"])
            for half in range(2):
                mrow = (PS[0] if half == 0 else PS[1])[0:1, 0:width]
                s2row = (PTf[0] if half == 0 else PTf[1])[0:1, 0:width]
                sq_guard = [0, 0]
                bf_guard = [0, 0]
                for k in range(KT):
                    xt = get_tile(k, half)
                    par = k % 2
                    sch.wait("scalar", "tensor", sq_guard[par])
                    c_sq = sch.op("scalar", lambda h, o=sqbufs[par][:, 0:width], i=xt:
                                  h.activation(o, i, AF.Square))
                    if src_f32:
                        btile = bfc[par][:, 0:width]
                        sch.wait("vector", "tensor", bf_guard[par])
                        vb = sch.op("vector", lambda h, o=btile, i=xt: h.tensor_copy(o, i))
                        sch.wait("tensor", "vector", vb)
                        mm(mrow, inv2048_b, btile, k == 0, k == KT - 1)
                        bf_guard[par] = sch.cnt["tensor"]
                    else:
                        mm(mrow, inv2048_b, xt, k == 0, k == KT - 1)
                    sch.wait("tensor", "scalar", c_sq)
                    mm(s2row, inv2048_b, sqbufs[par][:, 0:width], k == 0, k == KT - 1)
                    sq_guard[par] = sch.cnt["tensor"]
                sch.wait("vector", "tensor", sch.cnt["tensor"])
                m_f = rows_m[0:1, half * width: (half + 1) * width]
                m_b = rows_b[0:1, half * width: (half + 1) * width]
                r_f = rows_r[0:1, half * width: (half + 1) * width]
                sch.op("vector", lambda h, o=m_f, i=mrow: h.tensor_copy(o, i))
                sch.op("vector", lambda h, o=m_b, i=mrow: h.tensor_copy(o, i))
                sch.op("vector", lambda h, o=r_f, a=mrow, b=m_f:
                       h.tensor_tensor(o, a, b, OP.mult))
                sch.op("vector", lambda h, o=r_f, a=r_f, b=s2row:
                       h.scalar_tensor_tensor(o, a, -1.0, b, OP.mult, OP.add))
                sch.wait("scalar", "vector", sch.cnt["vector"])
                c_sd = sch.op("scalar", lambda h, o=r_f, i=r_f:
                              h.activation(o, i, AF.Sqrt, bias=EPS_LN))
                sch.wait("vector", "scalar", c_sd)
                sch.op("vector", lambda h, o=r_f, i=r_f: h.reciprocal(o, i))
            return sch.cnt["vector"]

        def ln_apply(get_in, get_out, rows_r, rows_b, tmp2, extra_res=None):
            for half in range(2):
                sch.wait("tensor", "vector", sch.cnt["vector"])
                mm(PS[2][:, 0:512], ones_row_b,
                   rows_b[0:1, half * 512: half * 512 + 512], True, True)
                tb = mm(PS[3][:, 0:512], ones_row_f,
                        rows_r[0:1, half * 512: half * 512 + 512], True, True)
                sch.wait("vector", "tensor", tb)
                for k in range(KT):
                    it = get_in(k, half)
                    ot = get_out(k, half)
                    tf = tmp2[k % 2][:]
                    sch.op("vector", lambda h, o=tf, a=it, b=PS[2][:, 0:512]:
                           h.tensor_tensor(o, a, b, OP.subtract))
                    if extra_res is None:
                        sch.op("vector", lambda h, o=ot, a=tf, b=PS[3][:, 0:512]:
                               h.tensor_tensor(o, a, b, OP.mult))
                    else:
                        rt = extra_res(k, half)
                        sch.op("vector", lambda h, o=tf, a=tf, b=PS[3][:, 0:512]:
                               h.tensor_tensor(o, a, b, OP.mult))
                        sch.op("vector", lambda h, o=ot, a=rt, b=tf:
                               h.tensor_tensor(o, a, b, OP.add))
            return sch.cnt["vector"]

        stack_q = contextlib.ExitStack()
        qnA = sb(stack_q, "qnA", [128, NHP * OWN], BF16, side="right")
        qpA = sb(stack_q, "qpA", [DR, NHP * OWN], BF16, side="right")

        with contextlib.ExitStack() as ph0:
            xq = sb(ph0, "xq", [128, KT * OWN], BF16)
            sqb = [sb(ph0, f"ph0sq{i}", [128, 512], BF16) for i in range(2)]
            rows0m = sb(ph0, "ph0rowm", [1, OWN], F32)
            rows0r = sb(ph0, "ph0rowr", [1, OWN], F32)
            rows0_b = sb(ph0, "ph0rows_b", [1, OWN], BF16)
            tmpf = [sb(ph0, f"ph0tmp{i}", [128, 512], F32) for i in range(2)]
            w_qa = sb(ph0, "w_qa", [128, KT * QL], BF16)
            w_qbn = sb(ph0, "w_qbn", [128, 4 * NH * DN], BF16)
            w_qbp = sb(ph0, "w_qbp", [128, 4 * NH * DR], BF16)
            w_qbps = sb(ph0, "w_qbps", [128, 4 * NH * DR], BF16)
            qa_sb = sb(ph0, "qa_sb", [128, 4 * OWN], BF16)
            qnB = sb(ph0, "qnB", [128, NHP * OWN], BF16)
            qpB = sb(ph0, "qpB", [DR, NHP * OWN], BF16)
            rq_c = sb(ph0, "rq_c", [DR, OWN], F32)
            rq_s = sb(ph0, "rq_s", [DR, OWN], F32)
            rop = [sb(ph0, f"ph0rop{i}", [DR, 512], F32) for i in range(2)]

            sch.dma("gpsimd", xq[:].rearrange("p (k t) -> p k t", k=KT), xqT.rearrange("(k p) t -> p k t", p=128), sem="ph0w")
            sch.dma("gpsimd", w_qa[:].rearrange("p (k t) -> p k t", k=KT), wqa_d.rearrange("(k p) t -> p k t", p=128), sem="ph0w")
            sch.dma("gpsimd", w_qbn[:].rearrange("p (k t) -> p k t", k=4), wqbn_d.rearrange("(k p) t -> p k t", p=128), sem="ph0w")
            sch.dma("gpsimd", w_qbp[:].rearrange("p (k t) -> p k t", k=4), wqbp_d.rearrange("(k p) t -> p k t", p=128), sem="ph0w")
            sch.dma("gpsimd", w_qbps[:].rearrange("p (k t) -> p k t", k=4), wqbps_d.rearrange("(k p) t -> p k t", p=128), sem="ph0w")
            sch.dma("sync", rq_c[:], ropeq_c, sem="ropeq")
            sch.dma("sync", rq_s[:], ropeq_s, sem="ropeq")
            sch.dwait("scalar", "ph0w")
            sch.dwait("tensor", "ph0w")
            sch.dwait("vector", "ropeq")

            stats_rows(lambda k, half: xq[:, k * OWN + half * 512:
                                          k * OWN + half * 512 + 512],
                       sqb, rows0m, rows0r, rows0_b)
            v_ln1 = ln_apply(
                lambda k, half: xq[:, k * OWN + half * 512: k * OWN + half * 512 + 512],
                lambda k, half: xq[:, k * OWN + half * 512: k * OWN + half * 512 + 512],
                rows0r, rows0_b, tmpf)

            prev_ev = v_ln1
            for m in range(4):
                for half in range(2):
                    ps = PS[(m * 2 + half) % 2]
                    sch.wait("tensor", "vector", prev_ev)
                    for k in range(KT):
                        lt = w_qa[:, k * QL + m * 128: k * QL + m * 128 + 128]
                        rt = xq[:, k * OWN + half * 512: k * OWN + half * 512 + 512]
                        mm(ps[:, 0:512], lt, rt, k == 0, k == KT - 1)
                    sch.wait("vector", "tensor", sch.cnt["tensor"])
                    dst = qa_sb[:, m * OWN + half * 512: m * OWN + half * 512 + 512]
                    prev_ev = sch.op("vector", lambda h, o=dst, i=ps[:, 0:512]:
                                     h.tensor_copy(o, i))

            prev_v = prev_ev
            for h_ in range(NH):
                qn_dst = qnA if h_ < NHP else qnB
                qp_dst = qpA if h_ < NHP else qpB
                hh = h_ % NHP
                for half in range(2):
                    sl = slice(hh * OWN + half * 512, hh * OWN + half * 512 + 512)
                    sch.wait("tensor", "vector", prev_v)
                    for k in range(4):
                        lt = w_qbn[:, k * NH * DN + h_ * DN: k * NH * DN + h_ * DN + 128]
                        rt = qa_sb[:, k * OWN + half * 512: k * OWN + half * 512 + 512]
                        mm(PS[0][:, 0:512], lt, rt, k == 0, k == 3)
                    for k in range(4):
                        lt = w_qbp[:, k * NH * DR + h_ * DR: k * NH * DR + h_ * DR + DR]
                        rt = qa_sb[:, k * OWN + half * 512: k * OWN + half * 512 + 512]
                        mm(PS[2][0:DR, 0:512], lt, rt, k == 0, k == 3)
                    for k in range(4):
                        lt = w_qbps[:, k * NH * DR + h_ * DR: k * NH * DR + h_ * DR + DR]
                        rt = qa_sb[:, k * OWN + half * 512: k * OWN + half * 512 + 512]
                        mm(PS[3][0:DR, 0:512], lt, rt, k == 0, k == 3)
                    sch.wait("vector", "tensor", sch.cnt["tensor"])
                    sch.op("vector", lambda h, o=qn_dst[:, sl], i=PS[0][:, 0:512]:
                           h.tensor_copy(o, i))
                    cs = rq_c[:, half * 512: half * 512 + 512]
                    sn = rq_s[:, half * 512: half * 512 + 512]
                    sch.op("vector", lambda h, o=rop[0][:], a=PS[3][0:DR, 0:512], b=sn:
                           h.tensor_tensor(o, a, b, OP.mult))
                    sch.op("vector", lambda h, o=rop[1][:], a=PS[2][0:DR, 0:512], b=cs:
                           h.tensor_tensor(o, a, b, OP.mult))
                    prev_v = sch.op("vector", lambda h, o=qp_dst[0:DR, sl], a=rop[0][:],
                                    b=rop[1][:]: h.tensor_tensor(o, a, b, OP.add))

            sch.wait("gpsimd", "vector", sch.cnt["vector"])
            sch.dma("gpsimd", sp_qnB, qnB[:], sem="qspill")
            sch.dma("gpsimd", sp_qpB, qpB[:], sem="qspill")
            sch.barrier()

        def attn_pass(pass_id, stack, qn, qp, odst):
            if pass_id == 0:
                xc = sb(stack, "xc", [128, KT * CH], BF16)
                sqb = [sb(stack, f"a0sq{i}", [128, 512], BF16) for i in range(2)]
                w_kva = sb(stack, "w_kva", [128, KT * 640], BF16)
                w_kbk = sb(stack, "w_kbk", [128, 4 * NH * DN], BF16)
                w_kbv = sb(stack, "w_kbv", [128, 4 * NH * DN], BF16)
                cT = sb(stack, "cT", [128, 4 * CH], BF16)
                nw1 = sb(stack, "nw1", [1, 640], BF16)
                rk_c = sb(stack, "rk_c", [DR, CH], F32)
                rk_s = sb(stack, "rk_s", [DR, CH], F32)
                crowm = sb(stack, "crowm", [1, CH], F32)
                crowr = sb(stack, "crowr", [1, CH], F32)
                crow_b = sb(stack, "crow_b", [1, CH], BF16)
                rbc = sb(stack, "rbc", [128, 512], F32)
                ropt = [sb(stack, f"a0rp{i}", [DR, 512], F32) for i in range(2)]
                kn_sb = [sb(stack, "kn_sb", [128, NH * CH], BF16)]
                v_sb = [sb(stack, "v_sb", [128, 4 * NH * 128], BF16)]
                kpe_sb = [sb(stack, "kpe_sb", [DR, CH], BF16)]
                nbuf = 1
            else:
                kn_sb = [sb(stack, f"kn_sb{i}", [128, NH * CH], BF16) for i in range(2)]
                v_sb = [sb(stack, f"v_sb{i}", [128, 4 * NH * 128], BF16) for i in range(2)]
                kpe_sb = [sb(stack, f"kpe_sb{i}", [DR, CH], BF16) for i in range(2)]
                nbuf = 2

            mk_sb = [sb(stack, f"mk{pass_id}_{i}", [128, 512], BF16) for i in range(2)]
            p_sb = [sb(stack, f"p{pass_id}_{i}", [128, 512], BF16) for i in range(2)]
            pt_sb = [sb(stack, f"pv{pass_id}_{i}", [128, 512], BF16) for i in range(2)]
            oacc = sb(stack, f"oacc{pass_id}", [128, NSLOT * NHP * 128], F32)
            sums_vis = [sb(stack, f"sumsv{pass_id}{i}", [128, NHP], F32)
                        for i in range(2)]
            onorm = [sb(stack, f"onorm{pass_id}_{i}", [128, NHP * 128], BF16)
                     for i in range(2)]

            if pass_id == 0:
                sch.dma("gpsimd", w_kva[:].rearrange("p (k t) -> p k t", k=KT), wkva_d.rearrange("(k p) t -> p k t", p=128), sem="kvaw")
                sch.dma("gpsimd", w_kbk[:].rearrange("p (k t) -> p k t", k=4), wkvbk_d.rearrange("(k p) t -> p k t", p=128), sem="kvaw")
                sch.dma("gpsimd", w_kbv[:].rearrange("p (k t) -> p k t", k=4), wkvbv_d.rearrange("(k p) t -> p k t", p=128), sem="kvaw")
                sch.dwait("tensor", "kvaw")
                ones_col_b = c_colsb[:, 1:2]
                for (n0, n1, ps) in ((0, 512, PS[0]), (512, 640, PS[1])):
                    for k in range(KT):
                        mm(ps[0:1, 0:n1 - n0], ones_col_b,
                           w_kva[:, k * 640 + n0: k * 640 + n1], k == 0, k == KT - 1)
                sch.wait("vector", "tensor", sch.cnt["tensor"])
                sch.op("vector", lambda h: h.tensor_scalar(
                    nw1[0:1, 0:512], PS[0][0:1, 0:512], -1.0, None, OP.mult, OP.bypass))
                sch.op("vector", lambda h: h.tensor_scalar(
                    nw1[0:1, 512:640], PS[1][0:1, 0:128], -1.0, None, OP.mult, OP.bypass))
                sch.wait("tensor", "vector", sch.cnt["vector"])

            mask_ld = {}
            kvload = [0] * (NCH + 1)
            if pass_id == 0:
                sch.dma("gpsimd", xc[:].rearrange("p (k t) -> p k t", k=KT), xT[:, 0:CH].rearrange("(k p) t -> p k t", p=128), sem="xc")
                sch.dma("sync", rk_c[:], ropek_c[:, 0:CH], sem="ropek")
                sch.dma("sync", rk_s[:], ropek_s[:, 0:CH], sem="ropek")
            else:
                sch.dwait("gpsimd", "spill")
                sch.dma("gpsimd", kn_sb[0][:], sp_kn[0], sem="kv0")
                sch.dma("gpsimd", v_sb[0][:], sp_v[0], sem="kv0")
                sch.dma("gpsimd", kpe_sb[0][:], sp_kpe[0], sem="kv0")
                kvload[0] = ("kv0", sch.dcnt["kv0"])
            mname = f"mask{pass_id}"
            sch.dma("sync", mk_sb[0][:], masks_d[0], sem=mname + "0")
            mask_ld[(0, 0)] = (mname + "0", sch.dcnt[mname + "0"])
            sch.dma("sync", mk_sb[1][:], masks_d[1], sem=mname + "1")
            mask_ld[(0, 1)] = (mname + "1", sch.dcnt[mname + "1"])

            s_banks = (PS[0], PS[1])
            o_pairs = ((PS[2], PS[3]), (PS[4], PS[5]))
            exp_done = [0, 0]
            ptevac_done = [0, 0]
            pmm_done = [0, 0]
            oadd_done = [0, 0]
            sums_read = 0
            sums_read_prev = 0
            opair_idx = 0
            kvspill_done = 0
            xload_g = sch.dcnt.get("xc", 0)
            rload_s = sch.dcnt.get("ropek", 0)

            for v in range(NCH):
                buf = v % nbuf
                if pass_id == 0:
                    sch.dwait("scalar", "xc", xload_g)
                    sch.dwait("tensor", "xc", xload_g)
                    sch.wait("tensor", "scalar", max(exp_done))
                    sch.dwait("vector", "spill", kvspill_done)
                    sq_guard = [0, 0]
                    for k in range(KT):
                        xt = xc[:, k * CH: k * CH + 512]
                        par = k % 2
                        sch.wait("scalar", "tensor", sq_guard[par])
                        c_sq = sch.op("scalar", lambda h, o=sqb[par][:], i=xt:
                                      h.activation(o, i, AF.Square))
                        mm(PS[0][0:1, 0:512], inv2048_b, xt, k == 0, k == KT - 1)
                        sch.wait("tensor", "scalar", c_sq)
                        mm(PS[1][0:1, 0:512], inv2048_b, sqb[par][:], k == 0, k == KT - 1)
                        sq_guard[par] = sch.cnt["tensor"]
                    sch.wait("vector", "tensor", sch.cnt["tensor"])
                    sch.op("vector", lambda h: h.tensor_copy(crowm[0:1, :], PS[0][0:1, 0:512]))
                    sch.op("vector", lambda h: h.tensor_copy(crow_b[0:1, :], PS[0][0:1, 0:512]))
                    sch.op("vector", lambda h: h.tensor_tensor(
                        crowr[0:1, :], PS[0][0:1, 0:512], crowm[0:1, :], OP.mult))
                    sch.op("vector", lambda h: h.scalar_tensor_tensor(
                        crowr[0:1, :], crowr[0:1, :], -1.0, PS[1][0:1, 0:512], OP.mult, OP.add))
                    sch.wait("scalar", "vector", sch.cnt["vector"])
                    c_sd = sch.op("scalar", lambda h: h.activation(
                        crowr[0:1, :], crowr[0:1, :], AF.Sqrt, bias=EPS_LN))
                    sch.wait("vector", "scalar", c_sd)
                    sch.op("vector", lambda h: h.reciprocal(crowr[0:1, :], crowr[0:1, :]))
                    v_r = sch.cnt["vector"]
                    sch.wait("tensor", "vector", v_r)
                    trb = mm(PS[2][:, 0:512], ones_row_f, crowr[0:1, :], True, True)
                    sch.wait("vector", "tensor", trb)
                    v_rbc = sch.op("vector", lambda h: h.tensor_copy(rbc[:], PS[2][:, 0:512]))

                    mt = [(0, 128), (128, 128), (256, 128), (384, 128),
                          (512, 64), (576, 64)]
                    banks = (PS[3], PS[4], PS[5], PS[3], PS[4], PS[5])
                    evac_cnt = [0, 0, 0]
                    pe_ps = {}
                    for mi, (c0, M) in enumerate(mt):
                        ps = banks[mi]
                        if mi >= 3:
                            sch.wait("tensor", "vector", evac_cnt[mi % 3])
                        if mi == 0:
                            sch.wait("tensor", "vector", v_rbc)
                        for k in range(KT):
                            lt = w_kva[:, k * 640 + c0: k * 640 + c0 + M]
                            rt = xc[:, k * CH: k * CH + 512]
                            mm(ps[0:M, 0:512], lt, rt, k == 0, False)
                        tmm = mm(ps[0:M, 0:512], nw1[0:1, c0:c0 + M],
                                 crow_b[0:1, :], False, True)
                        sch.wait("vector", "tensor", tmm)
                        if mi < 4:
                            dst = cT[:, mi * CH: mi * CH + 512]
                            evac_cnt[mi % 3] = sch.op(
                                "vector", lambda h, o=dst, a=ps[:, 0:512], b=rbc[:]:
                                h.tensor_tensor(o, a, b, OP.mult))
                        else:
                            pe_ps[mi] = ps
                    sch.dwait("vector", "ropek", rload_s)
                    sch.op("vector", lambda h: h.tensor_tensor(
                        ropt[0][:], pe_ps[5][0:DR, 0:512], rk_s[:], OP.mult))
                    sch.op("vector", lambda h: h.tensor_tensor(
                        ropt[1][:], pe_ps[4][0:DR, 0:512], rk_c[:], OP.mult))
                    sch.op("vector", lambda h: h.tensor_tensor(
                        ropt[0][:], ropt[0][:], ropt[1][:], OP.add))
                    sch.op("vector", lambda h, o=kpe_sb[buf][:]: h.tensor_tensor(
                        o, ropt[0][:], rbc[0:DR, 0:512], OP.mult))
                    v_ckv = sch.cnt["vector"]

                    if v + 1 < NCH:
                        sch.wait("gpsimd", "tensor", sch.cnt["tensor"])
                        sch.wait("gpsimd", "scalar", sch.cnt["scalar"])
                        sch.dma("gpsimd", xc[:].rearrange("p (k t) -> p k t", k=KT),
                                xT[:, (v + 1) * CH: (v + 2) * CH].rearrange(
                                    "(k p) t -> p k t", p=128), sem="xc")
                        xload_g = sch.dcnt["xc"]
                        sch.wait("sync", "vector", v_ckv)
                        sch.dma("sync", rk_c[:], ropek_c[:, (v + 1) * CH: (v + 2) * CH], sem="ropek")
                        sch.dma("sync", rk_s[:], ropek_s[:, (v + 1) * CH: (v + 2) * CH], sem="ropek")
                        rload_s = sch.dcnt["ropek"]

                    sch.wait("tensor", "vector", v_ckv)
                    ev = [0, 0]
                    for h_ in range(NH):
                        ps = s_banks[h_ % 2]
                        sch.wait("tensor", "vector", ev[h_ % 2])
                        for k in range(4):
                            lt = w_kbk[:, k * NH * DN + h_ * DN: k * NH * DN + h_ * DN + 128]
                            rt = cT[:, k * CH: k * CH + 512]
                            mm(ps[:, 0:512], lt, rt, k == 0, k == 3)
                        sch.wait("vector", "tensor", sch.cnt["tensor"])
                        dst = kn_sb[buf][:, h_ * CH: h_ * CH + 512]
                        ev[h_ % 2] = sch.op("vector", lambda h, o=dst, i=ps[:, 0:512]:
                                            h.tensor_copy(o, i))
                    evv = [0, 0, 0, 0]
                    for j in range(4):
                        for n in range(4):
                            bi = (j * 4 + n) % 4
                            ps = (PS[2], PS[3], PS[4], PS[5])[bi]
                            sch.wait("tensor", "vector", evv[bi])
                            for k in range(4):
                                lt = cT[:, k * CH + j * 128: k * CH + j * 128 + 128]
                                rt = w_kbv[:, k * NH * DN + n * 512:
                                           k * NH * DN + n * 512 + 512]
                                mm(ps[:, 0:512], lt, rt, k == 0, k == 3)
                            sch.wait("vector", "tensor", sch.cnt["tensor"])
                            dst = v_sb[buf][:, j * NH * 128 + n * 512:
                                            j * NH * 128 + n * 512 + 512]
                            evv[bi] = sch.op("vector", lambda h, o=dst, i=ps[:, 0:512]:
                                             h.tensor_copy(o, i))
                    v_kv = sch.cnt["vector"]
                    sch.wait("gpsimd", "vector", v_kv)
                    sch.dma("gpsimd", sp_kn[v], kn_sb[buf][:], sem="spill")
                    sch.dma("gpsimd", sp_v[v], v_sb[buf][:], sem="spill")
                    sch.dma("gpsimd", sp_kpe[v], kpe_sb[buf][:], sem="spill")
                    kvspill_done = sch.dcnt["spill"]
                    sch.wait("tensor", "vector", v_kv)
                    exp_done = [0, 0]
                else:
                    if v + 1 < NCH:
                        sch.wait("gpsimd", "tensor", sch.cnt["tensor"])
                        kb = f"kv{1 - buf}"
                        sch.dma("gpsimd", kn_sb[1 - buf][:], sp_kn[v + 1], sem=kb)
                        sch.dma("gpsimd", v_sb[1 - buf][:], sp_v[v + 1], sem=kb)
                        sch.dma("gpsimd", kpe_sb[1 - buf][:], sp_kpe[v + 1], sem=kb)
                        kvload[v + 1] = (kb, sch.dcnt[kb])
                    sch.dwait("tensor", kvload[v][0], kvload[v][1])

                for t in range(v // 2, NSLOT):
                    maskable = (t == v // 2)
                    mk_idx = v - 2 * t
                    first_visit = (v == 0)
                    pid = opair_idx % 2
                    op_pair = o_pairs[pid]
                    sch.wait("tensor", "vector", oadd_done[pid])
                    if maskable:
                        sch.dwait("tensor", mask_ld[(t, mk_idx)][0], mask_ld[(t, mk_idx)][1])
                    sv = sums_vis[opair_idx % 2]
                    sch.wait("scalar", "vector", sums_read_prev)

                    def emit_s(h_):
                        sbk = h_ % 2
                        ps = s_banks[sbk]
                        sch.wait("tensor", "scalar", exp_done[sbk])
                        hg = h_ + pass_id * NHP
                        lt1 = qn[:, h_ * OWN + t * 128: h_ * OWN + t * 128 + 128]
                        rt1 = kn_sb[buf][:, hg * CH: hg * CH + 512]
                        mm(ps[:, 0:512], lt1, rt1, True, False)
                        lt2 = qp[0:DR, h_ * OWN + t * 128: h_ * OWN + t * 128 + 128]
                        rt2 = kpe_sb[buf][0:DR, 0:512]
                        mm(ps[:, 0:512], lt2, rt2, False, not maskable)
                        if maskable:
                            mm(ps[:, 0:512], c_idb[:], mk_sb[mk_idx][:], False, True)
                        s_done[sbk] = sch.cnt["tensor"]

                    s_done = [0, 0]
                    pt_ready = [0, 0]

                    def emit_pv(h_):
                        sbk = h_ % 2
                        hg = h_ + pass_id * NHP
                        sch.wait("tensor", "vector", pt_ready[sbk])
                        opsl = op_pair[h_ // 4]
                        col = (h_ % 4) * 128
                        for j in range(4):
                            lt = v_sb[buf][:, j * NH * 128 + hg * 128:
                                           j * NH * 128 + hg * 128 + 128]
                            rt = pt_sb[sbk][:, j * 128: j * 128 + 128]
                            mm(opsl[:, col: col + 128], lt, rt,
                               (h_ % 4 == 0) and (j == 0), (h_ % 4 == 3) and (j == 3))
                        pv_done[sbk] = sch.cnt["tensor"]

                    pv_done = [0, 0]
                    emit_s(0)
                    for h_ in range(NHP):
                        sbk = h_ % 2
                        ps = s_banks[sbk]
                        sch.wait("scalar", "tensor", s_done[sbk])
                        c_e = sch.op("scalar", lambda h, o=p_sb[sbk][:], i=ps[:, 0:512],
                                     a=sv[:, h_: h_ + 1]:
                                     h.activation(o, i, AF.Exp, scale=SCALE, accum_out=a))
                        exp_done[sbk] = c_e
                        if h_ + 1 < NHP:
                            emit_s(h_ + 1)
                        sch.wait("tensor", "scalar", c_e)
                        sch.wait("tensor", "vector", ptevac_done[sbk])
                        for j in range(4):
                            sch.op("tensor", lambda h, o=PTb[sbk][:, j * 128: j * 128 + 128],
                                   i=p_sb[sbk][:, j * 128: j * 128 + 128], idm=c_idb[:]:
                                   h.transpose(o, i, idm))
                        pmm_done[sbk] = sch.cnt["tensor"]
                        if h_ >= 1:
                            emit_pv(h_ - 1)
                        sch.wait("vector", "tensor", pmm_done[sbk])
                        sch.wait("vector", "tensor", pv_done[sbk])
                        c_pt = sch.op("vector", lambda h, o=pt_sb[sbk][:],
                                      i=PTb[sbk][:, 0:512]: h.tensor_copy(o, i))
                        ptevac_done[sbk] = c_pt
                        pt_ready[sbk] = c_pt
                    emit_pv(NHP - 1)
                    t_pv = sch.cnt["tensor"]
                    sch.wait("vector", "tensor", t_pv)
                    sch.wait("vector", "scalar", max(exp_done))
                    for pi in range(2):
                        dst = oacc[:, t * NHP * 128 + pi * 512:
                                   t * NHP * 128 + pi * 512 + 512]
                        src = op_pair[pi][:, 0:512]
                        if first_visit:
                            sch.op("vector", lambda h, o=dst, i=src: h.tensor_copy(o, i))
                        else:
                            sch.op("vector", lambda h, o=dst, a=dst, b=src:
                                   h.tensor_tensor(o, a, b, OP.add))
                    ssl = sums_acc[:, t * NHP: (t + 1) * NHP]
                    if first_visit:
                        sch.op("vector", lambda h, o=ssl, i=sv[:]: h.tensor_copy(o, i))
                    else:
                        sch.op("vector", lambda h, o=ssl, a=ssl, b=sv[:]:
                               h.tensor_tensor(o, a, b, OP.add))
                    sums_read_prev, sums_read = sums_read, sch.cnt["vector"]
                    oadd_done[pid] = sch.cnt["vector"]
                    opair_idx += 1

                if v + 1 < NCH:
                    nt = (v + 1) // 2
                    nidx = (v + 1) - 2 * nt
                    sch.wait("sync", "tensor", sch.cnt["tensor"])
                    mn = f"mask{pass_id}{nidx}"
                    sch.dma("sync", mk_sb[nidx][:], masks_d[2 * nt + nidx], sem=mn)
                    mask_ld[(nt, nidx)] = (mn, sch.dcnt[mn])

            sch.wait("tensor", "vector", sch.cnt["vector"])
            sch.wait("tensor", "scalar", sch.cnt["scalar"])
            ospill = [0, 0]
            for t in range(NSLOT):
                ob = t % 2
                for h_ in range(NHP):
                    ptv = PTf[h_ // 4]
                    sch.op("tensor", lambda h, i=sums_acc[:, t * NHP + h_:
                                                          t * NHP + h_ + 1],
                           o=ptv[0:1, (h_ % 4) * 128: (h_ % 4) * 128 + 128]:
                           h.transpose(o, i, c_idf[:]))
                sch.wait("vector", "tensor", sch.cnt["tensor"])
                sch.op("vector", lambda h: h.reciprocal(
                    rsum_sb[0:1, 0:512], PTf[0][0:1, 0:512]))
                vv = sch.op("vector", lambda h: h.reciprocal(
                    rsum_sb[0:1, 512:1024], PTf[1][0:1, 0:512]))
                sch.dwait("vector", f"onorm{pass_id}{ob}", ospill[ob])
                for hb in range(2):
                    sch.wait("tensor", "vector", vv)
                    for hh in range(4):
                        h_ = hb * 4 + hh
                        mm(PS[0][:, hh * 128: hh * 128 + 128], ones_row_f,
                           rsum_sb[0:1, h_ * 128: h_ * 128 + 128], True, True)
                    sch.wait("vector", "tensor", sch.cnt["tensor"])
                    off = t * NHP * 128 + hb * 512
                    vv = sch.op("vector", lambda h, o=onorm[ob][:, hb * 512: hb * 512 + 512],
                                a=oacc[:, off: off + 512], b=PS[0][:, 0:512]:
                                h.tensor_tensor(o, a, b, OP.mult))
                sch.wait("gpsimd", "vector", vv)
                sch.dma("gpsimd", odst[:, t * NHP * 128: (t + 1) * NHP * 128],
                        onorm[ob][:], sem=f"onorm{pass_id}{ob}")
                ospill[ob] = sch.dcnt[f"onorm{pass_id}{ob}"]
            sch.barrier()

        with contextlib.ExitStack() as pa:
            attn_pass(0, pa, qnA[:], qpA[:], sp_oA)
        with contextlib.ExitStack() as pb:
            sch.dwait("gpsimd", "qspill")
            sch.dma("gpsimd", qnA[:], sp_qnB, sem="qload")
            sch.dma("gpsimd", qpA[:], sp_qpB, sem="qload")
            sch.dwait("tensor", "qload")
            attn_pass(1, pb, qnA[:], qpA[:], sp_oB)
        stack_q.close()

        stack_ln2 = contextlib.ExitStack()
        ln2 = sb(stack_ln2, "ln2", [128, KT * OWN], BF16, side="right")

        with contextlib.ExitStack() as p3:
            xo = sb(p3, "xo", [128, KT * OWN], F32)
            ab = sb(p3, "ab", [128, KT * OWN], BF16)
            wo_sb = [sb(p3, f"wo_sb{i}", [128, KT * 128], BF16) for i in range(2)]
            sqb3 = [sb(p3, f"p3sq{i}", [128, 512], BF16) for i in range(2)]
            rows3m = sb(p3, "p3rowm", [1, OWN], F32)
            rows3r = sb(p3, "p3rowr", [1, OWN], F32)
            rows3_b = sb(p3, "p3rows_b", [1, OWN], BF16)
            tmp3 = [sb(p3, f"p3tmp{i}", [128, 512], F32) for i in range(2)]

            oA = sb(p3, "oA3", [128, NSLOT * NHP * 128], BF16)
            oB = sb(p3, "oB3", [128, NSLOT * NHP * 128], BF16)
            sch.dma("sync", oA[:], sp_oA, sem="p3ld")
            sch.dma("sync", oB[:], sp_oB, sem="p3ld")
            sch.dma("sync", xo[:].rearrange("p (k t) -> p k t", k=KT), xqT.rearrange("(k p) t -> p k t", p=128), sem="p3ld")
            sch.dwait("tensor", "p3ld")
            sch.dma("gpsimd", wo_sb[0][:].rearrange("p (k t) -> p k t", k=KT),
                    wo_d[:, 0:128].rearrange("(k p) m2 -> p k m2", p=128), sem="wo0")
            wo_ld = [16, 0]

            oab = (oA, oB)
            prev_ev = 0
            for m in range(KT):
                if m + 1 < KT:
                    sch.wait("gpsimd", "tensor", sch.cnt["tensor"])
                    wn = f"wo{(m + 1) % 2}"
                    sch.dma("gpsimd", wo_sb[(m + 1) % 2][:].rearrange("p (k t) -> p k t", k=KT),
                            wo_d[:, (m + 1) * 128: (m + 2) * 128].rearrange(
                                "(k p) m2 -> p k m2", p=128), sem=wn)
                    wo_ld[(m + 1) % 2] = sch.dcnt[wn]
                ppair = (PS[2], PS[3]) if m % 2 == 0 else (PS[4], PS[5])
                sch.dwait("tensor", f"wo{m % 2}", wo_ld[m % 2])
                sch.wait("tensor", "vector", prev_ev)
                for half in range(2):
                    ps = ppair[half]
                    for k in range(NH):
                        lt = wo_sb[m % 2][:, k * 128: k * 128 + 128]
                        src = oab[k // NHP]
                        hh = k % NHP
                        rt = src[:].rearrange("p (t x) -> p t x", t=NSLOT)[
                            :, half * 4: half * 4 + 4, hh * 128: hh * 128 + 128]
                        mm(ps[:, 0:512], lt, rt, k == 0, k == NH - 1)
                sch.wait("vector", "tensor", sch.cnt["tensor"])
                for half in range(2):
                    dst = ab[:, m * OWN + half * 512: m * OWN + half * 512 + 512]
                    prev_ev = sch.op("vector", lambda h, o=dst, i=ppair[half][:, 0:512]:
                                     h.tensor_copy(o, i))

            stats_rows(lambda k, half: ab[:, k * OWN + half * 512:
                                          k * OWN + half * 512 + 512],
                       sqb3, rows3m, rows3r, rows3_b)
            sch.dwait("vector", "p3ld")
            v_x1 = ln_apply(
                lambda k, half: ab[:, k * OWN + half * 512: k * OWN + half * 512 + 512],
                lambda k, half: xo[:, k * OWN + half * 512: k * OWN + half * 512 + 512],
                rows3r, rows3_b, tmp3,
                extra_res=lambda k, half: xo[:, k * OWN + half * 512:
                                             k * OWN + half * 512 + 512])
            for k in range(KT):
                for half in range(2):
                    xsl = xo[:, k * OWN + half * 512: k * OWN + half * 512 + 512]
                    bsl = ln2[:, k * OWN + half * 512: k * OWN + half * 512 + 512]
                    sch.op("vector", lambda h, o=bsl, i=xsl: h.tensor_copy(o, i))
            v_x1b = sch.cnt["vector"]
            sch.wait("gpsimd", "vector", v_x1)
            sch.dma("gpsimd", sp_x1, xo[:], sem="x1sp")

            sch.wait("tensor", "vector", v_x1b)
            stats_rows(lambda k, half: ln2[:, k * OWN + half * 512:
                                           k * OWN + half * 512 + 512],
                       sqb3, rows3m, rows3r, rows3_b)
            ln_apply(
                lambda k, half: ln2[:, k * OWN + half * 512: k * OWN + half * 512 + 512],
                lambda k, half: ln2[:, k * OWN + half * 512: k * OWN + half * 512 + 512],
                rows3r, rows3_b, tmp3)
            sch.barrier()

        stack_sg = contextlib.ExitStack()
        sgh = sb(stack_sg, "sgh", [128, 44 * OWN], BF16)

        with contextlib.ExitStack() as p4:
            wraw = [sb(p4, f"wraw{i}", [128, 512], I16) for i in range(2)]
            wstage = [sb(p4, f"wstage{i}", [128, 512], BF16) for i in range(2)]
            wsc = [sb(p4, f"wsc{i}", [128, 512], F16) for i in range(2)]
            dq = [sb(p4, f"dq{i}", [128, 512], I16) for i in range(3)]
            wsl = [sb(p4, f"wsl{i}", [128, 16 * 512], BF16) for i in range(2)]

            NSL = 22
            wsl_free_t = [0, 0]
            wsl_ready = [0, 0]
            raw_free = [0, 0]
            raw_ld = [0, 0]
            evac_v = [0, 0, 0, 0]
            evac_s = [0, 0, 0, 0]
            for s in range(NSL):
                cur = s % 2
                sch.wait("vector", "tensor", wsl_free_t[cur])
                for g in range(16):
                    par = g % 2
                    sch.wait("gpsimd", "vector", raw_free[par])
                    rn = f"raw{par}"
                    sch.dma("gpsimd", wraw[par][:],
                            gu_rep[g * 128: (g + 1) * 128, s * 512: (s + 1) * 512], sem=rn)
                    sch.dma("gpsimd", wsc[par][:],
                            gu_sc_d[g: g + 1, s * 512: (s + 1) * 512].broadcast_to([128, 512]), sem=rn)
                    raw_ld[par] = sch.dcnt[rn]
                    sch.dwait("vector", rn, raw_ld[par])
                    sch.op("vector", lambda h, w=wraw[par][:]: h.tensor_tensor(
                        dq[0][:], w, c_shfull[:], OP.logical_shift_right))
                    raw_free[par] = sch.cnt["vector"]
                    sch.op("vector", lambda h: h.tensor_scalar(
                        dq[1][:], dq[0][:], 3, 1, OP.logical_shift_right, OP.bitwise_and))
                    sch.op("vector", lambda h: h.tensor_scalar(
                        dq[0][:], dq[0][:], 7, None, OP.bitwise_and, OP.bypass))
                    sch.op("vector", lambda h: h.tensor_scalar(
                        dq[2][:], dq[0][:], 252.0, None, OP.add, OP.bypass))
                    sch.op("vector", lambda h: h.scalar_tensor_tensor(
                        dq[0][:], dq[0][:], 252.0, dq[2][:], OP.mult, OP.min))
                    sch.op("vector", lambda h: h.tensor_scalar(
                        dq[0][:], dq[0][:], 6, None, OP.logical_shift_left, OP.bypass))
                    sch.op("vector", lambda h: h.tensor_scalar(
                        dq[1][:], dq[1][:], 15, None, OP.logical_shift_left, OP.bypass))
                    sch.op("vector", lambda h: h.tensor_tensor(
                        dq[2][:], dq[1][:], dq[0][:], OP.bitwise_or))
                    dst = wsl[cur][:, g * 512: (g + 1) * 512]
                    raw_free[par] = sch.op("vector", lambda h, o=dst, w=wsc[par][:]:
                                           h.tensor_tensor(o, dq[2][:].bitcast(BF16),
                                                           w, OP.mult))
                wsl_ready[cur] = sch.cnt["vector"]

                for dt_ in range(s * 8, min(s * 8 + 8, 176)):
                    g5, os5 = dt_ % 44, dt_ // 44
                    par = dt_ % 2
                    sch.wait("gpsimd", "vector", raw_free[par])
                    rn = f"raw{par}"
                    sch.dma("gpsimd", wraw[par][:],
                            dn_rep[g5 * 128: (g5 + 1) * 128, os5 * 512: (os5 + 1) * 512],
                            sem=rn)
                    sch.dma("gpsimd", wsc[par][:],
                            dn_sc_d[g5: g5 + 1, os5 * 512: (os5 + 1) * 512].broadcast_to([128, 512]),
                            sem=rn)
                    raw_ld[par] = sch.dcnt[rn]
                    sch.dwait("vector", rn, raw_ld[par])
                    sch.dwait("vector", f"wdsp{par}")
                    sch.op("vector", lambda h, w=wraw[par][:]: h.tensor_tensor(
                        dq[0][:], w, c_shfull[:], OP.logical_shift_right))
                    raw_free[par] = sch.cnt["vector"]
                    sch.op("vector", lambda h: h.tensor_scalar(
                        dq[1][:], dq[0][:], 3, 1, OP.logical_shift_right, OP.bitwise_and))
                    sch.op("vector", lambda h: h.tensor_scalar(
                        dq[0][:], dq[0][:], 7, None, OP.bitwise_and, OP.bypass))
                    sch.op("vector", lambda h: h.tensor_scalar(
                        dq[2][:], dq[0][:], 252.0, None, OP.add, OP.bypass))
                    sch.op("vector", lambda h: h.scalar_tensor_tensor(
                        dq[0][:], dq[0][:], 252.0, dq[2][:], OP.mult, OP.min))
                    sch.op("vector", lambda h: h.tensor_scalar(
                        dq[0][:], dq[0][:], 6, None, OP.logical_shift_left, OP.bypass))
                    sch.op("vector", lambda h: h.tensor_scalar(
                        dq[1][:], dq[1][:], 15, None, OP.logical_shift_left, OP.bypass))
                    sch.op("vector", lambda h: h.tensor_tensor(
                        dq[2][:], dq[1][:], dq[0][:], OP.bitwise_or))
                    vws = sch.op("vector", lambda h, o=wstage[par][:], w=wsc[par][:]:
                                 h.tensor_tensor(o, dq[2][:].bitcast(BF16), w, OP.mult))
                    sch.wait("gpsimd", "vector", vws)
                    sch.dma("gpsimd", sp_wd[g5, :, os5 * 512: (os5 + 1) * 512],
                            wstage[par][:], sem=f"wdsp{par}")

                sch.wait("tensor", "vector", wsl_ready[cur])
                for grp in range(2):
                    for g in range(16):
                        for m2 in range(2):
                            m4 = grp * 2 + m2
                            for half in range(2):
                                bi = m2 * 2 + half
                                ps = (PS[2], PS[3], PS[4], PS[5])[bi]
                                if g == 0:
                                    sch.wait("tensor", "vector", evac_v[bi])
                                    sch.wait("tensor", "scalar", evac_s[bi])
                                lt = wsl[cur][:, g * 512 + m4 * 128:
                                              g * 512 + m4 * 128 + 128]
                                rt = ln2[:, g * OWN + half * 512:
                                         g * OWN + half * 512 + 512]
                                mm(ps[:, 0:512], lt, rt, g == 0, g == 15)
                    t4 = sch.cnt["tensor"]
                    for m2 in range(2):
                        m4 = grp * 2 + m2
                        for half in range(2):
                            bi = m2 * 2 + half
                            ps = (PS[2], PS[3], PS[4], PS[5])[bi]
                            if s < 11:
                                dst = sgh[:, (s * 4 + m4) * OWN + half * 512:
                                          (s * 4 + m4) * OWN + half * 512 + 512]
                                sch.wait("scalar", "tensor", t4)
                                evac_s[bi] = sch.op(
                                    "scalar", lambda h, o=dst, i=ps[:, 0:512]:
                                    h.activation(o, i, AF.Silu))
                            else:
                                su = s - 11
                                dst = sgh[:, (su * 4 + m4) * OWN + half * 512:
                                          (su * 4 + m4) * OWN + half * 512 + 512]
                                sch.wait("vector", "tensor", t4)
                                evac_v[bi] = sch.op(
                                    "vector", lambda h, o=dst, a=dst, b=ps[:, 0:512]:
                                    h.tensor_tensor(o, a, b, OP.mult))
                wsl_free_t[cur] = sch.cnt["tensor"]
            sch.barrier()
        stack_ln2.close()

        stack_y = contextlib.ExitStack()
        y_sb = sb(stack_y, "y_sb", [128, KT * OWN], F32, side="right")

        with contextlib.ExitStack() as p5:
            wdb = [sb(p5, f"wdb{i}", [128, 8 * 256], BF16) for i in range(2)]

            ev5 = [0, 0, 0, 0]
            batches = []
            for os2 in range(8):
                for b0 in range(0, 44, 8):
                    batches.append((os2, list(range(b0, min(b0 + 8, 44)))))
            wb_ready = [0, 0]
            wb_free = [0, 0]

            def load_batch(bi_, os2, gl):
                cur = bi_ % 2
                wn = f"wdl{cur}"
                sch.dwait("gpsimd", f"wdsp0")
                sch.dwait("gpsimd", f"wdsp1")
                sch.wait("gpsimd", "tensor", wb_free[cur])
                for j, g in enumerate(gl):
                    sch.dma("gpsimd", wdb[cur][:, j * 256: j * 256 + 256],
                            sp_wd[g, :, os2 * 256: (os2 + 1) * 256], sem=wn)
                wb_ready[cur] = sch.dcnt[wn]

            load_batch(0, *batches[0])
            for bi_, (os2, gl) in enumerate(batches):
                cur = bi_ % 2
                if bi_ + 1 < len(batches):
                    load_batch(bi_ + 1, *batches[bi_ + 1])
                sch.dwait("tensor", f"wdl{cur}", wb_ready[cur])
                for j, g in enumerate(gl):
                    for m2 in range(2):
                        for half in range(2):
                            b_ = m2 * 2 + half
                            ps = (PS[2], PS[3], PS[4], PS[5])[b_]
                            if g == 0:
                                sch.wait("tensor", "vector", ev5[b_])
                            lt = wdb[cur][:, j * 256 + m2 * 128: j * 256 + m2 * 128 + 128]
                            rt = sgh[:, g * OWN + half * 512: g * OWN + half * 512 + 512]
                            mm(ps[:, 0:512], lt, rt, g == 0, g == 43)
                wb_free[cur] = sch.cnt["tensor"]
                if gl[-1] == 43:
                    t5 = sch.cnt["tensor"]
                    sch.wait("vector", "tensor", t5)
                    for m2 in range(2):
                        for half in range(2):
                            b_ = m2 * 2 + half
                            ps = (PS[2], PS[3], PS[4], PS[5])[b_]
                            o_tile = os2 * 2 + m2
                            dst = y_sb[:, o_tile * OWN + half * 512:
                                       o_tile * OWN + half * 512 + 512]
                            ev5[b_] = sch.op("vector", lambda h, o=dst, i=ps[:, 0:512]:
                                             h.tensor_copy(o, i))
            sch.barrier()
        stack_sg.close()

        with contextlib.ExitStack() as p6:
            xo2 = sb(p6, "xo2", [128, KT * OWN], F32)
            sqb6 = [sb(p6, f"p6sq{i}", [128, 512], BF16) for i in range(2)]
            rows6m = sb(p6, "p6rowm", [1, OWN], F32)
            rows6r = sb(p6, "p6rowr", [1, OWN], F32)
            rows6_b = sb(p6, "p6rows_b", [1, OWN], BF16)
            tmp6 = [sb(p6, f"p6tmp{i}", [128, 512], F32) for i in range(2)]
            outsb = [sb(p6, f"outsb{i}", [128, D], F32) for i in range(2)]

            sch.dwait("gpsimd", "x1sp")
            sch.dma("gpsimd", xo2[:], sp_x1, sem="x1ld")

            stats_rows(lambda k, half: y_sb[:, k * OWN + half * 512:
                                            k * OWN + half * 512 + 512],
                       sqb6, rows6m, rows6r, rows6_b, src_f32=True)
            sch.dwait("vector", "x1ld")
            v_x2 = ln_apply(
                lambda k, half: y_sb[:, k * OWN + half * 512: k * OWN + half * 512 + 512],
                lambda k, half: xo2[:, k * OWN + half * 512: k * OWN + half * 512 + 512],
                rows6r, rows6_b, tmp6,
                extra_res=lambda k, half: xo2[:, k * OWN + half * 512:
                                              k * OWN + half * 512 + 512])

            tr_roll = [0, 0]
            out_roll = [0, 0]
            for ti in range(NSLOT):
                ob = ti % 2
                sch.dwait("vector", f"out{ob}", out_roll[ob])
                sch.wait("tensor", "vector", v_x2)
                for k in range(KT):
                    par = k % 2
                    ps = (PS[0], PS[1])[par]
                    sch.wait("tensor", "vector", tr_roll[par])
                    sch.op("tensor", lambda h, o=ps[:, 0:128],
                           i=xo2[:, k * OWN + ti * 128: k * OWN + ti * 128 + 128],
                           idm=c_idf[:]: h.transpose(o, i, idm))
                    sch.wait("vector", "tensor", sch.cnt["tensor"])
                    tr_roll[par] = sch.op(
                        "vector", lambda h, o=outsb[ob][:, k * 128: k * 128 + 128],
                        i=ps[:, 0:128]: h.tensor_copy(o, i))
                sch.wait("gpsimd", "vector", sch.cnt["vector"])
                sch.dma("gpsimd", out_tok[ti * 128: (ti + 1) * 128, :], outsb[ob][:],
                        sem=f"out{ob}")
                out_roll[ob] = sch.dcnt[f"out{ob}"]
            sch.dwait("gpsimd", "out0")
            sch.dwait("gpsimd", "out1")
        stack_y.close()

        sch.finalize()
        with nc.Block() as block:
            @block.tensor
            def _(eng):
                for f in sch.ops["tensor"]:
                    f(eng)

            @block.vector
            def _(eng):
                for f in sch.ops["vector"]:
                    f(eng)

            @block.scalar
            def _(eng):
                for f in sch.ops["scalar"]:
                    f(eng)

            @block.sync
            def _(eng):
                for f in sch.ops["sync"]:
                    f(eng)

            @block.gpsimd
            def _(eng):
                for f in sch.ops["gpsimd"]:
                    f(eng)

    return nc



def _rope_tables(pos, n=DR):
    inv = 10000.0 ** (-np.arange(0, n, 2, dtype=np.float64) / n)
    ang = pos[None, :].astype(np.float64) * inv[:, None]
    c = np.cos(ang).astype(np.float32)
    s = np.sin(ang).astype(np.float32)
    C = np.empty((n, pos.shape[0]), np.float32)
    S = np.empty((n, pos.shape[0]), np.float32)
    C[0::2] = c
    C[1::2] = c
    S[0::2] = -s
    S[1::2] = s
    return np.ascontiguousarray(C), np.ascontiguousarray(S)


def _build_masks(core):
    out = np.zeros((NSLOT * 2, 128, CH), np.float32)
    for t in range(NSLOT):
        g = 8 * t + core
        k_cnt = g // 4 + 1
        jdiag = g % 4
        diag = np.zeros((128, CH), np.float32)
        diag[:, (jdiag + 1) * 128:] = NEG_MASK
        tri = np.triu(np.full((128, 128), NEG_MASK, np.float32), 1)
        diag[:, jdiag * 128: (jdiag + 1) * 128] = tri
        if k_cnt == 2 * t + 2:
            out[2 * t + 0] = 0.0
            out[2 * t + 1] = diag
        else:
            out[2 * t + 0] = diag
            out[2 * t + 1] = NEG_MASK
    return np.ascontiguousarray(out.astype(BF))


def kernel(L0_attn_x, L0_attn_Wqa, L0_attn_Wqb, L0_attn_Wkva, L0_attn_Wkvb,
           L0_attn_Wo, L0_ffn_gate_up_packed, L0_ffn_gate_up_scales,
           L0_ffn_down_packed, L0_ffn_down_scales):
    x = np.ascontiguousarray(np.asarray(L0_attn_x)[0])
    xT = np.ascontiguousarray(x.T)
    Wqa = np.ascontiguousarray(np.asarray(L0_attn_Wqa))
    Wqb = np.asarray(L0_attn_Wqb)
    Wkva = np.asarray(L0_attn_Wkva)
    Wkvb = np.asarray(L0_attn_Wkvb)
    Wo = np.ascontiguousarray(np.asarray(L0_attn_Wo))

    perm = np.arange(DR) ^ 1

    q3 = Wqb.reshape(QL, NH, DN + DR)
    wqbn = np.ascontiguousarray(q3[:, :, :DN].reshape(QL, NH * DN))
    wqbp = np.ascontiguousarray(q3[:, :, DN:].reshape(QL, NH * DR))
    wqbps = np.ascontiguousarray(q3[:, :, DN:][:, :, perm].reshape(QL, NH * DR))

    wkva_ext = np.ascontiguousarray(
        np.concatenate([Wkva, Wkva[:, KVL:KVL + DR][:, perm]], axis=1))

    kv3 = Wkvb.reshape(KVL, NH, 2 * DN)
    wkvbk = np.ascontiguousarray(kv3[:, :, :DN].reshape(KVL, NH * DN))
    wkvbv = np.ascontiguousarray(kv3[:, :, DN:].reshape(KVL, NH * DN))

    gu16 = np.asarray(L0_ffn_gate_up_packed).view(np.int16)
    gu_rep_h = np.ascontiguousarray(np.repeat(gu16.T, 4, axis=0))
    dn16 = np.asarray(L0_ffn_down_packed).view(np.int16)
    dn_rep_h = np.ascontiguousarray(np.repeat(dn16.T, 4, axis=0))
    gu_sc = np.ascontiguousarray(np.asarray(L0_ffn_gate_up_scales))
    dn_sc = np.ascontiguousarray(np.asarray(L0_ffn_down_scales))

    identb = np.ascontiguousarray(np.eye(128, dtype=BF))
    identf = np.ascontiguousarray(np.eye(128, dtype=np.float32))
    colsb = np.ascontiguousarray(
        np.stack([np.full(128, 1.0 / 2048.0), np.ones(128)], axis=1).astype(BF))
    colsf = np.ascontiguousarray(
        np.stack([np.full(128, 1.0 / 2048.0), np.ones(128),
                  np.full(128, EPS_LN)], axis=1).astype(np.float32))
    rowb = np.ones((1, 128), BF)
    rowf = np.ones((1, 128), np.float32)
    shf = np.ascontiguousarray(
        np.stack([4 * (np.arange(128) % 4), 4 * (np.arange(128) % 4) + 3],
                 axis=1).astype(np.int16))
    shfull = np.ascontiguousarray(
        np.broadcast_to((4 * (np.arange(128) % 4)).astype(np.int16)[:, None],
                        (128, 512)).copy())

    kc, ks = _rope_tables(np.arange(SEQ))

    nc = build_program()

    in_maps = []
    all_rows = []
    for c in range(N_CORES):
        rows = np.concatenate([128 * (8 * t + c) + np.arange(128)
                               for t in range(NSLOT)])
        all_rows.append(rows)
        xq = np.ascontiguousarray(xT[:, rows])
        qc, qs = _rope_tables(rows.astype(np.float64))
        in_maps.append(dict(
            xT=xT, xqT=xq,
            ropeq_c=qc, ropeq_s=qs, ropek_c=kc, ropek_s=ks,
            masks=_build_masks(c),
            wqa=Wqa, wqbn=wqbn, wqbp=wqbp, wqbps=wqbps,
            wkva=wkva_ext, wkvbk=wkvbk, wkvbv=wkvbv, wo=Wo,
            gu_rep=gu_rep_h, gu_sc=gu_sc, dn_rep=dn_rep_h, dn_sc=dn_sc,
            identb=identb, identf=identf, colsb=colsb, colsf=colsf,
            rowb=rowb, rowf=rowf, shf=shf, shfull=shfull,
        ))

    res = run_bass_kernel_spmd(nc, in_maps, core_ids=list(range(N_CORES)))

    out = np.empty((1, SEQ, D), np.float32)
    for c in range(N_CORES):
        out[0, all_rows[c], :] = np.asarray(res.results[c]["out_tok"])
    return out
